# version 14
# baseline (speedup 1.0000x reference)
import contextlib
import numpy as np
import concourse.bass as bass
import concourse.mybir as mybir
from concourse.bass_utils import run_bass_kernel_spmd

F32 = mybir.dt.float32
BF16 = mybir.dt.bfloat16
I32 = mybir.dt.int32
U32 = mybir.dt.uint32
AF = mybir.ActivationFunctionType
OP = mybir.AluOpType
AX = mybir.AxisListType
IOOA = bass.IndirectOffsetOnAxis

D, S, H, HD, KVH, LAT, WIN, E, FF = 1024, 2048, 16, 64, 4, 256, 256, 8, 4096
NCHUNK = D // 128
NTILE = S // 128
NQT = S // 512
EPS0 = 1e-6
EPS1 = 1e-5
OOB = 1 << 20

def _perm64():
    return np.concatenate([np.arange(0, 64, 2), np.arange(1, 64, 2)])

def perm_cols():
    p = np.arange(D).reshape(H, HD)
    p = p[:, _perm64()]
    return p.reshape(-1)

def perm_cols_kv():
    p = np.arange(KVH * HD).reshape(KVH, HD)
    p = p[:, _perm64()]
    return p.reshape(-1)

def l0_ropetab(base):
    inv = 1.0 / base ** (np.arange(0, HD, 2, dtype=np.float32) / HD)
    ang = np.arange(S, dtype=np.float32)[:, None] * inv[None, :]
    c, s = np.cos(ang).T.astype(np.float32), np.sin(ang).T.astype(np.float32)
    CS = np.concatenate([c, c, c, c], 0)
    SN = np.concatenate([-s, s, -s, s], 0)
    return CS, SN

def host_consts():
    C = {}
    C["IDN"] = np.eye(128, dtype=np.float32)
    sw = np.zeros((128, 128), np.float32)
    for b in range(2):
        for i in range(32):
            sw[b * 64 + i, b * 64 + 32 + i] = 1.0
            sw[b * 64 + 32 + i, b * 64 + i] = 1.0
    C["SWAP"] = sw
    C["TRIU_INC"] = np.triu(np.ones((128, 128), np.float32))
    C["TRIU_EXC"] = np.triu(np.ones((128, 128), np.float32), 1)
    C["TRIU_EXCF"] = C["TRIU_EXC"]
    C["ONES_CF"] = np.ones((128, 1), np.float32)
    C["ONES_R"] = np.ones((1, 128), np.float32)
    C["ONES_C"] = np.ones((128, 1), np.float32)
    hm = np.zeros((NCHUNK, 128, 16), np.float32)
    bm = np.zeros((NCHUNK, 16, 128), np.float32)
    for c in range(NCHUNK):
        for p in range(128):
            hm[c, p, 2 * c + p // 64] = 1.0
            bm[c, 2 * c + p // 64, p] = 1.0
    C["HMAP"] = hm.transpose(1, 0, 2).reshape(128, NCHUNK * 16)
    C["BMAP"] = bm.transpose(1, 0, 2).reshape(16, NCHUNK * 128)
    cg, sg = l0_ropetab(1e6)
    cl, sl = l0_ropetab(1e4)
    C["CSG"], C["SNG"], C["CSL"], C["SNL"] = cg, sg, cl, sl
    C["M_CAUS"] = np.triu(np.ones((128, 128), np.float32))
    C["M_WIN"] = np.tril(np.ones((128, 128), np.float32), -1)
    C["T32"] = np.tile(np.arange(S, dtype=np.float32)[None, :], (32, 1))
    hd2 = HD // 2
    top = 2595.0 * np.log10(np.float32(1.0 + 4000.0 / 200.0))
    mel = np.power(10.0, np.linspace(0.0, top, hd2, dtype=np.float32) / 2595.0) - 1.0
    C["MELB"] = ((200.0 * mel / 1000.0) / 220.0).astype(np.float32).reshape(32, 1)
    pm = np.zeros((32, 128), np.float32)
    sm = np.zeros((32, 128), np.float32)
    for f in range(32):
        for b in range(4):
            pm[f, b * 32 + f] = 1.0
            sm[f, b * 32 + f] = -1.0 if (b % 2 == 0) else 1.0
    C["CMAP"], C["SMAP"] = pm, sm
    C["IOTA512"] = np.arange(512, dtype=np.float32)[None, :]
    C["IOTA8R"] = np.tile(np.arange(8, dtype=np.float32)[None, :], (128, 1))
    C["IOTA128C"] = np.arange(128, dtype=np.float32)[:, None]
    e64 = np.zeros((65, 64), np.float32); e64[64, :] = 1.0
    C["E64BC"] = e64
    return C

CONSTS = host_consts()

class Sem:
    def __init__(self, h, name):
        self.h, self.name, self.n = h, name, 0

class Prog:
    ENGINES = ("sync", "scalar", "vector", "tensor", "gpsimd")

    def __init__(self, nc, st):
        self.nc = nc
        self.st = st
        self.items = {e: [] for e in self.ENGINES}
        self.sems = {}
        self.waited = {e: {} for e in self.ENGINES}
        self._breg = {}
        self.selfsem = {}
        self.iomap = {}

    def sem(self, name):
        h = self.st.enter_context(self.nc.semaphore(name))
        s = Sem(h, name)
        self.sems[name] = s
        return s

    def emit(self, engine, fn):
        self.items[engine].append(fn)

    def wait(self, engine, sem, val=None):
        v = sem.n if val is None else val
        if v <= 0:
            return
        if self.waited[engine].get(sem.name, -1) >= v:
            return
        self.waited[engine][sem.name] = v
        self.emit(engine, lambda e, s=sem, v=v: e.wait_ge(s.h, v))

    def op(self, engine, f, sem=None):
        if sem is not None:
            sem.n += 1
            v = sem.n
            self.emit(engine, lambda e, f=f, s=sem, v=v: f(e).then_inc(s.h, 1))
            self.waited[engine][sem.name] = v
        else:
            self.emit(engine, f)

    def dma(self, queue, out, in_, sem, **kw):
        sem.n += 16
        self.emit(queue, lambda e, o=out, i=in_, s=sem, kw=kw: e.dma_start(out=o, in_=i, **kw).then_inc(s.h, 16))

    def idma(self, out, out_off, in_, in_off, sem, bounds=None, **kw):
        sem.n += 16
        def fn(e, kw=dict(kw), s=sem, bounds=bounds):
            if bounds is not None:
                kw["bounds_check"] = bounds["r"]
            e.indirect_dma_start(out=out, out_offset=out_off, in_=in_, in_offset=in_off, **kw).then_inc(s.h, 16)
        self.emit("gpsimd", fn)

    def cc(self, kind, op, groups, ins, outs, sem):
        sem.n += 1
        self.emit("gpsimd", lambda e, s=sem: e.collective_compute(
            kind, op, replica_groups=groups, ins=ins, outs=outs).then_inc(s.h, 1))

    def breg(self, val):
        if val not in self._breg:
            holder = {}
            def mk(e, val=val, holder=holder):
                r = e.alloc_register(f"bnd{val}")
                e.reg_mov(r, val)
                holder["r"] = r
                return None
            self.emit("gpsimd", lambda e, mk=mk: mk(e))
            self._breg[val] = holder
        return self._breg[val]

    def finalize(self, force=False):
        if FUSE["on"] and not force:
            return
        with self.nc.Block() as block:
            for eng in self.ENGINES:
                items = self.items[eng]
                if not items:
                    continue
                def make(items):
                    def body(e):
                        for fn in items:
                            fn(e)
                    return body
                getattr(block, eng)(make(items))

FUSE = {"on": False, "P": None, "ctx": None, "stage": "", "stack": None}

def new_prog():
    if FUSE["on"] and FUSE["P"] is not None:
        return FUSE["P"]
    st = contextlib.ExitStack()
    nc = bass.Bass("TRN2", target_bir_lowering=False)
    P = Prog(nc, st)
    P.iomap = {}
    if FUSE["on"]:
        FUSE["P"] = P
    return P

def sbuf(P, name, shape, dt):
    if FUSE["on"] and FUSE["stack"] is not None:
        return FUSE["stack"].enter_context(P.nc.sbuf_tensor(FUSE["stage"] + name, list(shape), dt))
    return P.st.enter_context(P.nc.sbuf_tensor(name, list(shape), dt))

def psum(P, name, shape, dt=F32):
    return P.st.enter_context(P.nc.psum_tensor(name, list(shape), dt))

def din(P, name, shape, dt=F32):
    if FUSE["on"]:
        t = P.iomap.get(name)
        if t is not None:
            return t
        ap = P.nc.dram_tensor(name, list(shape), dt, kind="ExternalInput").ap()
        P.iomap[name] = ap
        return ap
    return P.nc.dram_tensor(name, list(shape), dt, kind="ExternalInput").ap()

def dout(P, name, shape, dt=F32):
    if FUSE["on"] and name != "out":
        ap = P.nc.dram_tensor(name, list(shape), dt).ap()
        P.iomap[name] = ap
        return ap
    return P.nc.dram_tensor(name, list(shape), dt, kind="ExternalOutput").ap()

def dtmp(P, name, shape, dt=F32):
    if FUSE["on"]:
        name = FUSE["stage"] + name
    return P.nc.dram_tensor(name, list(shape), dt).ap()

def load_consts(P, names, sem):
    tiles = {}
    for nm in names:
        arr = CONSTS[nm]
        ap = din(P, "c_" + nm, arr.shape, F32)
        bf = nm not in ("IDN",)
        t = sbuf(P, "ct_" + nm, arr.shape, BF16 if bf else F32)
        P.dma("gpsimd", t[:, :], ap, sem)
        tiles[nm] = t
    return tiles

def const_inputs(names):
    return {"c_" + nm: CONSTS[nm] for nm in names}

def _ins(P, engine, method, *args, sem=None, inc=1, **kw):
    ss = P.selfsem.get(engine)
    if ss is not None and ss.n > 0:
        P.wait(engine, ss, ss.n)
    if sem is not None:
        sem.n += inc
        def fn(e, method=method, args=args, kw=kw, s=sem, inc=inc):
            getattr(e, method)(*args, **kw).then_inc(s.h, inc)
        if P.selfsem.get(engine) is not sem:
            P.waited[engine][sem.name] = sem.n
    else:
        def fn(e, method=method, args=args, kw=kw):
            getattr(e, method)(*args, **kw)
    P.emit(engine, fn)

Prog.ins = _ins

def _mm(P, out, lhsT, rhs, start, stop, sem=None, tile_position=None):
    kw = {"start": start, "stop": stop}
    if tile_position is not None:
        kw["tile_position"] = tile_position
    P.ins("tensor", "matmul", out, lhsT, rhs, sem=sem, **kw)

Prog.mm = _mm

class Ctx:
    def __new__(cls, P, rstd=True):
        if FUSE["on"] and FUSE["ctx"] is not None:
            return FUSE["ctx"]
        obj = super().__new__(cls)
        if FUSE["on"]:
            FUSE["ctx"] = obj
        return obj

    def __init__(self, P, rstd=True):
        if getattr(self, "_inited", False):
            return
        self._inited = True
        self.P = P
        self.sdma = P.sem("sdma")
        self.adma = P.sem("adma")
        self.gdma = P.sem("gdma")
        self.pe = P.sem("pe")
        self.dve = P.sem("dve")
        self.act = P.sem("act")
        self.ccs = P.sem("ccs")
        self.gsc = P.sem("gsc")
        P.selfsem = {"vector": self.dve, "scalar": self.act}
        self.psum = [psum(P, f"psb{i}", [128, 512], F32) for i in range(8)]
        self.r1 = sbuf(P, "r1scr", [128, 512], BF16)
        self.r2 = sbuf(P, "r2scr", [128, 512], BF16)
        self.sw = sbuf(P, "swscr", [128, 512], BF16)
        self.bc = sbuf(P, "bcscr", [128, 512], BF16) if rstd else None
        self.sq = sbuf(P, "sqscr", [128, 512], BF16) if rstd else None
        self.scr1 = sbuf(P, "scr1", [128, 1024], F32)
        self.stat = sbuf(P, "stat", [128, 8], F32)
        self.rstd = sbuf(P, "rstd", [16, 2048], BF16) if rstd else None
        self.rowb = sbuf(P, "rowb", [1, 512], F32)
        self.epsb = sbuf(P, "epsb", [128, 8], F32)
        self._biases = {}
        self.C = {}

    def bias(self, val):
        if val not in self._biases:
            col = len(self._biases)
            self._biases[val] = col
            self.P.ins("vector", "memset", self.epsb[:, col:col + 1], float(val), sem=self.dve)
            self.P.wait("scalar", self.dve)
        col = self._biases[val]
        return self.epsb[:, col:col + 1]

    def loadc(self, names):
        P = self.P
        for nm in names:
            if nm in self.C:
                continue
            arr = CONSTS[nm]
            ap = din(P, "c_" + nm, arr.shape, F32)
            dt = F32 if nm in ("IDN", "ONES_R", "E64BC", "TRIU_EXCF", "ONES_CF", "T32", "MELB", "CMAP", "SMAP") else BF16
            t = sbuf(P, "ct_" + nm, arr.shape, dt)
            P.dma("gpsimd", t[:, :], ap, self.gdma)
            self.C[nm] = t
        for eng in ("tensor", "vector", "scalar"):
            P.wait(eng, self.gdma)

def bcast_row(ctx, row_ap, out_sb, n):
    P = ctx.P
    ps = ctx.psum[7]
    for j in range(0, n, 512):
        w = min(512, n - j)
        P.wait("sync", ctx.pe)
        P.dma("sync", ctx.rowb[0:1, 0:w], row_ap[0:1, j:j + w], ctx.sdma)
        P.wait("tensor", ctx.sdma)
        P.wait("tensor", ctx.gdma)
        P.wait("tensor", ctx.dve)
        P.mm(ps[:, 0:w], ctx.C["ONES_R"][0:1, :], ctx.rowb[0:1, 0:w], True, True, sem=ctx.pe)
        P.wait("vector", ctx.pe)
        P.ins("vector", "tensor_copy", out_sb[:, j:j + w], ps[:, 0:w], sem=ctx.dve)

def emit_norm(ctx, kind, get_in_tile, out_bf, w_bc=None, b_bc=None, eps=1e-5, out_fp_tile=None):
    P = ctx.P
    for tt in range(NTILE):
        xt = get_in_tile(tt)
        st = ctx.stat
        ot = out_bf[:, tt * D:(tt + 1) * D]
        if kind == "ln":
            P.ins("vector", "reduce_sum", st[:, 0:1], xt, axis=AX.X, sem=ctx.dve)
            P.ins("vector", "tensor_scalar", st[:, 0:1], st[:, 0:1], 1.0 / D, None, OP.mult, sem=ctx.dve)
            P.ins("vector", "tensor_scalar", ctx.scr1[:, :], xt, st[:, 0:1], None, OP.subtract, sem=ctx.dve)
            src = ctx.scr1[:, :]
        else:
            src = xt
        P.wait("scalar", ctx.dve)
        P.ins("scalar", "activation", ot, src, AF.Square, accum_out=st[:, 1:2], sem=ctx.act)
        P.wait("vector", ctx.act)
        P.ins("vector", "tensor_scalar", st[:, 1:2], st[:, 1:2], 1.0 / D, None, OP.mult, sem=ctx.dve)
        P.wait("scalar", ctx.dve)
        P.ins("scalar", "activation", st[:, 2:3], st[:, 1:2], AF.Ln, bias=ctx.bias(float(eps)), sem=ctx.act)
        P.ins("scalar", "activation", st[:, 2:3], st[:, 2:3], AF.Exp, scale=-0.5, sem=ctx.act)
        P.wait("vector", ctx.act)
        if kind == "ln":
            P.ins("vector", "tensor_scalar", ctx.scr1[:, :], ctx.scr1[:, :], st[:, 2:3], None, OP.mult,
                  sem=ctx.dve)
            nsrc = ctx.scr1[:, :]
        else:
            P.ins("vector", "tensor_scalar", ctx.scr1[:, :], src, st[:, 2:3], None, OP.mult, sem=ctx.dve)
            nsrc = ctx.scr1[:, :]
        if out_fp_tile is not None:
            out_fp_tile(tt, nsrc)
        if w_bc is None:
            P.ins("vector", "tensor_copy", ot, nsrc, sem=ctx.dve)
        else:
            P.ins("vector", "tensor_mul", ctx.scr1[:, :], nsrc, w_bc[:, :], sem=ctx.dve)
            if b_bc is not None:
                P.ins("vector", "tensor_add", ctx.scr1[:, :], ctx.scr1[:, :], b_bc[:, :], sem=ctx.dve)
            P.ins("vector", "tensor_copy", ot, ctx.scr1[:, :], sem=ctx.dve)

def transpose_to_fmajor(ctx, src_bf, dram_scratch, dst_T, n_chunks):
    P = ctx.P
    W = n_chunks * 128
    P.wait("sync", ctx.dve)
    P.dma("sync", dram_scratch.rearrange("(t p) j -> p t j", p=128),
          src_bf[:, 0:NTILE * W].rearrange("p (t j) -> p t j", j=W), ctx.sdma)
    P.wait("scalar", ctx.sdma)
    for c in range(n_chunks):
        P.dma("scalar", dst_T[:, c * S:(c + 1) * S], dram_scratch[:, c * 128:(c + 1) * 128], ctx.adma,
              transpose=True)

def proj_fmajor_rope(ctx, xT, n_cin, w_sb, w_cols, cout_list, outT, CS, SN, qknorm, vhead_of=None,
                     out_stride=None):
    P = ctx.P
    n_out = len(cout_list)
    ostr = S if out_stride is None else out_stride
    for qt in range(NQT):
        for j, co in enumerate(cout_list):
            ps = ctx.psum[j % 2]
            P.wait("tensor", ctx.act)
            for ci in range(n_cin):
                P.mm(ps[:, :], w_sb[:, ci * w_cols + co * 128: ci * w_cols + co * 128 + 128],
                     xT[:, ci * S + qt * 512: ci * S + qt * 512 + 512],
                     ci == 0, ci == n_cin - 1, sem=ctx.pe)
            P.wait("scalar", ctx.pe)
            osl = outT[:, j * ostr + qt * 512: j * ostr + qt * 512 + 512]
            P.ins("scalar", "activation", osl, ps[:, :], AF.Copy, sem=ctx.act)
            if qknorm:
                P.ins("scalar", "activation", ctx.sq[:, :], ps[:, :], AF.Square, sem=ctx.act)
                P.wait("tensor", ctx.act)
                hs = ctx.psum[2]
                vh = j if vhead_of is None else vhead_of[j]
                P.mm(hs[0:16, :], ctx.C["HMAP"][:, vh * 16:(vh + 1) * 16], ctx.sq[:, :],
                     j == 0, j == n_out - 1, sem=ctx.pe)
        if qknorm:
            P.wait("scalar", ctx.pe)
            P.ins("scalar", "activation", ctx.rstd[0:16, qt * 512:qt * 512 + 512], ctx.psum[2][0:16, :],
                  AF.Ln, bias=ctx.bias(EPS0)[0:16, 0:1], scale=1.0 / HD, sem=ctx.act)
            P.ins("scalar", "activation", ctx.rstd[0:16, qt * 512:qt * 512 + 512],
                  ctx.rstd[0:16, qt * 512:qt * 512 + 512], AF.Exp, scale=-0.5, sem=ctx.act)
    for j, co in enumerate(cout_list):
        for qt in range(NQT):
            osl = outT[:, j * ostr + qt * 512: j * ostr + qt * 512 + 512]
            ps_sw = ctx.psum[3]
            P.wait("tensor", ctx.act)
            P.wait("tensor", ctx.dve)
            P.mm(ps_sw[:, :], ctx.C["SWAP"][:, :], osl, True, True, sem=ctx.pe)
            if qknorm:
                ps_bc = ctx.psum[4]
                vh = j if vhead_of is None else vhead_of[j]
                P.mm(ps_bc[:, :], ctx.C["BMAP"][0:16, vh * 128:(vh + 1) * 128],
                     ctx.rstd[0:16, qt * 512:qt * 512 + 512], True, True, sem=ctx.pe)
            P.wait("scalar", ctx.pe)
            P.ins("scalar", "activation", ctx.sw[:, :], ps_sw[:, :], AF.Copy, sem=ctx.act)
            if qknorm:
                P.ins("scalar", "activation", ctx.bc[:, :], ps_bc[:, :], AF.Copy, sem=ctx.act)
            P.wait("vector", ctx.act)
            CSs = CS[:, qt * 512:qt * 512 + 512]
            SNs = SN[:, qt * 512:qt * 512 + 512]
            P.ins("vector", "tensor_mul", ctx.r1[:, :], osl, CSs, sem=ctx.dve)
            P.ins("vector", "tensor_mul", ctx.r2[:, :], ctx.sw[:, :], SNs, sem=ctx.dve)
            P.ins("vector", "tensor_add", ctx.r1[:, :], ctx.r1[:, :], ctx.r2[:, :], sem=ctx.dve)
            if qknorm:
                P.ins("vector", "tensor_mul", osl, ctx.r1[:, :], ctx.bc[:, :], sem=ctx.dve)
            else:
                P.ins("vector", "tensor_copy", osl, ctx.r1[:, :], sem=ctx.dve)

def attn_core(ctx, QT, KT_of_chunk, V_sb, vsec, OT, heads, act_kind, band_of, chunks=None):
    P = ctx.P
    if "p_sb" not in ctx.C:
        ctx.C["p_sb"] = sbuf(P, "p_sb", [128, 512], BF16)[:, :]
        ctx.C["p_sb2"] = sbuf(P, "p_sb2", [128, 512], BF16)[:, :]
        ctx.C["zt"] = sbuf(P, "zt", [128, 512], BF16)[:, :]
        P.ins("vector", "memset", ctx.C["zt"], 0.0, sem=ctx.dve)
    p_sbs = [ctx.C["p_sb"], ctx.C["p_sb2"]]
    zt = ctx.C["zt"]
    P.wait("tensor", ctx.act)
    P.wait("tensor", ctx.dve)
    am = {0: 0, 1: 0}
    dm = {0: 0, 1: 0}
    eo_mark = [0]
    for c in (range(NCHUNK) if chunks is None else chunks):
        KTc = KT_of_chunk(c)
        for hh in range(2):
            h = 2 * c + hh
            if h not in heads:
                continue
            base = hh * 64
            vm = heads[h]
            if act_kind == "taylor":
                ps_cs = ctx.psum[6]
                P.wait("tensor", ctx.dve)
                for kt in range(NTILE):
                    P.mm(ps_cs[0:65, 0:1], V_sb[:, (kt * vsec + vm) * 65:(kt * vsec + vm) * 65 + 65],
                         ctx.C["ONES_C"][:, 0:1], kt == 0, kt == NTILE - 1, sem=ctx.pe)
                P.wait("vector", ctx.pe)
                P.ins("vector", "tensor_scalar", ctx.stat[0:65, 3:4], ps_cs[0:65, 0:1], 0.5, None, OP.mult,
                      sem=ctx.dve)
            for qt in range(NQT):
                ps_o = ctx.psum[5]
                items = band_of(qt)
                nIt = len(items)
                zfill = not (items[0][1] == 0 and items[0][2] == 512)
                if zfill:
                    P.wait("tensor", ctx.dve, eo_mark[0])
                    P.mm(ps_o[0:65, 0:512], V_sb[:, vm * 65:vm * 65 + 65], zt, True, False, sem=ctx.pe)
                info = []
                for bi in range(nIt + 1):
                    if bi < nIt:
                        kc, qlo, qhi, masks = items[bi]
                        par = bi % 2
                        ps_st = ctx.psum[par]
                        pb = p_sbs[par]
                        N = qhi - qlo
                        P.wait("tensor", ctx.act, am[par])
                        P.mm(ps_st[:, 0:N],
                             KTc[base:base + 64, kc * 128:(kc + 1) * 128],
                             QT[base:base + 64, c * S + qt * 512 + qlo: c * S + qt * 512 + qhi],
                             True, True, sem=ctx.pe, tile_position=(base, 0))
                        P.wait("scalar", ctx.pe)
                        P.wait("scalar", ctx.dve, dm[par])
                        if act_kind == "exp":
                            P.ins("scalar", "activation", pb[:, 0:N], ps_st[:, 0:N], AF.Exp, scale=0.125,
                                  sem=ctx.act)
                        else:
                            P.ins("scalar", "activation", pb[:, 0:N], ps_st[:, 0:N], AF.Square,
                                  scale=0.08838834764831845, bias=ctx.bias(0.7071067811865476), sem=ctx.act)
                        am[par] = ctx.act.n
                        if masks:
                            P.wait("vector", ctx.act, am[par])
                            for (subcol, mname) in masks:
                                off = subcol - qlo
                                P.ins("vector", "tensor_mul", pb[:, off:off + 128], pb[:, off:off + 128],
                                      ctx.C[mname][:, :], sem=ctx.dve)
                            dm[par] = ctx.dve.n
                        info.append((kc, qlo, qhi, N, par, am[par], dm[par] if masks else 0))
                    if bi >= 1:
                        kc0, qlo0, qhi0, N0, par0, a0, d0 = info[bi - 1]
                        P.wait("tensor", ctx.act, a0)
                        if d0:
                            P.wait("tensor", ctx.dve, d0)
                        if bi - 1 == 0 and not zfill:
                            P.wait("tensor", ctx.dve, eo_mark[0])
                        P.mm(ps_o[0:65, qlo0:qhi0],
                             V_sb[:, (kc0 * vsec + vm) * 65:(kc0 * vsec + vm) * 65 + 65],
                             p_sbs[par0][:, 0:N0], (bi - 1 == 0) and not zfill, bi - 1 == nIt - 1,
                             sem=ctx.pe)
                P.wait("vector", ctx.pe)
                if act_kind == "taylor":
                    P.ins("vector", "tensor_scalar", ctx.scr1[0:65, 0:512], ps_o[0:65, :],
                          ctx.stat[0:65, 3:4], None, OP.add, sem=ctx.dve)
                else:
                    P.ins("vector", "tensor_copy", ctx.scr1[0:65, 0:512], ps_o[0:65, :], sem=ctx.dve)
                eo_mark[0] = ctx.dve.n
                ps_n = ctx.psum[4]
                P.wait("tensor", ctx.dve)
                P.mm(ps_n[0:64, :], ctx.C["E64BC"][0:65, 0:64], ctx.scr1[0:65, 0:512], True, True, sem=ctx.pe)
                P.wait("scalar", ctx.pe)
                P.ins("scalar", "activation", ctx.scr1[0:64, 512:1024], ps_n[0:64, :], AF.Ln, sem=ctx.act)
                P.ins("scalar", "activation", ctx.scr1[0:64, 512:1024], ctx.scr1[0:64, 512:1024], AF.Exp,
                      scale=-1.0, sem=ctx.act)
                P.wait("vector", ctx.act)
                P.ins("vector", "tensor_mul", OT[base:base + 64, c * S + qt * 512: c * S + qt * 512 + 512],
                      ctx.scr1[0:64, 0:512], ctx.scr1[0:64, 512:1024], sem=ctx.dve)

def band_causal(qt):
    items = []
    for kc in range(0, qt * 4 + 4):
        if kc < qt * 4:
            items.append((kc, 0, 512, []))
        else:
            j = kc - qt * 4
            items.append((kc, j * 128, 512, [(j * 128, "M_CAUS")]))
    return items

def band_window(qt):
    items = []
    for kc in range(max(0, qt * 4 - 2), qt * 4 + 4):
        j = kc - qt * 4
        if j == -2:
            items.append((kc, 0, 128, [(0, "M_WIN")]))
        elif j == -1:
            items.append((kc, 0, 256, [(128, "M_WIN")]))
        else:
            qhi = min(512, (j + 3) * 128)
            masks = [(j * 128, "M_CAUS")]
            if (j + 2) * 128 < 512:
                masks.append(((j + 2) * 128, "M_WIN"))
            items.append((kc, j * 128, qhi, masks))
    return items

def band_full(qt):
    return [(kc, 0, 512, []) for kc in range(NTILE)]

L1_CONSTS = ["SWAP", "ONES_R", "ONES_C", "HMAP", "BMAP", "M_CAUS", "M_WIN", "E64BC", "IDN"]

def build_l0a():
    P = new_prog()
    ctx = Ctx(P)
    x_in = din(P, "x", [S, D], F32)
    w_n1 = din(P, "n1w", [1, D], F32)
    w_n2 = din(P, "n2w", [1, D], F32)
    w_router = din(P, "wr", [D, E], F32)
    w_gq = din(P, "w_gq", [D, D], F32)
    w_kvd = din(P, "w_kvd", [D, LAT], F32)
    w_kup = din(P, "w_kup", [LAT, D], F32)
    w_vup = din(P, "w_vup", [LAT, LAT], F32)
    w_gwo = din(P, "w_gwo", [D, D], F32)
    w_lq = din(P, "w_lq", [D, D], F32)
    w_lk = din(P, "w_lk", [D, D], F32)
    w_lv = din(P, "w_lv", [D, LAT], F32)
    w_lwo = din(P, "w_lwo", [D, D], F32)
    o_xn = dout(P, "xn0", [S, D], BF16)
    o_logits = dout(P, "logits0", [S, E], F32)
    o_s0 = dout(P, "s0", [S, D], F32)
    a_dram = dtmp(P, "a_dram", [S, D], F32)
    tsc = dtmp(P, "tscratch", [S, D], BF16)
    tsc2 = dtmp(P, "tscratch2", [S, D], BF16)

    ctx.loadc(L1_CONSTS)
    c_csg = din(P, "c_CSG", [128, S], F32)
    c_sng = din(P, "c_SNG", [128, S], F32)
    c_csl = din(P, "c_CSL", [128, S], F32)
    c_snl = din(P, "c_SNL", [128, S], F32)
    CS_cur = sbuf(P, "CS_cur", [128, S], BF16)
    SN_cur = sbuf(P, "SN_cur", [128, S], BF16)
    wq_sb = sbuf(P, "wq_sb", [128, NCHUNK * D], BF16)
    wo_sb = sbuf(P, "wo_sb", [128, NCHUNK * D], BF16)
    wk_sb = sbuf(P, "wk_sb", [128, NCHUNK * D], BF16)
    wv_sb = sbuf(P, "wv_sb", [128, NCHUNK * LAT], BF16)
    wvu_sb = sbuf(P, "wvu_sb", [128, 2 * LAT], BF16)
    wr_sb = sbuf(P, "wr_sb", [128, NCHUNK * E], F32)
    xtT = sbuf(P, "xtT", [128, 128], F32)
    xT = sbuf(P, "xT", [128, NCHUNK * S], BF16)
    QT = sbuf(P, "QT", [128, NCHUNK * S], BF16)
    KTc = sbuf(P, "KTc", [128, S], BF16)
    latT = sbuf(P, "latT", [128, 2 * S], BF16)
    OT = sbuf(P, "OT", [128, NCHUNK * S], BF16)
    xnbf = OT
    V_sb = sbuf(P, "V_sb", [128, NTILE * 4 * 65], BF16)
    nwbc = sbuf(P, "nwbc", [128, D], BF16)
    atile = sbuf(P, "atile", [128, D], F32)
    xtile = atile
    logbuf = sbuf(P, "logbuf", [128, NTILE * E], F32)

    def load_w(dst, ap, rows, cols):
        nch = rows // 128
        P.dma("gpsimd", dst[:, 0:nch * cols].rearrange("p (ci j) -> p ci j", j=cols),
              ap.rearrange("(ci p) j -> p ci j", p=128), ctx.gdma)

    load_w(wq_sb, w_gq, D, D)
    load_w(wo_sb, w_gwo, D, D)
    load_w(wk_sb, w_kup, LAT, D)
    load_w(wv_sb, w_kvd, D, LAT)
    load_w(wvu_sb, w_vup, LAT, LAT)
    P.dma("sync", wr_sb[:, :].rearrange("p (ci j) -> p ci j", j=E),
          w_router.rearrange("(ci p) j -> p ci j", p=128), ctx.sdma)


    def get_x(tt):
        P.wait("sync", ctx.dve)
        P.dma("sync", xtile[:, :], x_in[tt * 128:(tt + 1) * 128, :], ctx.sdma)
        P.wait("vector", ctx.sdma)
        return xtile[:, :]

    def put_a(tt, src_ap):
        P.wait("sync", ctx.dve)
        P.dma("sync", a_dram[tt * 128:(tt + 1) * 128, :], src_ap, ctx.sdma)
        P.wait("vector", ctx.sdma)
        P.wait("tensor", ctx.dve)
        P.wait("tensor", ctx.sdma)
        P.ins("vector", "memset", logbuf[:, tt * E:(tt + 1) * E], 0.0, sem=ctx.dve)
        for cc in range(NCHUNK):
            pst = ctx.psum[4]
            P.wait("tensor", ctx.act)
            P.wait("tensor", ctx.dve)
            P.ins("tensor", "transpose", pst[:, 0:128], src_ap[:, cc * 128:(cc + 1) * 128],
                  ctx.C["IDN"][:, :], sem=ctx.pe)
            P.wait("scalar", ctx.pe)
            P.ins("scalar", "activation", xtT[:, :], pst[:, 0:128], AF.Copy, sem=ctx.act)
            P.wait("tensor", ctx.act)
            P.mm(ctx.psum[5][0:128, 0:E], xtT[:, :], wr_sb[:, cc * E:(cc + 1) * E],
                 True, True, sem=ctx.pe)
            P.wait("vector", ctx.pe)
            P.ins("vector", "tensor_add", logbuf[:, tt * E:(tt + 1) * E],
                  logbuf[:, tt * E:(tt + 1) * E], ctx.psum[5][0:128, 0:E], sem=ctx.dve)
        P.wait("vector", ctx.pe)

    emit_norm(ctx, "ln", get_x, xnbf, eps=1e-5, out_fp_tile=put_a)
    P.wait("sync", ctx.dve)
    P.dma("sync", o_xn.rearrange("(t p) j -> p t j", p=128), xnbf[:, :].rearrange("p (t j) -> p t j", j=D), ctx.sdma)
    transpose_to_fmajor(ctx, xnbf, tsc, xT, NCHUNK)
    P.wait("tensor", ctx.adma)
    P.wait("tensor", ctx.gdma)
    P.wait("sync", ctx.act)
    P.dma("sync", o_logits.rearrange("(t p) j -> p t j", p=128),
          logbuf[:, :].rearrange("p (t j) -> p t j", j=E), ctx.sdma)

    def attn_pass(kind):
        glob = kind == "glob"
        P.wait("gpsimd", ctx.dve)
        P.wait("gpsimd", ctx.pe)
        P.dma("gpsimd", CS_cur[:, :], c_csg if glob else c_csl, ctx.gdma)
        P.dma("gpsimd", SN_cur[:, :], c_sng if glob else c_snl, ctx.gdma)
        if FUSE["on"] and FUSE.get("cc_early"):
            FUSE["cc_early"]()
            FUSE["cc_early"] = None
        CS, SN = CS_cur, SN_cur
        bcast_row(ctx, w_n1 if glob else w_n2, nwbc, D)
        P.wait("vector", ctx.gdma)
        P.wait("vector", ctx.pe)
        def get_a(tt):
            P.wait("sync", ctx.dve)
            P.dma("sync", atile[:, :], a_dram[tt * 128:(tt + 1) * 128, :], ctx.sdma)
            P.wait("vector", ctx.sdma)
            return atile[:, :]
        emit_norm(ctx, "rms", get_a, xnbf, w_bc=nwbc, eps=EPS0)
        transpose_to_fmajor(ctx, xnbf, tsc, xT, NCHUNK)
        P.wait("tensor", ctx.adma)
        proj_fmajor_rope(ctx, xT, NCHUNK, wq_sb, D, list(range(NCHUNK)), QT, CS, SN, True)
        if glob:
            for c2 in range(2):
                for qt in range(NQT):
                    ps = ctx.psum[0]
                    P.wait("tensor", ctx.act)
                    for ci in range(NCHUNK):
                        P.mm(ps[:, :], wv_sb[:, ci * LAT + c2 * 128: ci * LAT + c2 * 128 + 128],
                             xT[:, ci * S + qt * 512: ci * S + qt * 512 + 512],
                             ci == 0, ci == NCHUNK - 1, sem=ctx.pe)
                    P.wait("scalar", ctx.pe)
                    P.ins("scalar", "activation", latT[:, c2 * S + qt * 512: c2 * S + qt * 512 + 512],
                          ps[:, :], AF.Copy, sem=ctx.act)
            P.wait("vector", ctx.gdma)
            P.ins("vector", "memset", V_sb[:, :], 1.0, sem=ctx.dve)
            for kt in range(NTILE):
                ps = ctx.psum[1]
                P.wait("tensor", ctx.act)
                P.wait("tensor", ctx.dve)
                for ci in range(2):
                    P.mm(ps[:, 0:LAT], latT[:, ci * S + kt * 128: ci * S + kt * 128 + 128],
                         wvu_sb[:, ci * LAT:(ci + 1) * LAT], ci == 0, ci == 1, sem=ctx.pe)
                P.wait("vector", ctx.pe)
                dst = V_sb[:, kt * 260: kt * 260 + 260].rearrange("p (s j) -> p s j", j=65)[:, :, 0:64]
                P.ins("vector", "tensor_copy", dst, ps[:, 0:LAT].rearrange("p (s j) -> p s j", j=64),
                      sem=ctx.dve)
        else:
            P.ins("vector", "memset", V_sb[:, :], 1.0, sem=ctx.dve)
            for kt in range(NTILE):
                ps = ctx.psum[1]
                P.wait("tensor", ctx.act)
                P.wait("tensor", ctx.dve)
                for ci in range(NCHUNK):
                    P.mm(ps[:, 0:LAT], xT[:, ci * S + kt * 128: ci * S + kt * 128 + 128],
                         wv_sb[:, ci * LAT:(ci + 1) * LAT], ci == 0, ci == NCHUNK - 1, sem=ctx.pe)
                P.wait("vector", ctx.pe)
                dst = V_sb[:, kt * 260: kt * 260 + 260].rearrange("p (s j) -> p s j", j=65)[:, :, 0:64]
                P.ins("vector", "tensor_copy", dst, ps[:, 0:LAT].rearrange("p (s j) -> p s j", j=64),
                      sem=ctx.dve)
        band = band_causal if glob else band_window
        for c in range(NCHUNK):
            n_cin = 2 if glob else NCHUNK
            srcT = latT if glob else xT
            proj_fmajor_rope(ctx, srcT, n_cin, wk_sb, D, [c], KTc, CS, SN, True, vhead_of={0: c},
                             out_stride=S)
            heads = {2 * c: (2 * c) // 4, 2 * c + 1: (2 * c + 1) // 4}
            attn_core(ctx, QT, lambda cc: KTc[:, :], V_sb, 4, OT, heads, "exp", band, chunks=[c])
        dst_dram = a_dram if glob else o_s0
        for tt in range(NTILE):
            P.wait("sync", ctx.dve)
            P.dma("sync", atile[:, :], a_dram[tt * 128:(tt + 1) * 128, :], ctx.sdma)
            P.wait("vector", ctx.sdma)
            for half in range(2):
                ps = ctx.psum[2 + half]
                P.wait("tensor", ctx.dve)
                for ci in range(NCHUNK):
                    P.mm(ps[:, :], OT[:, ci * S + tt * 128: ci * S + tt * 128 + 128],
                         wo_sb[:, ci * D + half * 512: ci * D + half * 512 + 512],
                         ci == 0, ci == NCHUNK - 1, sem=ctx.pe)
                P.wait("vector", ctx.pe)
                P.ins("vector", "tensor_add", atile[:, half * 512:(half + 1) * 512],
                      atile[:, half * 512:(half + 1) * 512], ps[:, :], sem=ctx.dve)
            P.wait("sync", ctx.dve)
            P.dma("sync", dst_dram[tt * 128:(tt + 1) * 128, :], atile[:, :], ctx.sdma)
            P.wait("vector", ctx.sdma)

    attn_pass("glob")
    P.wait("gpsimd", ctx.sdma)
    P.wait("gpsimd", ctx.pe)
    load_w(wq_sb, w_lq, D, D)
    load_w(wo_sb, w_lwo, D, D)
    load_w(wk_sb, w_lk, D, D)
    load_w(wv_sb, w_lv, D, LAT)
    P.wait("tensor", ctx.gdma)
    P.wait("scalar", ctx.gdma)
    P.wait("vector", ctx.gdma)
    attn_pass("loc")
    P.wait("gpsimd", ctx.sdma)
    P.wait("gpsimd", ctx.adma)
    P.finalize()
    return P

def prep_l0a_weights(inp):
    PC = perm_cols()
    P64 = _perm64()
    def rep_perm(w):
        out = np.empty((w.shape[0], D), np.float32)
        for h in range(H):
            kv = h // 4
            out[:, h * 64:(h + 1) * 64] = w[:, kv * 64 + P64]
        return out
    d = {}
    d["w_gq"] = np.ascontiguousarray(np.asarray(inp["L0_attn_g_wq"], np.float32)[:, PC])
    d["w_kvd"] = np.asarray(inp["L0_attn_g_kv_down"], np.float32)
    d["w_kup"] = np.ascontiguousarray(rep_perm(np.asarray(inp["L0_attn_g_k_up"], np.float32)))
    d["w_vup"] = np.asarray(inp["L0_attn_g_v_up"], np.float32)
    d["w_gwo"] = np.asarray(inp["L0_attn_g_wo"], np.float32)
    d["w_lq"] = np.ascontiguousarray(np.asarray(inp["L0_attn_l_wq"], np.float32)[:, PC])
    d["w_lk"] = np.ascontiguousarray(rep_perm(np.asarray(inp["L0_attn_l_wk"], np.float32)))
    d["w_lv"] = np.asarray(inp["L0_attn_l_wv"], np.float32)
    d["w_lwo"] = np.asarray(inp["L0_attn_l_wo"], np.float32)
    d["n1w"] = np.asarray(inp["L0_attn_norm1_w"], np.float32).reshape(1, D)
    d["n2w"] = np.asarray(inp["L0_attn_norm2_w"], np.float32).reshape(1, D)
    d["wr"] = np.asarray(inp["L0_ffn_w_router"], np.float32)
    for nm in L1_CONSTS + ["CSG", "SNG", "CSL", "SNL"]:
        d["c_" + nm] = CONSTS[nm]
    return d

L2_CONSTS = ["TRIU_INC", "TRIU_EXCF", "ONES_R", "ONES_C", "ONES_CF", "IOTA8R"]
TG = 8192
NTT = TG // 128

def build_l0f():
    P = new_prog()
    ctx = Ctx(P)
    lg_in = din(P, "logits_g", [TG, E], F32)
    xn_in = din(P, "xn_g", [TG, D], BF16)
    wi_in = [din(P, f"wi{i}", [D, FF], F32) for i in range(2)]
    wo_in = [din(P, f"wo{i}", [FF, D], F32) for i in range(2)]
    meta_in = din(P, "meta", [128, 2], F32)
    o_out = dout(P, "out_all", [2 * 640, D], BF16)
    o_gidx = dout(P, "gidx", [TG, 2], I32)
    o_gate = dout(P, "gates", [TG, 2], F32)
    xe_dram = dtmp(P, "xe_dram", [2 * 640, D], BF16)
    bscr = dtmp(P, "bscr", [1, 512], F32)
    ctx.loadc(L2_CONSTS)

    lgt = sbuf(P, "lgt", [128, NTT * E], F32)
    prb = sbuf(P, "prb", [128, NTT * E], F32)
    cnt = sbuf(P, "cnt", [128, NTT * E], F32)
    selb = sbuf(P, "selb", [128, NTT * E], BF16)
    mscr = sbuf(P, "mscr", [128, NTT * E], F32)
    ek = sbuf(P, "ek", [128, 2 * NTT], F32)
    pk = sbuf(P, "pk", [128, 2 * NTT], F32)
    vk = sbuf(P, "vk", [128, 2 * NTT], F32)
    gat = sbuf(P, "gat", [128, 2 * NTT], F32)
    didx = sbuf(P, "didx", [128, 2 * NTT], I32)
    gidx_sb = sbuf(P, "gidx_sb", [128, 2 * 16], I32)
    meta = sbuf(P, "meta_sb", [128, 2], F32)
    rowa = sbuf(P, "rowa", [1, 512], F32)
    t64 = sbuf(P, "t64", [64, E], F32)
    o64 = sbuf(P, "o64", [64, E], F32)
    tot0 = sbuf(P, "tot0", [1, E], F32)
    stg = [sbuf(P, f"stg{i}", [128, D], BF16) for i in range(2)]
    zt = sbuf(P, "ztile", [128, D], BF16)
    xeT = sbuf(P, "xeT", [128, NCHUNK * 640], BF16)
    wi_sl = [sbuf(P, f"wi_sl{i}", [128, NCHUNK * 512], BF16) for i in range(2)]
    wo_sl = [sbuf(P, f"wo_sl{i}", [128, 4 * D], BF16) for i in range(2)]
    hmT = sbuf(P, "hmT", [128, 4 * 128], BF16)
    ye_sb = sbuf(P, "ye_sb", [128, 5 * D], F32)
    outb = sbuf(P, "outb", [128, D], BF16)

    L3 = lambda t: t.rearrange("p (tt e) -> p tt e", e=E)
    P.dma("sync", lgt[:, :].rearrange("p (tt e) -> p tt e", e=E),
          lg_in.rearrange("(tt p) e -> p tt e", p=128), ctx.sdma)
    P.dma("sync", meta[:, :], meta_in, ctx.sdma)
    P.ins("vector", "memset", zt[:, :], 0.0, sem=ctx.dve)
    P.wait("gpsimd", ctx.dve)
    for i in range(10):
        P.dma("gpsimd", xe_dram[i * 128:(i + 1) * 128, :], zt[:, :], ctx.gdma)
    P.wait("vector", ctx.sdma)
    P.ins("vector", "tensor_reduce", mscr[:, 0:NTT], L3(lgt), axis=AX.X, op=OP.max, sem=ctx.dve)
    P.ins("vector", "tensor_tensor", L3(prb), L3(lgt),
          mscr[:, 0:NTT].unsqueeze(2).to_broadcast([128, NTT, E]), OP.subtract, sem=ctx.dve)
    P.wait("scalar", ctx.dve)
    P.ins("scalar", "activation", prb[:, :], prb[:, :], AF.Exp, sem=ctx.act)
    P.wait("vector", ctx.act)
    P.ins("vector", "tensor_reduce", mscr[:, 0:NTT], L3(prb), axis=AX.X, op=OP.add, sem=ctx.dve)
    P.ins("vector", "reciprocal", mscr[:, NTT:2 * NTT], mscr[:, 0:NTT], sem=ctx.dve)
    P.ins("vector", "tensor_tensor", L3(prb), L3(prb),
          mscr[:, NTT:2 * NTT].unsqueeze(2).to_broadcast([128, NTT, E]), OP.mult, sem=ctx.dve)
    P.ins("vector", "memset", cnt[:, :], 0.0, sem=ctx.dve)
    for dd in range(1, E):
        w = E - dd
        A, B = L3(lgt)[:, :, dd:E], L3(lgt)[:, :, 0:w]
        P.ins("vector", "tensor_tensor", L3(mscr)[:, :, 0:w], B, A, OP.is_ge, sem=ctx.dve)
        P.ins("vector", "tensor_tensor", L3(cnt)[:, :, dd:E], L3(cnt)[:, :, dd:E], L3(mscr)[:, :, 0:w],
              OP.add, sem=ctx.dve)
        P.ins("vector", "tensor_tensor", L3(mscr)[:, :, 0:w], A, B, OP.is_gt, sem=ctx.dve)
        P.ins("vector", "tensor_tensor", L3(cnt)[:, :, 0:w], L3(cnt)[:, :, 0:w], L3(mscr)[:, :, 0:w],
              OP.add, sem=ctx.dve)
    for k in range(2):
        kc = lambda t: t[:, k * NTT:(k + 1) * NTT]
        P.ins("vector", "tensor_scalar", mscr[:, :], cnt[:, :], float(k), None, OP.is_equal, sem=ctx.dve)
        P.ins("vector", "tensor_copy", selb[:, :], mscr[:, :], sem=ctx.dve)
        P.ins("vector", "tensor_tensor", mscr[:, :], mscr[:, :], prb[:, :], OP.mult, sem=ctx.dve)
        P.ins("vector", "tensor_reduce", kc(gat), L3(mscr), axis=AX.X, op=OP.add, sem=ctx.dve)
        P.ins("vector", "tensor_scalar", mscr[:, :], cnt[:, :], float(k), None, OP.is_equal, sem=ctx.dve)
        P.ins("vector", "tensor_tensor", L3(mscr), L3(mscr),
              ctx.C["IOTA8R"][:, 0:E].unsqueeze(1).to_broadcast([128, NTT, E]), OP.mult,
              sem=ctx.dve)
        P.ins("vector", "tensor_reduce", kc(ek), L3(mscr), axis=AX.X, op=OP.add, sem=ctx.dve)
        P.wait("tensor", ctx.dve)
        P.mm(ctx.psum[1][0:1, 0:512], ctx.C["ONES_C"][:, 0:1], selb[:, :], True, True, sem=ctx.pe)
        P.wait("vector", ctx.pe)
        P.ins("vector", "tensor_copy", rowa[0:1, :], ctx.psum[1][0:1, 0:512], sem=ctx.dve)
        P.wait("sync", ctx.dve)
        P.dma("sync", bscr, rowa[0:1, :], ctx.sdma)
        P.wait("sync", ctx.sdma)
        P.dma("sync", t64[0:64, 0:E], bscr.rearrange("a (t e) -> (a t) e", e=E), ctx.sdma)
        P.wait("tensor", ctx.sdma)
        P.mm(ctx.psum[2][0:64, 0:E], ctx.C["TRIU_EXCF"][0:64, 0:64], t64[0:64, 0:E], True,
             k == 0, sem=ctx.pe)
        if k == 0:
            P.mm(ctx.psum[3][0:1, 0:E], ctx.C["ONES_CF"][0:64, 0:1], t64[0:64, 0:E], True, True, sem=ctx.pe)
            P.wait("vector", ctx.pe)
            P.ins("vector", "tensor_copy", tot0[0:1, :], ctx.psum[3][0:1, 0:E], sem=ctx.dve)
        else:
            P.wait("tensor", ctx.dve)
            P.mm(ctx.psum[2][0:64, 0:E], ctx.C["ONES_R"][0:1, 0:64], tot0[0:1, :], False, True, sem=ctx.pe)
        P.wait("vector", ctx.pe)
        P.ins("vector", "tensor_copy", o64[0:64, :], ctx.psum[2][0:64, 0:E], sem=ctx.dve)
        P.wait("sync", ctx.dve)
        P.dma("sync", bscr.rearrange("a (t e) -> (a t) e", e=E), o64[0:64, :], ctx.sdma)
        P.wait("sync", ctx.sdma)
        P.dma("sync", rowa[0:1, :], bscr, ctx.sdma)
        P.wait("tensor", ctx.sdma)
        P.wait("tensor", ctx.dve)
        P.mm(ctx.psum[0][:, 0:512], ctx.C["TRIU_INC"][:, :], selb[:, :], True, False, sem=ctx.pe)
        P.mm(ctx.psum[0][:, 0:512], ctx.C["ONES_R"][0:1, :], rowa[0:1, :], False, True, sem=ctx.pe)
        P.wait("vector", ctx.pe)
        P.ins("vector", "tensor_scalar", mscr[:, :], ctx.psum[0][:, 0:512], 1.0, None, OP.subtract,
              sem=ctx.dve)
        P.ins("vector", "tensor_scalar", selb[:, :], cnt[:, :], float(k), None, OP.is_equal, sem=ctx.dve)
        P.ins("vector", "tensor_tensor", mscr[:, :], mscr[:, :], selb[:, :], OP.mult, sem=ctx.dve)
        P.ins("vector", "tensor_reduce", kc(pk), L3(mscr), axis=AX.X, op=OP.add, sem=ctx.dve)
        P.ins("vector", "tensor_scalar", kc(vk), kc(pk), 640.0, None, OP.is_lt, sem=ctx.dve)
    for k in range(2):
        kc = lambda t: t[:, k * NTT:(k + 1) * NTT]
        P.ins("vector", "tensor_scalar", kc(mscr), kc(ek), 640.0, None, OP.mult, sem=ctx.dve)
        P.ins("vector", "tensor_tensor", kc(mscr), kc(mscr), kc(pk), OP.add, sem=ctx.dve)
        P.ins("vector", "tensor_scalar", kc(mscr), kc(mscr), float(OOB), None, OP.subtract, sem=ctx.dve)
        P.ins("vector", "tensor_tensor", kc(mscr), kc(mscr), kc(vk), OP.mult, sem=ctx.dve)
        P.ins("vector", "tensor_scalar", kc(mscr), kc(mscr), float(OOB), None, OP.add, sem=ctx.dve)
    return P, ctx, dict(lgt=lgt, prb=prb, cnt=cnt, selb=selb, mscr=mscr, ek=ek, pk=pk, vk=vk, gat=gat,
                        didx=didx, gidx_sb=gidx_sb, meta=meta, stg=stg, zt=zt, xeT=xeT, wi_sl=wi_sl,
                        wo_sl=wo_sl, hmT=hmT, ye_sb=ye_sb, outb=outb, xe_dram=xe_dram, xn_in=xn_in,
                        wi_in=wi_in, wo_in=wo_in, o_out=o_out, o_gidx=o_gidx, o_gate=o_gate)

def build_l0f_full():
    P, ctx, B = build_l0f()
    lgt, prb, cnt, selb, mscr = B["lgt"], B["prb"], B["cnt"], B["selb"], B["mscr"]
    ek, pk, vk, gat, didx = B["ek"], B["pk"], B["vk"], B["gat"], B["didx"]
    meta, stg, zt, xeT = B["meta"], B["stg"], B["zt"], B["xeT"]
    wi_sl, wo_sl, hmT, ye_sb, outb = B["wi_sl"], B["wo_sl"], B["hmT"], B["ye_sb"], B["outb"]
    xe_dram, xn_in, wi_in, wo_in = B["xe_dram"], B["xn_in"], B["wi_in"], B["wo_in"]
    o_out, o_gidx, o_gate = B["o_out"], B["o_gidx"], B["o_gate"]
    for k in range(2):
        kc = lambda t: t[:, k * NTT:(k + 1) * NTT]
        P.ins("vector", "tensor_scalar", kc(cnt), kc(ek), meta[:, 0:1], None, OP.is_equal, sem=ctx.dve)
        P.ins("vector", "tensor_scalar", kc(mscr), kc(ek), meta[:, 1:2], None, OP.is_equal, sem=ctx.dve)
        P.ins("vector", "tensor_tensor", kc(cnt), kc(cnt), kc(mscr), OP.add, sem=ctx.dve)
        P.ins("vector", "tensor_tensor", kc(cnt), kc(cnt), kc(vk), OP.mult, sem=ctx.dve)
        P.ins("vector", "tensor_scalar", kc(mscr), kc(mscr), 640.0, None, OP.mult, sem=ctx.dve)
        P.ins("vector", "tensor_tensor", kc(mscr), kc(mscr), kc(pk), OP.add, sem=ctx.dve)
        P.ins("vector", "tensor_scalar", kc(mscr), kc(mscr), float(OOB), None, OP.subtract, sem=ctx.dve)
        P.ins("vector", "tensor_tensor", kc(mscr), kc(mscr), kc(cnt), OP.mult, sem=ctx.dve)
        P.ins("vector", "tensor_scalar", kc(mscr), kc(mscr), float(OOB), None, OP.add, sem=ctx.dve)
        P.ins("vector", "tensor_copy", kc(didx), kc(mscr), sem=ctx.dve)
    P.wait("gpsimd", ctx.dve)
    breg = P.breg(2 * 640 - 1)
    g0 = ctx.gsc.n
    for tt in range(NTT):
        par = tt % 2
        if tt >= 2:
            P.wait("sync", ctx.gsc, g0 + 32 * (tt - 1))
        P.dma("sync", stg[par][:, :], xn_in[tt * 128:(tt + 1) * 128, :], ctx.sdma)
        P.wait("gpsimd", ctx.sdma)
        for k in range(2):
            P.idma(xe_dram, IOOA(didx[:, k * NTT + tt:k * NTT + tt + 1], 0), stg[par][:, :], None,
                   ctx.gsc, bounds=breg, oob_is_err=False)
    gsc2 = sbuf(P, "gsc2", [128, 2 * NTT], F32)
    for k in range(2):
        kc = lambda t: t[:, k * NTT:(k + 1) * NTT]
        P.ins("vector", "tensor_scalar", kc(gsc2), kc(ek), 640.0, None, OP.mult, sem=ctx.dve)
        P.ins("vector", "tensor_tensor", kc(gsc2), kc(gsc2), kc(pk), OP.add, sem=ctx.dve)
        P.ins("vector", "tensor_scalar", kc(gsc2), kc(gsc2), float(OOB), None, OP.subtract, sem=ctx.dve)
        P.ins("vector", "tensor_tensor", kc(gsc2), kc(gsc2), kc(vk), OP.mult, sem=ctx.dve)
        P.ins("vector", "tensor_scalar", kc(gsc2), kc(gsc2), float(OOB), None, OP.add, sem=ctx.dve)
    gidx_i = sbuf(P, "gidx_i", [128, 2 * NTT], I32)
    gat2 = sbuf(P, "gat2", [128, 2 * NTT], F32)
    P.ins("vector", "tensor_copy", gidx_i[:, :].rearrange("p (tt k) -> p k tt", k=2),
          gsc2[:, :].rearrange("p (k tt) -> p k tt", k=2), sem=ctx.dve)
    P.ins("vector", "tensor_copy", gat2[:, :].rearrange("p (tt k) -> p k tt", k=2),
          gat[:, :].rearrange("p (k tt) -> p k tt", k=2), sem=ctx.dve)
    P.wait("sync", ctx.dve)
    P.dma("sync", o_gidx.rearrange("(tt p) k -> p tt k", p=128),
          gidx_i[:, :].rearrange("p (tt k) -> p tt k", k=2), ctx.sdma)
    P.dma("sync", o_gate.rearrange("(tt p) k -> p tt k", p=128),
          gat2[:, :].rearrange("p (tt k) -> p tt k", k=2), ctx.sdma)
    if FUSE["on"]:
        gidxp = dout(P, "gidxp", [TG, 16], I32)
        gatep = dout(P, "gatep", [TG, 16], F32)
        P.dma("sync", gidxp[:, 0:2].rearrange("(tt p) k -> p tt k", p=128),
              gidx_i[:, :].rearrange("p (tt k) -> p tt k", k=2), ctx.sdma)
        P.dma("sync", gatep[:, 0:2].rearrange("(tt p) k -> p tt k", p=128),
              gat2[:, :].rearrange("p (tt k) -> p tt k", k=2), ctx.sdma)
    P.wait("scalar", ctx.gsc)
    for el in range(2):
        P.wait("scalar", ctx.pe)
        for hc in range(NCHUNK):
            P.dma("scalar", xeT[:, hc * 640:(hc + 1) * 640],
                  xe_dram[el * 640:(el + 1) * 640, hc * 128:(hc + 1) * 128], ctx.adma, transpose=True)
        P.wait("vector", ctx.gdma)
        P.ins("vector", "memset", ye_sb[:, :], 0.0, sem=ctx.dve)
        P.wait("tensor", ctx.adma)
        for fs in range(NCHUNK):
            par = fs % 2
            if fs >= 2 or el > 0:
                P.wait("gpsimd", ctx.pe)
            P.dma("gpsimd", wi_sl[par][:, :].rearrange("p (hc f) -> p hc f", f=512),
                  wi_in[el][:, fs * 512:(fs + 1) * 512].rearrange("(hc p) f -> p hc f", p=128), ctx.gdma)
            P.dma("gpsimd", wo_sl[par][:, :].rearrange("p (fc h) -> p fc h", h=D),
                  wo_in[el][fs * 512:(fs + 1) * 512, :].rearrange("(fc p) h -> p fc h", p=128), ctx.gdma)
            P.wait("tensor", ctx.gdma)
            for rt in range(5):
                for fc in range(4):
                    ps = ctx.psum[fc % 2]
                    P.wait("tensor", ctx.act)
                    for hcc in range(NCHUNK):
                        P.mm(ps[0:128, 0:128], wi_sl[par][:, hcc * 512 + fc * 128: hcc * 512 + fc * 128 + 128],
                             xeT[:, hcc * 640 + rt * 128: hcc * 640 + rt * 128 + 128],
                             hcc == 0, hcc == NCHUNK - 1, sem=ctx.pe)
                    P.wait("scalar", ctx.pe)
                    P.ins("scalar", "activation", hmT[:, fc * 128:(fc + 1) * 128], ps[0:128, 0:128],
                          AF.Relu, sem=ctx.act)
                for half in range(2):
                    ps = ctx.psum[4 + half]
                    P.wait("tensor", ctx.act)
                    P.wait("tensor", ctx.dve)
                    for fc in range(4):
                        P.mm(ps[:, :], hmT[:, fc * 128:(fc + 1) * 128],
                             wo_sl[par][:, fc * D + half * 512: fc * D + half * 512 + 512],
                             fc == 0, fc == 3, sem=ctx.pe)
                    P.wait("vector", ctx.pe)
                    sl = ye_sb[:, rt * D + half * 512: rt * D + half * 512 + 512]
                    P.ins("vector", "tensor_add", sl, sl, ps[:, :], sem=ctx.dve)
        for rt in range(5):
            P.ins("vector", "tensor_copy", outb[:, :], ye_sb[:, rt * D:(rt + 1) * D], sem=ctx.dve)
            P.wait("sync", ctx.dve)
            P.dma("sync", o_out[el * 640 + rt * 128: el * 640 + (rt + 1) * 128, :], outb[:, :], ctx.sdma)
            P.wait("vector", ctx.sdma)
    P.wait("gpsimd", ctx.sdma)
    P.wait("gpsimd", ctx.adma)
    P.finalize()
    return P

L3_CONSTS = ["SWAP", "ONES_R", "ONES_C", "HMAP", "BMAP", "E64BC", "CMAP", "SMAP", "MELB", "IDN"]

def build_l1a(upto="all"):
    P = new_prog()
    ctx = Ctx(P, rstd=False)
    x_in = din(P, "x", [S, D], F32)
    s0_in = din(P, "s0", [S, D], F32)
    oa_in = din(P, "out_all_g", [5120, D], BF16)
    gi_in = din(P, "gidx", [S, 2], I32)
    ga_in = din(P, "gates", [S, 2], F32)
    lnw_in = din(P, "ln_w", [1, D], F32)
    lnb_in = din(P, "ln_b", [1, D], F32)
    nw_in = din(P, "nw", [1, D], F32)
    wq_in = din(P, "w_q", [D, D], F32)
    wk_in = din(P, "w_k", [D, D], F32)
    wv_in = din(P, "w_v", [D, D], F32)
    wo_in = din(P, "w_o", [D, D], F32)
    rwT_in = din(P, "rwT", [D, E], F32)
    rb_in = din(P, "rb", [1, E], F32)
    th_in = din(P, "theta", [1, 1], F32)
    o_x1 = dout(P, "x1", [S, D], F32)
    o_xnn = dout(P, "xnn", [S, D], BF16)
    o_lg = dout(P, "logits1", [S, E], F32)
    o_s1 = dout(P, "s1", [S, D], F32)
    tsc = dtmp(P, "tscratch", [S, D], BF16)
    tsc2 = dtmp(P, "tscratch2", [S, D], BF16)
    ctx.loadc(L3_CONSTS)
    t32_in = din(P, "c_T32", [32, S], F32)
    CS_cur = sbuf(P, "CS_cur", [128, S], BF16)
    SN_cur = sbuf(P, "SN_cur", [128, S], BF16)
    wA = sbuf(P, "wA", [128, NCHUNK * D], BF16)
    wB = sbuf(P, "wB", [128, NCHUNK * D], BF16)
    wr_sb = sbuf(P, "wr_sb", [128, NCHUNK * E], F32)
    xT = sbuf(P, "xT", [128, NCHUNK * S], BF16)
    QT = sbuf(P, "QT", [128, NCHUNK * S], BF16)
    KTc = sbuf(P, "KTc", [128, S], BF16)
    OT = sbuf(P, "OT", [128, NCHUNK * S], BF16)
    xnbf = OT
    V_sb = sbuf(P, "V_sb", [128, NTILE * 16 * 65], BF16)
    lnwbc = sbuf(P, "lnwbc", [128, D], BF16)
    lnbbc = sbuf(P, "lnbbc", [128, D], BF16)
    nwbc = sbuf(P, "nwbc", [128, D], F32)
    lgacc = sbuf(P, "lgacc", [128, E], F32)
    rbbc = sbuf(P, "rbbc", [128, E], F32)
    atile = sbuf(P, "atile", [128, D], F32)
    btile = sbuf(P, "btile", [128, D], F32)
    r0 = KTc[:, 0:D]
    r1b = KTc[:, D:2 * D]
    xtT = btile[:, 0:128]
    gidx_sb = sbuf(P, "gidx_sb", [128, NTILE * 2], I32)
    gat_sb = sbuf(P, "gat_sb", [128, NTILE * 2], F32)

    def load_w(dst, ap, rows, cols):
        nch = rows // 128
        P.dma("gpsimd", dst[:, 0:nch * cols].rearrange("p (ci j) -> p ci j", j=cols),
              ap.rearrange("(ci p) j -> p ci j", p=128), ctx.gdma)

    load_w(wA, wq_in, D, D)
    P.dma("sync", wr_sb[:, :].rearrange("p (ci j) -> p ci j", j=E),
          rwT_in.rearrange("(ci p) j -> p ci j", p=128), ctx.sdma)
    bcast_row(ctx, lnw_in, ctx.scr1, D)
    P.ins("vector", "tensor_copy", lnwbc[:, :], ctx.scr1[:, :], sem=ctx.dve)
    bcast_row(ctx, lnb_in, ctx.scr1, D)
    P.ins("vector", "tensor_copy", lnbbc[:, :], ctx.scr1[:, :], sem=ctx.dve)
    bcast_row(ctx, nw_in, ctx.scr1, D)
    P.ins("vector", "tensor_copy", nwbc[:, :], ctx.scr1[:, :], sem=ctx.dve)
    bcast_row(ctx, rb_in, ctx.scr1, E)
    P.ins("vector", "tensor_copy", rbbc[:, :], ctx.scr1[:, 0:E], sem=ctx.dve)
    if FUSE["on"]:
        rsel_in = din(P, "rowsel", [128, NTILE], I32)
        gidxp_in = din(P, "gidxp", [TG, 16], I32)
        gatep_in = din(P, "gatep", [TG, 16], F32)
        rsel_sb = sbuf(P, "rsel_sb", [128, NTILE], I32)
        gtmp_i = sbuf(P, "gtmp_i", [128, 16], I32)
        gtmp_f = sbuf(P, "gtmp_f", [128, 16], F32)
        P.dma("sync", rsel_sb[:, :], rsel_in, ctx.sdma)
        P.wait("gpsimd", ctx.sdma)
        brg = P.breg(TG - 1)
        for tt in range(NTILE):
            P.wait("gpsimd", ctx.dve)
            P.idma(gtmp_i[:, :], None, gidxp_in, IOOA(rsel_sb[:, tt:tt + 1], 0), ctx.gsc,
                   bounds=brg, oob_is_err=False)
            P.idma(gtmp_f[:, :], None, gatep_in, IOOA(rsel_sb[:, tt:tt + 1], 0), ctx.gsc,
                   bounds=brg, oob_is_err=False)
            P.wait("vector", ctx.gsc)
            P.ins("vector", "tensor_copy", gidx_sb[:, 2 * tt:2 * tt + 2], gtmp_i[:, 0:2], sem=ctx.dve)
            P.ins("vector", "tensor_copy", gat_sb[:, 2 * tt:2 * tt + 2], gtmp_f[:, 0:2], sem=ctx.dve)
    else:
        P.dma("sync", gidx_sb[:, :].rearrange("p (tt k) -> p tt k", k=2),
              gi_in.rearrange("(tt p) k -> p tt k", p=128), ctx.sdma)
        P.dma("sync", gat_sb[:, :].rearrange("p (tt k) -> p tt k", k=2),
              ga_in.rearrange("(tt p) k -> p tt k", p=128), ctx.sdma)

    if upto == "bc":
        P.wait("gpsimd", ctx.sdma)
        P.wait("gpsimd", ctx.gdma)
        P.finalize()
        return P
    t32_pc = atile[0:32, 0:512]
    th_bc = sbuf(P, "th_bc", [32, 1], F32)
    P.dma("sync", th_bc[0:1, 0:1], th_in, ctx.sdma)
    P.wait("tensor", ctx.sdma)
    P.wait("tensor", ctx.gdma)
    ps = ctx.psum[6]
    P.wait("tensor", ctx.dve)
    P.mm(ps[0:32, 0:1], ctx.C["ONES_R"][0:1, 0:32], th_bc[0:1, 0:1], True, True, sem=ctx.pe)
    P.wait("vector", ctx.pe)
    P.ins("vector", "tensor_copy", th_bc[0:32, 0:1], ps[0:32, 0:1], sem=ctx.dve)
    P.ins("vector", "tensor_mul", th_bc[0:32, 0:1], th_bc[0:32, 0:1], ctx.C["MELB"][0:32, 0:1], sem=ctx.dve)
    TWO_PI = float(2 * np.pi)
    C2PI_HI = float(np.float32(TWO_PI))
    C2PI_LO = float(TWO_PI - np.float64(np.float32(TWO_PI)))
    ang_pc = ctx.scr1[0:32, 0:512]
    kf_pc = ctx.scr1[0:32, 512:1024]
    for qt in range(NQT):
        P.wait("sync", ctx.pe)
        P.wait("sync", ctx.dve)
        P.dma("sync", t32_pc[0:32, :], t32_in[:, qt * 512:(qt + 1) * 512], ctx.sdma)
        P.wait("vector", ctx.sdma)
        for trig in range(2):
            P.wait("vector", ctx.pe)
            P.ins("vector", "tensor_scalar", ang_pc, t32_pc[0:32, :], th_bc[0:32, 0:1], None, OP.mult,
                  sem=ctx.dve)
            if trig == 1:
                P.ins("vector", "tensor_scalar", ang_pc, ang_pc, float(np.pi / 2), None, OP.add, sem=ctx.dve)
            P.ins("vector", "tensor_scalar", kf_pc, ang_pc, 1.0 / TWO_PI, None, OP.mult, sem=ctx.dve)
            MAGIC = float(1.5 * 2 ** 23)
            P.ins("vector", "tensor_scalar", kf_pc, kf_pc, MAGIC, None, OP.add, sem=ctx.dve)
            P.ins("vector", "tensor_scalar", kf_pc, kf_pc, MAGIC, None, OP.subtract, sem=ctx.dve)
            P.ins("vector", "scalar_tensor_tensor", ang_pc, kf_pc, -C2PI_HI, ang_pc, OP.mult, OP.add,
                  sem=ctx.dve)
            P.ins("vector", "scalar_tensor_tensor", ang_pc, kf_pc, -C2PI_LO, ang_pc, OP.mult, OP.add,
                  sem=ctx.dve)
            P.ins("vector", "tensor_scalar", kf_pc, ang_pc, float(np.pi), None, OP.is_gt, sem=ctx.dve)
            P.ins("vector", "scalar_tensor_tensor", ang_pc, kf_pc, -C2PI_HI, ang_pc, OP.mult, OP.add,
                  sem=ctx.dve)
            P.ins("vector", "tensor_scalar", kf_pc, ang_pc, float(-np.pi), None, OP.is_lt, sem=ctx.dve)
            P.ins("vector", "scalar_tensor_tensor", ang_pc, kf_pc, C2PI_HI, ang_pc, OP.mult, OP.add,
                  sem=ctx.dve)
            P.wait("scalar", ctx.dve)
            P.ins("scalar", "activation", ang_pc, ang_pc, AF.Sin, sem=ctx.act)
            P.wait("tensor", ctx.act)
            P.wait("tensor", ctx.dve)
            pst = ctx.psum[5 + trig]
            MAPN = "SMAP" if trig == 0 else "CMAP"
            P.mm(pst[:, :], ctx.C[MAPN][0:32, :], ang_pc, True, True, sem=ctx.pe)
            P.wait("scalar", ctx.pe)
            dstt = (SN_cur if trig == 0 else CS_cur)[:, qt * 512:(qt + 1) * 512]
            P.ins("scalar", "activation", dstt, pst[:, :], AF.Copy, sem=ctx.act)
    if upto == "tab":
        P.wait("gpsimd", ctx.sdma)
        P.wait("gpsimd", ctx.adma)
        P.finalize()
        return P
    breg = P.breg(5120 - 1)
    for tt in range(NTILE):
        P.wait("vector", ctx.gsc)
        P.ins("vector", "memset", r0[:, :], 0.0, sem=ctx.dve)
        P.ins("vector", "memset", r1b[:, :], 0.0, sem=ctx.dve)
        P.wait("gpsimd", ctx.dve)
        P.wait("gpsimd", ctx.sdma)
        P.idma(r0[:, :], None, oa_in, IOOA(gidx_sb[:, 2 * tt:2 * tt + 1], 0), ctx.gsc, bounds=breg,
               oob_is_err=False)
        P.idma(r1b[:, :], None, oa_in, IOOA(gidx_sb[:, 2 * tt + 1:2 * tt + 2], 0), ctx.gsc, bounds=breg,
               oob_is_err=False)
        P.wait("sync", ctx.dve)
        P.wait("sync", ctx.pe)
        P.dma("sync", atile[:, :], x_in[tt * 128:(tt + 1) * 128, :], ctx.sdma)
        P.dma("sync", btile[:, :], s0_in[tt * 128:(tt + 1) * 128, :], ctx.sdma)
        P.wait("vector", ctx.gsc)
        P.wait("vector", ctx.sdma)
        P.ins("vector", "tensor_scalar", ctx.scr1[:, :], r0[:, :], gat_sb[:, 2 * tt:2 * tt + 1], None,
              OP.mult, sem=ctx.dve)
        P.ins("vector", "scalar_tensor_tensor", ctx.scr1[:, :], r1b[:, :], gat_sb[:, 2 * tt + 1:2 * tt + 2],
              ctx.scr1[:, :], OP.mult, OP.add, sem=ctx.dve)
        P.wait("scalar", ctx.dve)
        P.ins("scalar", "activation", ctx.scr1[:, :], ctx.scr1[:, :], AF.Silu, sem=ctx.act)
        P.wait("vector", ctx.act)
        P.ins("vector", "tensor_mul", btile[:, :], btile[:, :], ctx.scr1[:, :], sem=ctx.dve)
        P.ins("vector", "tensor_add", atile[:, :], atile[:, :], btile[:, :], sem=ctx.dve)
        P.wait("sync", ctx.dve)
        P.dma("sync", o_x1[tt * 128:(tt + 1) * 128, :], atile[:, :], ctx.sdma)
        P.wait("vector", ctx.sdma)
        st = ctx.stat
        P.ins("vector", "reduce_sum", st[:, 0:1], atile[:, :], axis=AX.X, sem=ctx.dve)
        P.ins("vector", "tensor_scalar", st[:, 0:1], st[:, 0:1], 1.0 / D, None, OP.mult, sem=ctx.dve)
        P.ins("vector", "tensor_scalar", atile[:, :], atile[:, :], st[:, 0:1], None, OP.subtract, sem=ctx.dve)
        P.wait("scalar", ctx.dve)
        P.ins("scalar", "activation", btile[:, :], atile[:, :], AF.Square, accum_out=st[:, 1:2], sem=ctx.act)
        P.wait("vector", ctx.act)
        P.ins("vector", "tensor_scalar", st[:, 1:2], st[:, 1:2], 1.0 / D, None, OP.mult, sem=ctx.dve)
        P.wait("scalar", ctx.dve)
        P.ins("scalar", "activation", st[:, 2:3], st[:, 1:2], AF.Ln, bias=ctx.bias(EPS1), sem=ctx.act)
        P.ins("scalar", "activation", st[:, 2:3], st[:, 2:3], AF.Exp, scale=-0.5, sem=ctx.act)
        P.wait("vector", ctx.act)
        P.ins("vector", "tensor_scalar", atile[:, :], atile[:, :], st[:, 2:3], None, OP.mult, sem=ctx.dve)
        P.wait("scalar", ctx.dve)
        P.ins("scalar", "activation", btile[:, :], atile[:, :], AF.Square, accum_out=st[:, 1:2], sem=ctx.act)
        P.wait("vector", ctx.act)
        P.ins("vector", "tensor_scalar", st[:, 1:2], st[:, 1:2], 1.0 / D, None, OP.mult, sem=ctx.dve)
        P.wait("scalar", ctx.dve)
        P.ins("scalar", "activation", st[:, 3:4], st[:, 1:2], AF.Ln, bias=ctx.bias(EPS1), sem=ctx.act)
        P.ins("scalar", "activation", st[:, 3:4], st[:, 3:4], AF.Exp, scale=-0.5, sem=ctx.act)
        P.wait("vector", ctx.act)
        P.ins("vector", "tensor_scalar", ctx.scr1[:, :], atile[:, :], st[:, 3:4], None, OP.mult, sem=ctx.dve)
        P.ins("vector", "tensor_mul", ctx.scr1[:, :], ctx.scr1[:, :], nwbc[:, :], sem=ctx.dve)
        P.ins("vector", "tensor_copy", r0[:, :], ctx.scr1[:, :], sem=ctx.dve)
        P.wait("sync", ctx.dve)
        P.dma("sync", o_xnn[tt * 128:(tt + 1) * 128, :], r0[:, :], ctx.sdma)
        P.wait("tensor", ctx.dve)
        P.wait("tensor", ctx.sdma)
        P.wait("vector", ctx.sdma)
        P.ins("vector", "tensor_copy", lgacc[:, 0:E], rbbc[:, 0:E], sem=ctx.dve)
        for cc in range(NCHUNK):
            pst = ctx.psum[4]
            P.wait("tensor", ctx.act)
            P.wait("tensor", ctx.dve)
            P.ins("tensor", "transpose", pst[:, 0:128], ctx.scr1[:, cc * 128:(cc + 1) * 128],
                  ctx.C["IDN"][:, :], sem=ctx.pe)
            P.wait("scalar", ctx.pe)
            P.ins("scalar", "activation", xtT[:, :], pst[:, 0:128], AF.Copy, sem=ctx.act)
            P.wait("tensor", ctx.act)
            P.mm(ctx.psum[5][0:128, 0:E], xtT[:, :], wr_sb[:, cc * E:(cc + 1) * E],
                 True, True, sem=ctx.pe)
            P.wait("vector", ctx.pe)
            P.ins("vector", "tensor_add", lgacc[:, 0:E], lgacc[:, 0:E],
                  ctx.psum[5][0:128, 0:E], sem=ctx.dve)
        P.wait("sync", ctx.dve)
        P.dma("sync", o_lg[tt * 128:(tt + 1) * 128, :], lgacc[:, 0:E], ctx.sdma)
        P.ins("vector", "reduce_sum", st[:, 0:1], atile[:, :], axis=AX.X, sem=ctx.dve)
        P.ins("vector", "tensor_scalar", st[:, 0:1], st[:, 0:1], 1.0 / D, None, OP.mult, sem=ctx.dve)
        P.ins("vector", "tensor_scalar", atile[:, :], atile[:, :], st[:, 0:1], None, OP.subtract, sem=ctx.dve)
        P.wait("scalar", ctx.dve)
        P.wait("scalar", ctx.pe)
        P.ins("scalar", "activation", btile[:, :], atile[:, :], AF.Square, accum_out=st[:, 1:2], sem=ctx.act)
        P.wait("vector", ctx.act)
        P.ins("vector", "tensor_scalar", st[:, 1:2], st[:, 1:2], 1.0 / D, None, OP.mult, sem=ctx.dve)
        P.wait("scalar", ctx.dve)
        P.ins("scalar", "activation", st[:, 2:3], st[:, 1:2], AF.Ln, bias=ctx.bias(EPS1), sem=ctx.act)
        P.ins("scalar", "activation", st[:, 2:3], st[:, 2:3], AF.Exp, scale=-0.5, sem=ctx.act)
        P.wait("vector", ctx.act)
        P.ins("vector", "tensor_scalar", atile[:, :], atile[:, :], st[:, 2:3], None, OP.mult, sem=ctx.dve)
        P.ins("vector", "tensor_mul", atile[:, :], atile[:, :], lnwbc[:, :], sem=ctx.dve)
        P.ins("vector", "tensor_add", atile[:, :], atile[:, :], lnbbc[:, :], sem=ctx.dve)
        P.ins("vector", "tensor_copy", r1b[:, :], atile[:, :], sem=ctx.dve)
        P.wait("sync", ctx.dve)
        P.dma("sync", tsc[tt * 128:(tt + 1) * 128, :], r1b[:, :], ctx.sdma)
        P.wait("vector", ctx.sdma)
    if upto == "comb":
        P.wait("gpsimd", ctx.sdma)
        P.wait("gpsimd", ctx.adma)
        P.finalize()
        return P
    P.wait("scalar", ctx.sdma)
    for c in range(NCHUNK):
        P.dma("scalar", xT[:, c * S:(c + 1) * S], tsc[:, c * 128:(c + 1) * 128], ctx.adma, transpose=True)
    P.wait("tensor", ctx.adma)
    if upto == "logits":
        P.wait("gpsimd", ctx.sdma)
        P.wait("gpsimd", ctx.adma)
        P.finalize()
        return P
    P.wait("vector", ctx.sdma)
    P.wait("vector", ctx.adma)
    ctx.C["p_sb"] = btile[:, :].bitcast(BF16)[:, 0:512]
    ctx.C["p_sb2"] = btile[:, :].bitcast(BF16)[:, 1024:1536]
    ctx.C["zt"] = btile[:, :].bitcast(BF16)[:, 512:1024]
    P.ins("vector", "memset", ctx.C["zt"], 0.0, sem=ctx.dve)
    proj_fmajor_rope(ctx, xT, NCHUNK, wA, D, list(range(NCHUNK)), QT, CS_cur, SN_cur, False)
    P.wait("gpsimd", ctx.pe)
    load_w(wB, wv_in, D, D)
    P.wait("vector", ctx.gdma)
    P.ins("vector", "memset", V_sb[:, :], 1.0, sem=ctx.dve)
    P.wait("tensor", ctx.gdma)
    for kt in range(NTILE):
        for vh in range(2):
            psv = ctx.psum[1]
            P.wait("tensor", ctx.act)
            P.wait("tensor", ctx.dve)
            for ci in range(NCHUNK):
                P.mm(psv[:, :], xT[:, ci * S + kt * 128: ci * S + kt * 128 + 128],
                     wB[:, ci * D + vh * 512: ci * D + vh * 512 + 512], ci == 0, ci == NCHUNK - 1,
                     sem=ctx.pe)
            P.wait("vector", ctx.pe)
            dst = V_sb[:, kt * 1040 + vh * 8 * 65: kt * 1040 + (vh + 1) * 8 * 65].rearrange(
                "p (s j) -> p s j", j=65)[:, :, 0:64]
            P.ins("vector", "tensor_copy", dst, psv[:, :].rearrange("p (s j) -> p s j", j=64), sem=ctx.dve)
    P.wait("gpsimd", ctx.pe)
    load_w(wA, wk_in, D, D)
    if FUSE["on"] and FUSE.get("cc_early"):
        FUSE["cc_early"]()
        FUSE["cc_early"] = None
    P.wait("tensor", ctx.gdma)
    P.wait("scalar", ctx.sdma)
    for c in range(NCHUNK):
        proj_fmajor_rope(ctx, xT, NCHUNK, wA, D, [c], KTc, CS_cur, SN_cur, False, out_stride=S)
        heads = {2 * c: 2 * c, 2 * c + 1: 2 * c + 1}
        attn_core(ctx, QT, lambda cc: KTc[:, :], V_sb, 16, OT, heads, "taylor", band_full, chunks=[c])
    P.wait("gpsimd", ctx.pe)
    load_w(wB, wo_in, D, D)
    P.wait("tensor", ctx.gdma)
    for tt in range(NTILE):
        for half in range(2):
            ps = ctx.psum[2 + half]
            P.wait("tensor", ctx.dve)
            for ci in range(NCHUNK):
                P.mm(ps[:, :], OT[:, ci * S + tt * 128: ci * S + tt * 128 + 128],
                     wB[:, ci * D + half * 512: ci * D + half * 512 + 512], ci == 0, ci == NCHUNK - 1,
                     sem=ctx.pe)
            P.wait("vector", ctx.pe)
            P.ins("vector", "tensor_copy", atile[:, half * 512:(half + 1) * 512], ps[:, :], sem=ctx.dve)
        P.wait("sync", ctx.dve)
        P.dma("sync", o_s1[tt * 128:(tt + 1) * 128, :], atile[:, :], ctx.sdma)
        P.wait("vector", ctx.sdma)
    P.wait("gpsimd", ctx.sdma)
    P.wait("gpsimd", ctx.adma)
    P.finalize()
    return P

L4_CONSTS = ["TRIU_INC", "TRIU_EXCF", "ONES_R", "ONES_C", "ONES_CF", "IOTA8R"]
NT2 = 16384 // 128
BIS_IT = 50

def build_l1f():
    P = new_prog()
    ctx = Ctx(P, rstd=False)
    lg_in = din(P, "logits_a", [16384, E], F32)
    xn_in = din(P, "xnn_a", [16384, D], BF16)
    f1T_in = din(P, "fc1T", [D, FF], F32)
    f1b_in = din(P, "fc1b", [128, 32], F32)
    f2T_in = din(P, "fc2T", [FF, D], F32)
    f2b_in = din(P, "fc2b", [1, D], F32)
    meta_in = din(P, "meta4", [128, 1], F32)
    o_ye = dout(P, "ye", [S, D], BF16)
    o_thr = dout(P, "thr", [2, E], F32)
    xe_dram = dtmp(P, "xe_dram", [S, 1040], BF16)
    bscr = dtmp(P, "bscr", [1, 1024], F32)
    ctx.loadc(L4_CONSTS)
    lgt = sbuf(P, "lgt", [128, NT2 * E], F32)
    prb = sbuf(P, "prb", [128, NT2 * E], F32)
    mscr = sbuf(P, "mscr", [128, NT2 * E], F32)
    selb = sbuf(P, "selb", [128, NT2], BF16)
    lo = sbuf(P, "lo", [128, E], F32)
    hi = sbuf(P, "hi", [128, E], F32)
    mid = sbuf(P, "mid", [128, E], F32)
    cnt8 = sbuf(P, "cnt8", [128, E], F32)
    meta = sbuf(P, "meta_sb", [128, 1], F32)
    pe_ = sbuf(P, "p_e", [128, NT2], F32)
    re_ = sbuf(P, "r_e", [128, NT2], F32)
    ve_ = sbuf(P, "v_e", [128, NT2], F32)
    didx = sbuf(P, "didx", [128, NT2], I32)
    rowa = sbuf(P, "rowa", [1, 1024], F32)
    t128 = sbuf(P, "t128", [128, 1], F32)
    stg = [sbuf(P, f"stg{i}", [128, 1040], BF16) for i in range(2)]
    zt = sbuf(P, "ztile", [128, 1040], BF16)
    xeT = sbuf(P, "xeT", [128, NCHUNK * 512], BF16)
    w1_sl = [sbuf(P, f"w1_sl{i}", [128, NCHUNK * 512], BF16) for i in range(2)]
    w2_sl = [sbuf(P, f"w2_sl{i}", [128, 4 * D], BF16) for i in range(2)]
    f1b_sb = sbuf(P, "f1b_sb", [128, 32], F32)
    f2bbc = sbuf(P, "f2bbc", [128, D], F32)
    hmT = sbuf(P, "hmT", [128, 4 * 512], BF16)
    ye_sb = sbuf(P, "ye_sb", [128, 4 * D], F32)
    gats = sbuf(P, "gats", [128, 16], F32)
    outb = sbuf(P, "outb", [128, D], BF16)
    L3v = lambda t: t.rearrange("p (tt e) -> p tt e", e=E)

    P.dma("sync", lgt[:, :].rearrange("p (tt e) -> p tt e", e=E),
          lg_in.rearrange("(tt p) e -> p tt e", p=128), ctx.sdma)
    P.dma("sync", meta[:, :], meta_in, ctx.sdma)
    P.dma("gpsimd", f1b_sb[:, :], f1b_in, ctx.gdma)
    bcast_row(ctx, f2b_in, f2bbc, D)
    P.ins("vector", "memset", zt[:, :], 0.0, sem=ctx.dve)
    P.wait("gpsimd", ctx.dve)
    for i in range(16):
        P.dma("gpsimd", xe_dram[i * 128:(i + 1) * 128, :], zt[:, :], ctx.gdma)
    P.wait("vector", ctx.sdma)
    P.ins("vector", "tensor_reduce", mscr[:, 0:NT2], L3v(lgt), axis=AX.X, op=OP.max, sem=ctx.dve)
    P.ins("vector", "tensor_tensor", L3v(prb), L3v(lgt),
          mscr[:, 0:NT2].unsqueeze(2).to_broadcast([128, NT2, E]), OP.subtract, sem=ctx.dve)
    P.wait("scalar", ctx.dve)
    P.ins("scalar", "activation", prb[:, :], prb[:, :], AF.Exp, sem=ctx.act)
    P.wait("vector", ctx.act)
    P.ins("vector", "tensor_reduce", mscr[:, 0:NT2], L3v(prb), axis=AX.X, op=OP.add, sem=ctx.dve)
    P.ins("vector", "reciprocal", mscr[:, NT2:2 * NT2], mscr[:, 0:NT2], sem=ctx.dve)
    P.ins("vector", "tensor_tensor", L3v(prb), L3v(prb),
          mscr[:, NT2:2 * NT2].unsqueeze(2).to_broadcast([128, NT2, E]), OP.mult, sem=ctx.dve)
    P.ins("vector", "memset", lo[:, :], 0.0, sem=ctx.dve)
    P.ins("vector", "memset", hi[:, :], 1.0, sem=ctx.dve)
    for it in range(BIS_IT):
        P.ins("vector", "tensor_add", mid[:, :], lo[:, :], hi[:, :], sem=ctx.dve)
        P.ins("vector", "tensor_scalar", mid[:, :], mid[:, :], 0.5, None, OP.mult, sem=ctx.dve)
        P.ins("vector", "tensor_tensor", L3v(mscr), L3v(prb),
              mid[:, :].unsqueeze(1).to_broadcast([128, NT2, E]), OP.is_ge, sem=ctx.dve)
        P.ins("vector", "tensor_reduce", L3v(mscr)[:, :, 0:E].rearrange("p t e -> p (t e)") if False else mscr[:, :],
              L3v(mscr), axis=AX.X, op=OP.add, sem=ctx.dve) if False else None
        P.wait("tensor", ctx.dve)
        ps = ctx.psum[0]
        P.mm(ps[0:1, 0:NT2 * E], ctx.C["ONES_CF"][:, 0:1], mscr[:, :], True, True, sem=ctx.pe) if False else None
        P.mm(ps[0:1, 0:512], ctx.C["ONES_CF"][:, 0:1], mscr[:, 0:512], True, True, sem=ctx.pe)
        P.mm(ctx.psum[1][0:1, 0:512], ctx.C["ONES_CF"][:, 0:1], mscr[:, 512:1024], True, True, sem=ctx.pe)
        P.wait("vector", ctx.pe)
        P.ins("vector", "tensor_copy", rowa[0:1, 0:512], ps[0:1, 0:512], sem=ctx.dve)
        P.ins("vector", "tensor_copy", rowa[0:1, 512:1024], ctx.psum[1][0:1, 0:512], sem=ctx.dve)
        P.ins("vector", "tensor_reduce", rowa[0:1, 0:E],
              rowa[0:1, :].rearrange("a (tt e) -> a e tt", e=E), axis=AX.X, op=OP.add, sem=ctx.dve)
        P.wait("tensor", ctx.dve)
        P.mm(ctx.psum[2][0:128, 0:E], ctx.C["ONES_R"][0:1, :], rowa[0:1, 0:E], True, True, sem=ctx.pe)
        P.wait("vector", ctx.pe)
        P.ins("vector", "tensor_copy", cnt8[:, :], ctx.psum[2][0:128, 0:E], sem=ctx.dve)
        P.ins("vector", "tensor_scalar", cnt8[:, :], cnt8[:, :], 2048.0, None, OP.is_ge, sem=ctx.dve)
        P.ins("vector", "tensor_sub", mid[:, :], mid[:, :], lo[:, :], sem=ctx.dve)
        P.ins("vector", "tensor_tensor", mscr[:, 0:E], mid[:, :], cnt8[:, :], OP.mult, sem=ctx.dve)
        P.ins("vector", "tensor_add", lo[:, :], lo[:, :], mscr[:, 0:E], sem=ctx.dve)
        P.ins("vector", "tensor_add", mid[:, :], mid[:, :], lo[:, :], sem=ctx.dve) if False else None
        P.ins("vector", "tensor_scalar", cnt8[:, :], cnt8[:, :], -1.0, None, OP.mult, sem=ctx.dve)
        P.ins("vector", "tensor_scalar", cnt8[:, :], cnt8[:, :], 1.0, None, OP.add, sem=ctx.dve)
        P.ins("vector", "tensor_tensor", mscr[:, 0:E], mid[:, :], cnt8[:, :], OP.mult, sem=ctx.dve)
        P.ins("vector", "tensor_tensor", mscr[:, E:2 * E], lo[:, :], cnt8[:, :], OP.mult, sem=ctx.dve)
        P.ins("vector", "tensor_add", mscr[:, 0:E], mscr[:, 0:E], mscr[:, E:2 * E], sem=ctx.dve)
        P.ins("vector", "tensor_scalar", cnt8[:, :], cnt8[:, :], -1.0, None, OP.mult, sem=ctx.dve)
        P.ins("vector", "tensor_scalar", cnt8[:, :], cnt8[:, :], 1.0, None, OP.add, sem=ctx.dve)
        P.ins("vector", "tensor_tensor", hi[:, :], hi[:, :], cnt8[:, :], OP.mult, sem=ctx.dve)
        P.ins("vector", "tensor_add", hi[:, :], hi[:, :], mscr[:, 0:E], sem=ctx.dve) if False else None
        P.ins("vector", "tensor_scalar", cnt8[:, :], cnt8[:, :], -1.0, None, OP.mult, sem=ctx.dve)
        P.ins("vector", "tensor_scalar", cnt8[:, :], cnt8[:, :], 1.0, None, OP.add, sem=ctx.dve)
        P.ins("vector", "tensor_tensor", mscr[:, 0:E], mscr[:, 0:E], cnt8[:, :], OP.mult, sem=ctx.dve)
        P.ins("vector", "tensor_add", hi[:, :], hi[:, :], mscr[:, 0:E], sem=ctx.dve)
    P.wait("sync", ctx.dve)
    P.dma("sync", o_thr[0:1, :], lo[0:1, :], ctx.sdma)
    P.dma("sync", o_thr[1:2, :], hi[0:1, :], ctx.sdma)
    onehot = cnt8
    P.ins("vector", "tensor_scalar", onehot[:, :], ctx.C["IOTA8R"][:, 0:E], meta[:, 0:1], None,
          OP.is_equal, sem=ctx.dve)
    def col_extract(dst, src3):
        P.ins("vector", "tensor_tensor", L3v(mscr), src3,
              onehot[:, :].unsqueeze(1).to_broadcast([128, NT2, E]), OP.mult, sem=ctx.dve)
        P.ins("vector", "tensor_reduce", dst, L3v(mscr), axis=AX.X, op=OP.add, sem=ctx.dve)
    col_extract(pe_[:, :], L3v(prb))
    P.ins("vector", "tensor_tensor", mscr[:, 0:E], lo[:, :], onehot[:, :], OP.mult, sem=ctx.dve)
    P.ins("vector", "tensor_reduce", t128[:, 0:1], mscr[:, 0:E], axis=AX.X, op=OP.add, sem=ctx.dve)
    P.ins("vector", "tensor_scalar", re_[:, 0:1], t128[:, 0:1], 0.0, None, OP.add, sem=ctx.dve)
    P.ins("vector", "tensor_tensor", mscr[:, 0:E], hi[:, :], onehot[:, :], OP.mult, sem=ctx.dve)
    P.ins("vector", "tensor_reduce", t128[:, 0:1], mscr[:, 0:E], axis=AX.X, op=OP.add, sem=ctx.dve)
    P.ins("vector", "tensor_scalar", ve_[:, :], pe_[:, :], t128[:, 0:1], None, OP.is_ge, sem=ctx.dve)
    P.ins("vector", "tensor_scalar", mscr[:, 0:NT2], pe_[:, :], re_[:, 0:1], None, OP.is_ge, sem=ctx.dve)
    P.ins("vector", "tensor_scalar", mscr[:, NT2:2 * NT2], pe_[:, :], t128[:, 0:1], None, OP.is_lt, sem=ctx.dve)
    P.ins("vector", "tensor_tensor", mscr[:, 0:NT2], mscr[:, 0:NT2], mscr[:, NT2:2 * NT2], OP.mult,
          sem=ctx.dve)
    P.ins("vector", "tensor_copy", selb[:, :], ve_[:, :], sem=ctx.dve)
    P.wait("tensor", ctx.dve)
    P.mm(ctx.psum[0][0:1, 0:NT2], ctx.C["ONES_CF"][:, 0:1], ve_[:, :], True, True, sem=ctx.pe)
    P.wait("vector", ctx.pe)
    P.ins("vector", "tensor_reduce", rowa[0:1, 0:1], ctx.psum[0][0:1, 0:NT2], axis=AX.X, op=OP.add,
          sem=ctx.dve)
    P.ins("vector", "tensor_scalar", rowa[0:1, 0:1], rowa[0:1, 0:1], -1.0, None, OP.mult, sem=ctx.dve)
    P.ins("vector", "tensor_scalar", rowa[0:1, 0:1], rowa[0:1, 0:1], 2048.0, None, OP.add, sem=ctx.dve)
    P.wait("tensor", ctx.dve)
    P.mm(ctx.psum[2][0:128, 0:1], ctx.C["ONES_R"][0:1, :], rowa[0:1, 0:1], True, True, sem=ctx.pe)
    P.wait("vector", ctx.pe)
    P.ins("vector", "tensor_copy", t128[:, 0:1], ctx.psum[2][0:128, 0:1], sem=ctx.dve)
    def gcum(src_col_sb, out_sb):
        P.ins("vector", "tensor_copy", selb[:, :], src_col_sb, sem=ctx.dve)
        P.wait("tensor", ctx.dve)
        P.mm(ctx.psum[0][0:1, 0:NT2], ctx.C["ONES_CF"][:, 0:1], src_col_sb, True, True, sem=ctx.pe)
        P.wait("vector", ctx.pe)
        P.ins("vector", "tensor_copy", rowa[0:1, 0:NT2], ctx.psum[0][0:1, 0:NT2], sem=ctx.dve)
        P.wait("sync", ctx.dve)
        P.dma("sync", bscr[0:1, 0:NT2], rowa[0:1, 0:NT2], ctx.sdma)
        P.wait("sync", ctx.sdma)
        P.dma("sync", ctx.stat[:, 4:5], bscr[0:1, 0:NT2].rearrange("a (t b) -> (a t) b", b=1), ctx.sdma)
        P.wait("tensor", ctx.sdma)
        P.mm(ctx.psum[1][0:128, 0:1], ctx.C["TRIU_EXCF"][:, :], ctx.stat[:, 4:5], True, True, sem=ctx.pe)
        P.wait("vector", ctx.pe)
        P.ins("vector", "tensor_copy", ctx.stat[:, 5:6], ctx.psum[1][0:128, 0:1], sem=ctx.dve)
        P.wait("sync", ctx.dve)
        P.dma("sync", bscr[0:1, 0:NT2].rearrange("a (t b) -> (a t) b", b=1), ctx.stat[:, 5:6], ctx.sdma)
        P.wait("sync", ctx.sdma)
        P.dma("sync", rowa[0:1, 0:NT2], bscr[0:1, 0:NT2], ctx.sdma)
        P.wait("tensor", ctx.sdma)
        P.wait("tensor", ctx.dve)
        P.mm(ctx.psum[3][:, 0:NT2], ctx.C["TRIU_INC"][:, :], selb[:, :], True, False, sem=ctx.pe)
        P.mm(ctx.psum[3][:, 0:NT2], ctx.C["ONES_R"][0:1, :], rowa[0:1, 0:NT2], False, True, sem=ctx.pe)
        P.wait("vector", ctx.pe)
        P.ins("vector", "tensor_copy", out_sb, ctx.psum[3][:, 0:NT2], sem=ctx.dve)
    gcum(mscr[:, 0:NT2], mscr[:, NT2:2 * NT2])
    P.ins("vector", "tensor_scalar", mscr[:, NT2:2 * NT2], mscr[:, NT2:2 * NT2], t128[:, 0:1], None,
          OP.is_le, sem=ctx.dve)
    P.ins("vector", "tensor_tensor", mscr[:, 0:NT2], mscr[:, 0:NT2], mscr[:, NT2:2 * NT2], OP.mult,
          sem=ctx.dve)
    P.ins("vector", "tensor_add", ve_[:, :], ve_[:, :], mscr[:, 0:NT2], sem=ctx.dve)
    gcum(ve_[:, :], re_[:, :])
    P.ins("vector", "tensor_scalar", re_[:, :], re_[:, :], 1.0, None, OP.subtract, sem=ctx.dve)
    P.ins("vector", "tensor_scalar", mscr[:, 0:NT2], re_[:, :], float(OOB), None, OP.subtract, sem=ctx.dve)
    P.ins("vector", "tensor_tensor", mscr[:, 0:NT2], mscr[:, 0:NT2], ve_[:, :], OP.mult, sem=ctx.dve)
    P.ins("vector", "tensor_scalar", mscr[:, 0:NT2], mscr[:, 0:NT2], float(OOB), None, OP.add, sem=ctx.dve)
    P.ins("vector", "tensor_copy", didx[:, :], mscr[:, 0:NT2], sem=ctx.dve)
    breg = P.breg(S - 1)
    P.ins("vector", "memset", stg[0][:, 1024:1040], 0.0, sem=ctx.dve)
    P.ins("vector", "memset", stg[1][:, 1024:1040], 0.0, sem=ctx.dve)
    P.wait("gpsimd", ctx.dve)
    g0 = ctx.gsc.n
    for tt in range(NT2):
        par = tt % 2
        if tt >= 2:
            P.wait("sync", ctx.gsc, g0 + 16 * (tt - 1))
        P.dma("sync", stg[par][:, 0:1024], xn_in[tt * 128:(tt + 1) * 128, :], ctx.sdma)
        P.wait("vector", ctx.sdma)
        P.wait("vector", ctx.gsc, g0 + 16 * max(0, tt - 1))
        P.ins("vector", "tensor_copy", stg[par][:, 1024:1025], pe_[:, tt:tt + 1], sem=ctx.dve)
        P.wait("gpsimd", ctx.sdma)
        P.wait("gpsimd", ctx.dve)
        P.idma(xe_dram, IOOA(didx[:, tt:tt + 1], 0), stg[par][:, :], None, ctx.gsc, bounds=breg,
               oob_is_err=False)
    P.wait("scalar", ctx.gsc)
    P.wait("gpsimd", ctx.gsc)
    P.dma("gpsimd", gats[:, :].rearrange("p (rt a) -> p rt a", a=1),
          xe_dram.rearrange("(rt p) j -> p rt j", p=128)[:, :, 1024:1025], ctx.gdma,
          allow_slow_non_contiguous=True)
    for rg in range(4):
        P.wait("scalar", ctx.pe)
        for hc in range(NCHUNK):
            P.dma("scalar", xeT[:, hc * 512:(hc + 1) * 512],
                  xe_dram[rg * 512:(rg + 1) * 512, hc * 128:(hc + 1) * 128], ctx.adma, transpose=True)
        P.wait("vector", ctx.gdma)
        P.wait("vector", ctx.pe)
        for rsub in range(4):
            P.ins("vector", "tensor_copy", ye_sb[:, rsub * D:(rsub + 1) * D], f2bbc[:, :], sem=ctx.dve)
        P.wait("tensor", ctx.adma)
        for fs in range(NCHUNK):
            par = fs % 2
            if fs >= 2 or rg > 0:
                P.wait("gpsimd", ctx.pe)
            P.dma("gpsimd", w1_sl[par][:, :].rearrange("p (hc f) -> p hc f", f=512),
                  f1T_in[:, fs * 512:(fs + 1) * 512].rearrange("(hc p) f -> p hc f", p=128), ctx.gdma)
            P.dma("gpsimd", w2_sl[par][:, :].rearrange("p (fc h) -> p fc h", h=D),
                  f2T_in[fs * 512:(fs + 1) * 512, :].rearrange("(fc p) h -> p fc h", p=128), ctx.gdma)
            P.wait("tensor", ctx.gdma)
            for fc in range(4):
                ps = ctx.psum[fc % 2]
                P.wait("tensor", ctx.act)
                for hcc in range(NCHUNK):
                    P.mm(ps[:, :], w1_sl[par][:, hcc * 512 + fc * 128: hcc * 512 + fc * 128 + 128],
                         xeT[:, hcc * 512:(hcc + 1) * 512], hcc == 0, hcc == NCHUNK - 1, sem=ctx.pe)
                P.wait("scalar", ctx.pe)
                P.ins("scalar", "activation", hmT[:, fc * 512:(fc + 1) * 512], ps[:, :], AF.Gelu,
                      bias=f1b_sb[:, fs * 4 + fc:fs * 4 + fc + 1], sem=ctx.act)
            for rsub in range(4):
                for half in range(2):
                    ps = ctx.psum[4 + half]
                    P.wait("tensor", ctx.act)
                    P.wait("tensor", ctx.dve)
                    for fc in range(4):
                        P.mm(ps[:, :], hmT[:, fc * 512 + rsub * 128: fc * 512 + rsub * 128 + 128],
                             w2_sl[par][:, fc * D + half * 512: fc * D + half * 512 + 512],
                             fc == 0, fc == 3, sem=ctx.pe)
                    P.wait("vector", ctx.pe)
                    sl = ye_sb[:, rsub * D + half * 512: rsub * D + half * 512 + 512]
                    P.ins("vector", "tensor_add", sl, sl, ps[:, :], sem=ctx.dve)
        for rsub in range(4):
            P.ins("vector", "tensor_scalar", outb[:, :], ye_sb[:, rsub * D:(rsub + 1) * D],
                  gats[:, rg * 4 + rsub:rg * 4 + rsub + 1], None, OP.mult, sem=ctx.dve)
            P.wait("sync", ctx.dve)
            P.dma("sync", o_ye[rg * 512 + rsub * 128: rg * 512 + (rsub + 1) * 128, :], outb[:, :], ctx.sdma)
            P.wait("vector", ctx.sdma)
    P.wait("gpsimd", ctx.sdma)
    P.wait("gpsimd", ctx.adma)
    P.finalize()
    return P

L5_CONSTS = ["TRIU_INC", "TRIU_EXCF", "ONES_R", "ONES_CF", "IOTA8R"]

def build_l1c():
    P = new_prog()
    ctx = Ctx(P, rstd=False)
    x1_in = din(P, "x1", [S, D], F32)
    s1_in = din(P, "s1", [S, D], F32)
    ye_in = din(P, "ye_all", [16384, D], BF16)
    thr_in = din(P, "thr", [2, E], F32)
    lg_in = din(P, "logits_a", [16384, E], F32)
    mylg_in = din(P, "my_lg", [S, E], F32)
    mask_in = din(P, "mymask", [128, 1], F32)
    o_out = dout(P, "out", [S, D], F32)
    bscr = dtmp(P, "bscr", [1, 1024], F32)
    ctx.loadc(L5_CONSTS)
    lgt = sbuf(P, "lgt", [128, NT2 * E], F32)
    prb = sbuf(P, "prb", [128, NT2 * E], F32)
    mscr = sbuf(P, "mscr", [128, NT2 * E], F32)
    selb = sbuf(P, "selb", [128, 128], BF16)
    thr2 = sbuf(P, "thr2", [1, E], F32)
    thr2b = sbuf(P, "thr2b", [1, E], F32)
    lo_bc = sbuf(P, "lo_bc", [128, E], F32)
    hi_bc = sbuf(P, "hi_bc", [128, E], F32)
    tge = sbuf(P, "tge", [128, E], F32)
    ttd = sbuf(P, "ttd", [128, E], F32)
    aux = sbuf(P, "aux", [128, 4 * E], F32)
    mymask = sbuf(P, "mymask_sb", [128, 1], F32)
    rowa = sbuf(P, "rowa", [1, 1024], F32)
    mprb = sbuf(P, "mprb", [128, 16 * E], F32)
    mge = sbuf(P, "mge", [128, 16 * E], F32)
    mtd = sbuf(P, "mtd", [128, 16 * E], F32)
    mrk = sbuf(P, "mrk", [128, 16 * E], F32)
    didx = sbuf(P, "didx", [128, 16 * E], I32)
    y1 = sbuf(P, "y1", [128, NTILE * D], BF16)
    gsum = sbuf(P, "gsum", [128, 16], F32)
    atile = sbuf(P, "atile", [128, D], F32)
    btile = sbuf(P, "btile", [128, D], F32)
    L3v = lambda t: t.rearrange("p (tt e) -> p tt e", e=E)
    M3 = lambda t: t.rearrange("p (tt e) -> p tt e", e=E)

    P.dma("sync", lgt[:, :].rearrange("p (tt e) -> p tt e", e=E),
          lg_in.rearrange("(tt p) e -> p tt e", p=128), ctx.sdma)
    P.dma("sync", thr2[0:1, :], thr_in[0:1, :], ctx.sdma)
    P.dma("sync", thr2b[0:1, :], thr_in[1:2, :], ctx.sdma)
    P.dma("sync", mymask[:, :], mask_in, ctx.sdma)
    P.wait("vector", ctx.sdma)
    P.wait("vector", ctx.gdma)

    def softmax_ops(buf, n):
        L = lambda t: t.rearrange("p (tt e) -> p tt e", e=E)
        P.ins("vector", "tensor_reduce", mscr[:, 0:n], L(buf), axis=AX.X, op=OP.max, sem=ctx.dve)
        P.ins("vector", "tensor_tensor", L(buf), L(buf),
              mscr[:, 0:n].unsqueeze(2).to_broadcast([128, n, E]), OP.subtract, sem=ctx.dve)
        P.wait("scalar", ctx.dve)
        P.ins("scalar", "activation", buf[:, 0:n * E], buf[:, 0:n * E], AF.Exp, sem=ctx.act)
        P.wait("vector", ctx.act)
        P.ins("vector", "tensor_reduce", mscr[:, 0:n], L(buf), axis=AX.X, op=OP.add, sem=ctx.dve)
        P.ins("vector", "reciprocal", mscr[:, n:2 * n], mscr[:, 0:n], sem=ctx.dve)
        P.ins("vector", "tensor_tensor", L(buf), L(buf),
              mscr[:, n:2 * n].unsqueeze(2).to_broadcast([128, n, E]), OP.mult, sem=ctx.dve)

    P.ins("vector", "tensor_copy", prb[:, :], lgt[:, :], sem=ctx.dve)
    softmax_ops(prb, NT2)
    P.wait("tensor", ctx.sdma)
    P.wait("tensor", ctx.gdma)
    P.wait("tensor", ctx.dve)
    P.mm(ctx.psum[2][0:128, 0:E], ctx.C["ONES_R"][0:1, :], thr2[0:1, :], True, True, sem=ctx.pe)
    P.mm(ctx.psum[3][0:128, 0:E], ctx.C["ONES_R"][0:1, :], thr2b[0:1, :], True, True, sem=ctx.pe)
    P.wait("vector", ctx.pe)
    P.ins("vector", "tensor_copy", lo_bc[:, :], ctx.psum[2][0:128, 0:E], sem=ctx.dve)
    P.ins("vector", "tensor_copy", hi_bc[:, :], ctx.psum[3][0:128, 0:E], sem=ctx.dve)
    P.ins("vector", "tensor_tensor", L3v(lgt), L3v(prb),
          hi_bc[:, :].unsqueeze(1).to_broadcast([128, NT2, E]), OP.is_ge, sem=ctx.dve)
    P.ins("vector", "tensor_tensor", L3v(mscr), L3v(prb),
          lo_bc[:, :].unsqueeze(1).to_broadcast([128, NT2, E]), OP.is_ge, sem=ctx.dve)
    P.ins("vector", "tensor_sub", prb[:, :], mscr[:, :], lgt[:, :], sem=ctx.dve)
    def tile_counts(buf, dst):
        P.wait("tensor", ctx.dve)
        P.mm(ctx.psum[0][0:1, 0:512], ctx.C["ONES_CF"][:, 0:1], buf[:, 0:512], True, True, sem=ctx.pe)
        P.mm(ctx.psum[1][0:1, 0:512], ctx.C["ONES_CF"][:, 0:1], buf[:, 512:1024], True, True, sem=ctx.pe)
        P.wait("vector", ctx.pe)
        P.ins("vector", "tensor_copy", rowa[0:1, 0:512], ctx.psum[0][0:1, 0:512], sem=ctx.dve)
        P.ins("vector", "tensor_copy", rowa[0:1, 512:1024], ctx.psum[1][0:1, 0:512], sem=ctx.dve)
        P.wait("sync", ctx.dve)
        P.dma("sync", bscr[0:1, :], rowa[0:1, :], ctx.sdma)
        P.wait("sync", ctx.sdma)
        P.dma("sync", dst[:, :], bscr.rearrange("a (t e) -> (a t) e", e=E), ctx.sdma)
        P.wait("vector", ctx.sdma)
        P.wait("tensor", ctx.sdma)
    tile_counts(lgt, tge)
    tile_counts(prb, ttd)
    P.wait("tensor", ctx.dve)
    P.mm(ctx.psum[0][0:1, 0:E], ctx.C["ONES_CF"][:, 0:1], tge[:, :], True, True, sem=ctx.pe)
    P.wait("vector", ctx.pe)
    P.ins("vector", "tensor_scalar", rowa[0:1, 0:E], ctx.psum[0][0:1, 0:E], -1.0, None, OP.mult, sem=ctx.dve)
    P.ins("vector", "tensor_scalar", rowa[0:1, 0:E], rowa[0:1, 0:E], 2048.0, None, OP.add, sem=ctx.dve)
    P.wait("tensor", ctx.dve)
    P.mm(ctx.psum[2][0:128, 0:E], ctx.C["ONES_R"][0:1, :], rowa[0:1, 0:E], True, True, sem=ctx.pe)
    P.wait("vector", ctx.pe)
    P.ins("vector", "tensor_copy", aux[:, 0:E], ctx.psum[2][0:128, 0:E], sem=ctx.dve)
    P.wait("tensor", ctx.dve)
    P.mm(ctx.psum[3][0:128, 0:E], ctx.C["TRIU_EXCF"][:, :], ttd[:, :], True, True, sem=ctx.pe)
    P.wait("vector", ctx.pe)
    P.ins("vector", "tensor_copy", aux[:, E:2 * E], ctx.psum[3][0:128, 0:E], sem=ctx.dve)
    P.ins("vector", "tensor_sub", aux[:, 2 * E:3 * E], aux[:, 0:E], aux[:, E:2 * E], sem=ctx.dve)
    P.ins("vector", "tensor_scalar", aux[:, 2 * E:3 * E], aux[:, 2 * E:3 * E], 0.0, None, OP.max, sem=ctx.dve)
    P.ins("vector", "tensor_tensor", aux[:, 2 * E:3 * E], aux[:, 2 * E:3 * E], ttd[:, :], OP.min, sem=ctx.dve)
    P.ins("vector", "tensor_add", aux[:, 3 * E:4 * E], aux[:, 2 * E:3 * E], tge[:, :], sem=ctx.dve)
    P.wait("tensor", ctx.dve)
    P.mm(ctx.psum[0][0:1, 0:E], mymask[:, 0:1], aux[:, 3 * E:4 * E], True, True, sem=ctx.pe)
    P.mm(ctx.psum[1][0:1, 0:E], mymask[:, 0:1], ttd[:, :], True, True, sem=ctx.pe)
    P.wait("vector", ctx.pe)
    P.ins("vector", "tensor_copy", rowa[0:1, 0:E], ctx.psum[0][0:1, 0:E], sem=ctx.dve)
    P.ins("vector", "tensor_copy", rowa[0:1, E:2 * E], ctx.psum[1][0:1, 0:E], sem=ctx.dve)
    P.wait("tensor", ctx.dve)
    P.mm(ctx.psum[2][0:128, 0:E], ctx.C["ONES_R"][0:1, :], rowa[0:1, 0:E], True, True, sem=ctx.pe)
    P.mm(ctx.psum[3][0:128, 0:E], ctx.C["ONES_R"][0:1, :], rowa[0:1, E:2 * E], True, True, sem=ctx.pe)
    P.wait("vector", ctx.pe)
    P.ins("vector", "tensor_copy", aux[:, E:2 * E], ctx.psum[2][0:128, 0:E], sem=ctx.dve)
    P.ins("vector", "tensor_copy", aux[:, 2 * E:3 * E], ctx.psum[3][0:128, 0:E], sem=ctx.dve)
    P.wait("sync", ctx.pe)
    P.dma("sync", mprb[:, :].rearrange("p (tt e) -> p tt e", e=E),
          mylg_in.rearrange("(tt p) e -> p tt e", p=128), ctx.sdma)
    P.wait("vector", ctx.sdma)
    softmax_ops(mprb, 16)
    P.ins("vector", "tensor_tensor", M3(mge), M3(mprb),
          hi_bc[:, :].unsqueeze(1).to_broadcast([128, 16, E]), OP.is_ge, sem=ctx.dve)
    P.ins("vector", "tensor_tensor", M3(mtd), M3(mprb),
          lo_bc[:, :].unsqueeze(1).to_broadcast([128, 16, E]), OP.is_ge, sem=ctx.dve)
    P.ins("vector", "tensor_sub", mtd[:, :], mtd[:, :], mge[:, :], sem=ctx.dve)
    def gcum16(src, out, off_col):
        P.ins("vector", "tensor_copy", selb[:, :], src, sem=ctx.dve)
        P.wait("tensor", ctx.dve)
        P.mm(ctx.psum[0][0:1, 0:128], ctx.C["ONES_CF"][:, 0:1], src, True, True, sem=ctx.pe)
        P.wait("vector", ctx.pe)
        P.ins("vector", "tensor_copy", rowa[0:1, 0:128], ctx.psum[0][0:1, 0:128], sem=ctx.dve)
        P.wait("sync", ctx.dve)
        P.dma("sync", bscr[0:1, 0:128], rowa[0:1, 0:128], ctx.sdma)
        P.wait("sync", ctx.sdma)
        P.dma("sync", tge[0:16, 0:E], bscr.rearrange("a (t e) -> (a t) e", e=E)[0:16, :], ctx.sdma)
        P.wait("tensor", ctx.sdma)
        P.mm(ctx.psum[1][0:16, 0:E], ctx.C["TRIU_EXCF"][0:16, 0:16], tge[0:16, 0:E], True, True,
             sem=ctx.pe)
        P.wait("vector", ctx.pe)
        P.ins("vector", "tensor_copy", tge[0:16, 0:E], ctx.psum[1][0:16, 0:E], sem=ctx.dve)
        P.wait("sync", ctx.dve)
        P.dma("sync", bscr.rearrange("a (t e) -> (a t) e", e=E)[0:16, :], tge[0:16, 0:E], ctx.sdma)
        P.wait("sync", ctx.sdma)
        P.dma("sync", rowa[0:1, 0:128], bscr[0:1, 0:128], ctx.sdma)
        P.wait("tensor", ctx.sdma)
        P.wait("tensor", ctx.dve)
        P.mm(ctx.psum[2][:, 0:128], ctx.C["TRIU_INC"][:, :], selb[:, :], True, False, sem=ctx.pe)
        P.mm(ctx.psum[2][:, 0:128], ctx.C["ONES_R"][0:1, :], rowa[0:1, 0:128], False, True, sem=ctx.pe)
        P.wait("vector", ctx.pe)
        P.ins("vector", "tensor_tensor", M3(out), ctx.psum[2][:, 0:128].rearrange("p (tt e) -> p tt e", e=E),
              aux[:, off_col:off_col + E].unsqueeze(1).to_broadcast([128, 16, E]), OP.add, sem=ctx.dve)
    gcum16(mtd[:, :], mrk, 2 * E)
    P.ins("vector", "tensor_tensor", M3(mrk), M3(mrk),
          aux[:, 0:E].unsqueeze(1).to_broadcast([128, 16, E]), OP.is_le, sem=ctx.dve)
    P.ins("vector", "tensor_tensor", mtd[:, :], mtd[:, :], mrk[:, :], OP.mult, sem=ctx.dve)
    P.ins("vector", "tensor_add", mge[:, :], mge[:, :], mtd[:, :], sem=ctx.dve)
    gcum16(mge[:, :], mrk, E)
    P.ins("vector", "tensor_scalar", mrk[:, :], mrk[:, :], 1.0, None, OP.subtract, sem=ctx.dve)
    P.ins("vector", "tensor_scalar", M3(mscr[:, 0:16 * E]),
          ctx.C["IOTA8R"][:, 0:E].unsqueeze(1).to_broadcast([128, 16, E]), 2048.0, None, OP.mult,
          sem=ctx.dve)
    P.ins("vector", "tensor_add", mrk[:, :], mrk[:, :], mscr[:, 0:16 * E], sem=ctx.dve)
    P.ins("vector", "tensor_scalar", mrk[:, :], mrk[:, :], float(OOB), None, OP.subtract, sem=ctx.dve)
    P.ins("vector", "tensor_tensor", mrk[:, :], mrk[:, :], mge[:, :], OP.mult, sem=ctx.dve)
    P.ins("vector", "tensor_scalar", mrk[:, :], mrk[:, :], float(OOB), None, OP.add, sem=ctx.dve)
    P.ins("vector", "tensor_copy", didx[:, :], mrk[:, :], sem=ctx.dve)
    P.ins("vector", "memset", y1[:, :], 0.0, sem=ctx.dve)
    gb = [sbuf(P, f"gbuf{i}", [128, D], BF16) for i in range(2)]
    breg = P.breg(16384 - 1)
    P.wait("gpsimd", ctx.dve)
    g0 = ctx.gsc.n
    for j, (e, mt) in enumerate([(e, mt) for e in range(E) for mt in range(NTILE)]):
        par = j % 2
        if j >= 2:
            P.wait("vector", ctx.gsc, g0 + 16 * (j - 1))
        P.ins("vector", "memset", gb[par][:, :], 0.0, sem=ctx.dve)
        P.wait("gpsimd", ctx.dve)
        P.idma(gb[par][:, :], None, ye_in, IOOA(didx[:, mt * E + e:mt * E + e + 1], 0),
               ctx.gsc, bounds=breg, oob_is_err=False)
        P.wait("vector", ctx.gsc)
        sl = y1[:, mt * D:(mt + 1) * D]
        P.ins("vector", "tensor_add", sl, sl, gb[par][:, :], sem=ctx.dve)
    P.ins("vector", "tensor_tensor", mscr[:, 0:16 * E], mge[:, :], mprb[:, :], OP.mult, sem=ctx.dve)
    P.ins("vector", "tensor_reduce", gsum[:, :], M3(mscr[:, 0:16 * E]), axis=AX.X, op=OP.add, sem=ctx.dve)
    P.ins("vector", "tensor_scalar", gsum[:, :], gsum[:, :], 1e-9, None, OP.max, sem=ctx.dve)
    P.wait("scalar", ctx.dve)
    P.ins("scalar", "activation", gsum[:, :], gsum[:, :], AF.Ln, sem=ctx.act)
    P.ins("scalar", "activation", gsum[:, :], gsum[:, :], AF.Exp, scale=-1.0, sem=ctx.act)
    P.wait("vector", ctx.act)
    P.wait("vector", ctx.gsc)
    for mt in range(NTILE):
        P.wait("sync", ctx.dve)
        P.dma("sync", atile[:, :], x1_in[mt * 128:(mt + 1) * 128, :], ctx.sdma)
        P.dma("sync", btile[:, :], s1_in[mt * 128:(mt + 1) * 128, :], ctx.sdma)
        P.wait("vector", ctx.sdma)
        P.ins("vector", "tensor_scalar", ctx.scr1[:, :], y1[:, mt * D:(mt + 1) * D],
              gsum[:, mt:mt + 1], None, OP.mult, sem=ctx.dve)
        P.wait("scalar", ctx.dve)
        P.ins("scalar", "activation", ctx.scr1[:, :], ctx.scr1[:, :], AF.Silu, sem=ctx.act)
        P.wait("vector", ctx.act)
        P.ins("vector", "tensor_mul", btile[:, :], btile[:, :], ctx.scr1[:, :], sem=ctx.dve)
        P.ins("vector", "tensor_add", atile[:, :], atile[:, :], btile[:, :], sem=ctx.dve)
        P.wait("sync", ctx.dve)
        P.dma("sync", o_out[mt * 128:(mt + 1) * 128, :], atile[:, :], ctx.sdma)
        P.wait("vector", ctx.sdma)
    P.wait("gpsimd", ctx.sdma)
    P.finalize()
    return P

FUSED_CONSTS = ["SWAP", "ONES_R", "ONES_C", "HMAP", "BMAP", "M_CAUS", "M_WIN", "E64BC", "IDN",
                "TRIU_INC", "TRIU_EXCF", "ONES_CF", "IOTA8R", "CMAP", "SMAP", "MELB"]

def stage_barrier(P, ctx):
    for eng in Prog.ENGINES:
        for s in (ctx.sdma, ctx.adma, ctx.gdma, ctx.pe, ctx.dve, ctx.act, ctx.gsc, ctx.ccs):
            P.wait(eng, s)

def _stage(tag):
    FUSE["stage"] = tag
    FUSE["stack"] = contextlib.ExitStack()

def _stage_end():
    FUSE["stack"].close()
    FUSE["stack"] = None
    FUSE["ctx"].C.clear()

def build_fused():
    FUSE["on"] = True
    FUSE["P"] = None
    FUSE["ctx"] = None
    FUSE["stage"] = ""
    FUSE["stack"] = None
    P = new_prog()
    ctx = Ctx(P, rstd=False)
    nc = P.nc
    OPB = OP.bypass
    G4 = [[0, 1, 2, 3], [4, 5, 6, 7]]
    G8 = [list(range(8))]
    xn_g = nc.dram_tensor("xn_g", [TG, D], BF16).ap()
    lg_g = nc.dram_tensor("logits_g", [TG, E], F32).ap()
    oa_g = nc.dram_tensor("out_all_g", [4 * 1280, D], BF16).ap()
    xnn_a = nc.dram_tensor("xnn_a", [8 * S, D], BF16).ap()
    lg_a = nc.dram_tensor("logits_a", [8 * S, E], F32).ap()
    ye_a = nc.dram_tensor("ye_all", [8 * S, D], BF16).ap()
    P.iomap.update({"xn_g": xn_g, "logits_g": lg_g, "out_all_g": oa_g,
                    "xnn_a": xnn_a, "logits_a": lg_a, "ye_all": ye_a})
    _stage("s1_")
    ctx.bc = sbuf(P, "bcscr", [128, 512], BF16)
    ctx.sq = sbuf(P, "sqscr", [128, 512], BF16)
    ctx.rstd = sbuf(P, "rstdscr", [16, 2048], BF16)
    def _early1():
        P.wait("gpsimd", ctx.sdma)
        P.cc("AllGather", OPB, G4, [P.iomap["xn0"].opt()], [xn_g.opt()], ctx.ccs)
        P.cc("AllGather", OPB, G4, [P.iomap["logits0"].opt()], [lg_g.opt()], ctx.ccs)
    FUSE["cc_early"] = _early1
    build_l0a()
    _stage_end()
    ctx.bc = ctx.sq = ctx.rstd = None
    stage_barrier(P, ctx)
    for eng in Prog.ENGINES:
        P.wait(eng, ctx.ccs)
    _stage("s2_")
    build_l0f_full()
    _stage_end()
    stage_barrier(P, ctx)
    P.cc("AllGather", OPB, G4, [P.iomap["out_all"].opt()], [oa_g.opt()], ctx.ccs)
    for eng in Prog.ENGINES:
        P.wait(eng, ctx.ccs)
    _stage("s3_")
    def _early3():
        P.wait("gpsimd", ctx.sdma)
        P.cc("AllGather", OPB, G8, [P.iomap["xnn"].opt()], [xnn_a.opt()], ctx.ccs)
        P.cc("AllGather", OPB, G8, [P.iomap["logits1"].opt()], [lg_a.opt()], ctx.ccs)
    FUSE["cc_early"] = _early3
    build_l1a()
    _stage_end()
    stage_barrier(P, ctx)
    for eng in Prog.ENGINES:
        P.wait(eng, ctx.ccs)
    P.iomap["my_lg"] = P.iomap["logits1"]
    _stage("s4_")
    build_l1f()
    _stage_end()
    stage_barrier(P, ctx)
    P.cc("AllGather", OPB, G8, [P.iomap["ye"].opt()], [ye_a.opt()], ctx.ccs)
    for eng in Prog.ENGINES:
        P.wait(eng, ctx.ccs)
    _stage("s5_")
    build_l1c()
    _stage_end()
    stage_barrier(P, ctx)
    if FUSE.get("dbg"):
        for nm in ("xn0", "logits0", "s0", "out_all", "gidx", "gates", "gidxp",
                   "x1", "xnn", "logits1", "s1", "ye", "thr",
                   "xn_g", "logits_g", "out_all_g", "xnn_a", "logits_a", "ye_all"):
            src = P.iomap[nm]
            dst = nc.dram_tensor("dbg_" + nm, list(src.tensor.shape), src.tensor.dtype,
                                 kind="ExternalOutput").ap()
            P.dma("sync", dst, src, ctx.sdma)
        P.wait("gpsimd", ctx.sdma)
    P.finalize(force=True)
    FUSE["on"] = False
    FUSE["P"] = None
    FUSE["ctx"] = None
    return P

def kernel(**inputs):
    inp = {k: np.asarray(v) for k, v in inputs.items()}
    B = 8
    x = np.ascontiguousarray(inp["L0_attn_hidden"], dtype=np.float32).reshape(B, S, D)
    d1 = prep_l0a_weights(inp)
    wi = np.asarray(inp["L0_ffn_wi"], np.float32)
    wo = np.asarray(inp["L0_ffn_wo"], np.float32)
    PC = perm_cols()
    w_kv = np.asarray(inp["L1_attn_w_kv"], np.float32)
    d3 = dict(ln_w=np.asarray(inp["L1_attn_ln_w"], np.float32).reshape(1, D),
              ln_b=np.asarray(inp["L1_attn_ln_b"], np.float32).reshape(1, D),
              nw=np.asarray(inp["L1_ffn_norm_weight"], np.float32).reshape(1, D),
              w_q=np.ascontiguousarray(np.asarray(inp["L1_attn_w_q"], np.float32)[:, PC]),
              w_k=np.ascontiguousarray(w_kv[:, :D][:, PC]),
              w_v=np.ascontiguousarray(w_kv[:, D:]),
              w_o=np.asarray(inp["L1_attn_w_o"], np.float32),
              rwT=np.ascontiguousarray(np.asarray(inp["L1_ffn_router_w"], np.float32).T),
              rb=np.asarray(inp["L1_ffn_router_b"], np.float32).reshape(1, E),
              theta=np.asarray(inp["L1_attn_theta"], np.float32).reshape(1, 1))
    fc1 = np.asarray(inp["L1_ffn_fc1_w"], np.float32)
    fc1b = np.asarray(inp["L1_ffn_fc1_b"], np.float32)
    fc2 = np.asarray(inp["L1_ffn_fc2_w"], np.float32)
    fc2b = np.asarray(inp["L1_ffn_fc2_b"], np.float32)
    dc = {"c_" + nm: CONSTS[nm] for nm in FUSED_CONSTS}
    dc["c_T32"] = CONSTS["T32"]
    Pf = build_fused()
    maps = []
    for c in range(B):
        e0 = 2 * (c % 4)
        mask = np.zeros((128, 1), np.float32)
        mask[:c * 16] = 1.0
        rowsel = ((c % 4) * S + np.arange(NTILE)[None, :] * 128
                  + np.arange(128)[:, None]).astype(np.int32)
        m = dict(d1)
        m.update(d3)
        m.update(dc)
        m.update(x=np.ascontiguousarray(x[c]),
                 wi0=np.ascontiguousarray(wi[e0]), wi1=np.ascontiguousarray(wi[e0 + 1]),
                 wo0=np.ascontiguousarray(wo[e0]), wo1=np.ascontiguousarray(wo[e0 + 1]),
                 meta=np.tile(np.array([[e0, e0 + 1]], np.float32), (128, 1)),
                 fc1T=np.ascontiguousarray(fc1[c].T),
                 fc1b=np.ascontiguousarray(fc1b[c].reshape(32, 128).T),
                 fc2T=np.ascontiguousarray(fc2[c].T),
                 fc2b=np.ascontiguousarray(fc2b[c].reshape(1, D)),
                 meta4=np.full((128, 1), float(c), np.float32),
                 mymask=mask, rowsel=np.ascontiguousarray(rowsel))
        maps.append(m)
    r = run_bass_kernel_spmd(Pf.nc, maps, core_ids=list(range(B))).results
    globals()["_last_results"] = r
    out = np.stack([np.asarray(r[c]["out"], np.float32) for c in range(B)], 0)
    return out.reshape(B, S, D)

def kernel_unfused(**inputs):
    inp = {k: np.asarray(v) for k, v in inputs.items()}
    B = 8
    x = np.ascontiguousarray(inp["L0_attn_hidden"], dtype=np.float32).reshape(B, S, D)
    d1 = prep_l0a_weights(inp)
    P1 = build_l0a()
    maps1 = [dict(d1, x=np.ascontiguousarray(x[c])) for c in range(B)]
    r1 = run_bass_kernel_spmd(P1.nc, maps1, core_ids=list(range(B))).results
    xn0 = [r1[c]["xn0"] for c in range(B)]
    lg0 = [np.asarray(r1[c]["logits0"], np.float32) for c in range(B)]
    s0 = [np.asarray(r1[c]["s0"], np.float32) for c in range(B)]
    P2 = build_l0f_full()
    wi = np.asarray(inp["L0_ffn_wi"], np.float32)
    wo = np.asarray(inp["L0_ffn_wo"], np.float32)
    c2 = {"c_" + nm: CONSTS[nm] for nm in L2_CONSTS}
    maps2 = []
    for c in range(B):
        g, e0 = c // 4, 2 * (c % 4)
        maps2.append(dict(c2, logits_g=np.concatenate(lg0[g * 4:(g + 1) * 4], 0),
                          xn_g=np.concatenate(xn0[g * 4:(g + 1) * 4], 0),
                          wi0=np.ascontiguousarray(wi[e0]), wi1=np.ascontiguousarray(wi[e0 + 1]),
                          wo0=np.ascontiguousarray(wo[e0]), wo1=np.ascontiguousarray(wo[e0 + 1]),
                          meta=np.tile(np.array([[e0, e0 + 1]], np.float32), (128, 1))))
    r2 = run_bass_kernel_spmd(P2.nc, maps2, core_ids=list(range(B))).results
    out_all_g = [np.concatenate([r2[g * 4 + j]["out_all"] for j in range(4)], 0) for g in range(2)]
    P3 = build_l1a()
    PC = perm_cols()
    w_kv = np.asarray(inp["L1_attn_w_kv"], np.float32)
    c3 = {"c_" + nm: CONSTS[nm] for nm in L3_CONSTS + ["T32"]}
    d3 = dict(c3,
              ln_w=np.asarray(inp["L1_attn_ln_w"], np.float32).reshape(1, D),
              ln_b=np.asarray(inp["L1_attn_ln_b"], np.float32).reshape(1, D),
              nw=np.asarray(inp["L1_ffn_norm_weight"], np.float32).reshape(1, D),
              w_q=np.ascontiguousarray(np.asarray(inp["L1_attn_w_q"], np.float32)[:, PC]),
              w_k=np.ascontiguousarray(w_kv[:, :D][:, PC]),
              w_v=np.ascontiguousarray(w_kv[:, D:]),
              w_o=np.asarray(inp["L1_attn_w_o"], np.float32),
              rwT=np.ascontiguousarray(np.asarray(inp["L1_ffn_router_w"], np.float32).T),
              rb=np.asarray(inp["L1_ffn_router_b"], np.float32).reshape(1, E),
              theta=np.asarray(inp["L1_attn_theta"], np.float32).reshape(1, 1))
    maps3 = []
    for c in range(B):
        g = c // 4
        gi = np.asarray(r2[c]["gidx"])[(c % 4) * S:(c % 4 + 1) * S]
        ga = np.asarray(r2[c]["gates"], np.float32)[(c % 4) * S:(c % 4 + 1) * S]
        maps3.append(dict(d3, x=np.ascontiguousarray(x[c]), s0=s0[c], out_all_g=out_all_g[g],
                          gidx=np.ascontiguousarray(gi), gates=np.ascontiguousarray(ga)))
    r3 = run_bass_kernel_spmd(P3.nc, maps3, core_ids=list(range(B))).results
    x1 = [np.asarray(r3[c]["x1"], np.float32) for c in range(B)]
    xnn = [r3[c]["xnn"] for c in range(B)]
    lg1 = [np.asarray(r3[c]["logits1"], np.float32) for c in range(B)]
    s1 = [np.asarray(r3[c]["s1"], np.float32) for c in range(B)]
    P4 = build_l1f()
    fc1 = np.asarray(inp["L1_ffn_fc1_w"], np.float32)
    fc1b = np.asarray(inp["L1_ffn_fc1_b"], np.float32)
    fc2 = np.asarray(inp["L1_ffn_fc2_w"], np.float32)
    fc2b = np.asarray(inp["L1_ffn_fc2_b"], np.float32)
    xnn_all = np.concatenate(xnn, 0)
    lg1_all = np.concatenate(lg1, 0)
    c4 = {"c_" + nm: CONSTS[nm] for nm in L4_CONSTS}
    maps4 = []
    for c in range(B):
        maps4.append(dict(c4, logits_a=lg1_all, xnn_a=xnn_all,
                          fc1T=np.ascontiguousarray(fc1[c].T),
                          fc1b=np.ascontiguousarray(fc1b[c].reshape(32, 128).T),
                          fc2T=np.ascontiguousarray(fc2[c].T),
                          fc2b=np.ascontiguousarray(fc2b[c].reshape(1, D)),
                          meta4=np.full((128, 1), float(c), np.float32)))
    r4 = run_bass_kernel_spmd(P4.nc, maps4, core_ids=list(range(B))).results
    ye_all = np.concatenate([r4[c]["ye"] for c in range(B)], 0)
    thr = np.asarray(r4[0]["thr"], np.float32)
    P5 = build_l1c()
    c5 = {"c_" + nm: CONSTS[nm] for nm in L5_CONSTS}
    maps5 = []
    for c in range(B):
        mask = np.zeros((128, 1), np.float32)
        mask[:c * 16] = 1.0
        maps5.append(dict(c5, x1=x1[c], s1=s1[c], ye_all=ye_all, thr=thr, logits_a=lg1_all,
                          my_lg=np.ascontiguousarray(lg1_all[c * S:(c + 1) * S]), mymask=mask))
    r5 = run_bass_kernel_spmd(P5.nc, maps5, core_ids=list(range(B))).results
    out = np.stack([np.asarray(r5[c]["out"], np.float32) for c in range(B)], 0)
    return out.reshape(B, S, D)


# version 15 (final)
# speedup vs baseline: 1.1595x; 1.1595x over previous
import contextlib
import numpy as np
import concourse.bass as bass
import concourse.mybir as mybir
from concourse.bass_utils import run_bass_kernel_spmd

F32 = mybir.dt.float32
BF16 = mybir.dt.bfloat16
I32 = mybir.dt.int32
U32 = mybir.dt.uint32
AF = mybir.ActivationFunctionType
OP = mybir.AluOpType
AX = mybir.AxisListType
IOOA = bass.IndirectOffsetOnAxis

D, S, H, HD, KVH, LAT, WIN, E, FF = 1024, 2048, 16, 64, 4, 256, 256, 8, 4096
NCHUNK = D // 128
NTILE = S // 128
NQT = S // 512
EPS0 = 1e-6
EPS1 = 1e-5
OOB = 1 << 20

def _perm64():
    return np.concatenate([np.arange(0, 64, 2), np.arange(1, 64, 2)])

def perm_cols():
    p = np.arange(D).reshape(H, HD)
    p = p[:, _perm64()]
    return p.reshape(-1)

def perm_cols_kv():
    p = np.arange(KVH * HD).reshape(KVH, HD)
    p = p[:, _perm64()]
    return p.reshape(-1)

def l0_ropetab(base):
    inv = 1.0 / base ** (np.arange(0, HD, 2, dtype=np.float32) / HD)
    ang = np.arange(S, dtype=np.float32)[:, None] * inv[None, :]
    c, s = np.cos(ang).T.astype(np.float32), np.sin(ang).T.astype(np.float32)
    CS = np.concatenate([c, c, c, c], 0)
    SN = np.concatenate([-s, s, -s, s], 0)
    return CS, SN

def host_consts():
    C = {}
    C["IDN"] = np.eye(128, dtype=np.float32)
    sw = np.zeros((128, 128), np.float32)
    for b in range(2):
        for i in range(32):
            sw[b * 64 + i, b * 64 + 32 + i] = 1.0
            sw[b * 64 + 32 + i, b * 64 + i] = 1.0
    C["SWAP"] = sw
    C["TRIU_INC"] = np.triu(np.ones((128, 128), np.float32))
    C["TRIU_EXC"] = np.triu(np.ones((128, 128), np.float32), 1)
    C["TRIU_EXCF"] = C["TRIU_EXC"]
    C["ONES_CF"] = np.ones((128, 1), np.float32)
    C["ONES_R"] = np.ones((1, 128), np.float32)
    C["ONES_C"] = np.ones((128, 1), np.float32)
    hm = np.zeros((NCHUNK, 128, 16), np.float32)
    bm = np.zeros((NCHUNK, 16, 128), np.float32)
    for c in range(NCHUNK):
        for p in range(128):
            hm[c, p, 2 * c + p // 64] = 1.0
            bm[c, 2 * c + p // 64, p] = 1.0
    C["HMAP"] = hm.transpose(1, 0, 2).reshape(128, NCHUNK * 16)
    C["BMAP"] = bm.transpose(1, 0, 2).reshape(16, NCHUNK * 128)
    cg, sg = l0_ropetab(1e6)
    cl, sl = l0_ropetab(1e4)
    C["CSG"], C["SNG"], C["CSL"], C["SNL"] = cg, sg, cl, sl
    C["M_CAUS"] = np.triu(np.ones((128, 128), np.float32))
    C["M_WIN"] = np.tril(np.ones((128, 128), np.float32), -1)
    C["T32"] = np.tile(np.arange(S, dtype=np.float32)[None, :], (32, 1))
    hd2 = HD // 2
    top = 2595.0 * np.log10(np.float32(1.0 + 4000.0 / 200.0))
    mel = np.power(10.0, np.linspace(0.0, top, hd2, dtype=np.float32) / 2595.0) - 1.0
    C["MELB"] = ((200.0 * mel / 1000.0) / 220.0).astype(np.float32).reshape(32, 1)
    pm = np.zeros((32, 128), np.float32)
    sm = np.zeros((32, 128), np.float32)
    for f in range(32):
        for b in range(4):
            pm[f, b * 32 + f] = 1.0
            sm[f, b * 32 + f] = -1.0 if (b % 2 == 0) else 1.0
    C["CMAP"], C["SMAP"] = pm, sm
    C["IOTA512"] = np.arange(512, dtype=np.float32)[None, :]
    C["IOTA8R"] = np.tile(np.arange(8, dtype=np.float32)[None, :], (128, 1))
    C["IOTA128C"] = np.arange(128, dtype=np.float32)[:, None]
    e64 = np.zeros((65, 64), np.float32); e64[64, :] = 1.0
    C["E64BC"] = e64
    return C

CONSTS = host_consts()

class Sem:
    def __init__(self, h, name):
        self.h, self.name, self.n = h, name, 0

class Prog:
    ENGINES = ("sync", "scalar", "vector", "tensor", "gpsimd")

    def __init__(self, nc, st):
        self.nc = nc
        self.st = st
        self.items = {e: [] for e in self.ENGINES}
        self.sems = {}
        self.waited = {e: {} for e in self.ENGINES}
        self._breg = {}
        self.selfsem = {}
        self.iomap = {}

    def sem(self, name):
        h = self.st.enter_context(self.nc.semaphore(name))
        s = Sem(h, name)
        self.sems[name] = s
        return s

    def emit(self, engine, fn):
        self.items[engine].append(fn)

    def wait(self, engine, sem, val=None):
        v = sem.n if val is None else val
        if v <= 0:
            return
        if self.waited[engine].get(sem.name, -1) >= v:
            return
        self.waited[engine][sem.name] = v
        self.emit(engine, lambda e, s=sem, v=v: e.wait_ge(s.h, v))

    def op(self, engine, f, sem=None):
        if sem is not None:
            sem.n += 1
            v = sem.n
            self.emit(engine, lambda e, f=f, s=sem, v=v: f(e).then_inc(s.h, 1))
            self.waited[engine][sem.name] = v
        else:
            self.emit(engine, f)

    def dma(self, queue, out, in_, sem, **kw):
        sem.n += 16
        self.emit(queue, lambda e, o=out, i=in_, s=sem, kw=kw: e.dma_start(out=o, in_=i, **kw).then_inc(s.h, 16))

    def idma(self, out, out_off, in_, in_off, sem, bounds=None, **kw):
        sem.n += 16
        def fn(e, kw=dict(kw), s=sem, bounds=bounds):
            if bounds is not None:
                kw["bounds_check"] = bounds["r"]
            e.indirect_dma_start(out=out, out_offset=out_off, in_=in_, in_offset=in_off, **kw).then_inc(s.h, 16)
        self.emit("gpsimd", fn)

    def cc(self, kind, op, groups, ins, outs, sem):
        sem.n += 1
        self.emit("gpsimd", lambda e, s=sem: e.collective_compute(
            kind, op, replica_groups=groups, ins=ins, outs=outs).then_inc(s.h, 1))

    def breg(self, val):
        if val not in self._breg:
            holder = {}
            def mk(e, val=val, holder=holder):
                r = e.alloc_register(f"bnd{val}")
                e.reg_mov(r, val)
                holder["r"] = r
                return None
            self.emit("gpsimd", lambda e, mk=mk: mk(e))
            self._breg[val] = holder
        return self._breg[val]

    def finalize(self, force=False):
        if FUSE["on"] and not force:
            return
        with self.nc.Block() as block:
            for eng in self.ENGINES:
                items = self.items[eng]
                if not items:
                    continue
                def make(items):
                    def body(e):
                        for fn in items:
                            fn(e)
                    return body
                getattr(block, eng)(make(items))

FUSE = {"on": False, "P": None, "ctx": None, "stage": "", "stack": None}

def new_prog():
    if FUSE["on"] and FUSE["P"] is not None:
        return FUSE["P"]
    st = contextlib.ExitStack()
    nc = bass.Bass("TRN2", target_bir_lowering=False)
    P = Prog(nc, st)
    P.iomap = {}
    if FUSE["on"]:
        FUSE["P"] = P
    return P

def sbuf(P, name, shape, dt):
    if FUSE["on"] and FUSE["stack"] is not None:
        return FUSE["stack"].enter_context(P.nc.sbuf_tensor(FUSE["stage"] + name, list(shape), dt))
    return P.st.enter_context(P.nc.sbuf_tensor(name, list(shape), dt))

def psum(P, name, shape, dt=F32):
    return P.st.enter_context(P.nc.psum_tensor(name, list(shape), dt))

def din(P, name, shape, dt=F32):
    if FUSE["on"]:
        t = P.iomap.get(name)
        if t is not None:
            return t
        ap = P.nc.dram_tensor(name, list(shape), dt, kind="ExternalInput").ap()
        P.iomap[name] = ap
        return ap
    return P.nc.dram_tensor(name, list(shape), dt, kind="ExternalInput").ap()

def dout(P, name, shape, dt=F32):
    if FUSE["on"] and name != "out":
        ap = P.nc.dram_tensor(name, list(shape), dt).ap()
        P.iomap[name] = ap
        return ap
    return P.nc.dram_tensor(name, list(shape), dt, kind="ExternalOutput").ap()

def dtmp(P, name, shape, dt=F32):
    if FUSE["on"]:
        name = FUSE["stage"] + name
    return P.nc.dram_tensor(name, list(shape), dt).ap()

def load_consts(P, names, sem):
    tiles = {}
    for nm in names:
        arr = CONSTS[nm]
        ap = din(P, "c_" + nm, arr.shape, F32)
        bf = nm not in ("IDN",)
        t = sbuf(P, "ct_" + nm, arr.shape, BF16 if bf else F32)
        P.dma("gpsimd", t[:, :], ap, sem)
        tiles[nm] = t
    return tiles

def const_inputs(names):
    return {"c_" + nm: CONSTS[nm] for nm in names}

def _ins(P, engine, method, *args, sem=None, inc=1, **kw):
    ss = P.selfsem.get(engine)
    if ss is not None and ss.n > 0:
        P.wait(engine, ss, ss.n)
    if sem is not None:
        sem.n += inc
        def fn(e, method=method, args=args, kw=kw, s=sem, inc=inc):
            getattr(e, method)(*args, **kw).then_inc(s.h, inc)
        if P.selfsem.get(engine) is not sem:
            P.waited[engine][sem.name] = sem.n
    else:
        def fn(e, method=method, args=args, kw=kw):
            getattr(e, method)(*args, **kw)
    P.emit(engine, fn)

Prog.ins = _ins

def _mm(P, out, lhsT, rhs, start, stop, sem=None, tile_position=None):
    kw = {"start": start, "stop": stop}
    if tile_position is not None:
        kw["tile_position"] = tile_position
    P.ins("tensor", "matmul", out, lhsT, rhs, sem=sem, **kw)

Prog.mm = _mm

class Ctx:
    def __new__(cls, P, rstd=True):
        if FUSE["on"] and FUSE["ctx"] is not None:
            return FUSE["ctx"]
        obj = super().__new__(cls)
        if FUSE["on"]:
            FUSE["ctx"] = obj
        return obj

    def __init__(self, P, rstd=True):
        if getattr(self, "_inited", False):
            return
        self._inited = True
        self.P = P
        self.sdma = P.sem("sdma")
        self.adma = P.sem("adma")
        self.gdma = P.sem("gdma")
        self.pe = P.sem("pe")
        self.dve = P.sem("dve")
        self.act = P.sem("act")
        self.ccs = P.sem("ccs")
        self.gsc = P.sem("gsc")
        P.selfsem = {"vector": self.dve, "scalar": self.act}
        self.psum = [psum(P, f"psb{i}", [128, 512], F32) for i in range(8)]
        self.r1 = sbuf(P, "r1scr", [128, 512], BF16)
        self.r2 = sbuf(P, "r2scr", [128, 512], BF16)
        self.sw = sbuf(P, "swscr", [128, 512], BF16)
        self.bc = sbuf(P, "bcscr", [128, 512], BF16) if rstd else None
        self.sq = sbuf(P, "sqscr", [128, 512], BF16) if rstd else None
        self.scr1 = sbuf(P, "scr1", [128, 1024], F32)
        self.stat = sbuf(P, "stat", [128, 8], F32)
        self.rstd = sbuf(P, "rstd", [16, 2048], BF16) if rstd else None
        self.rowb = sbuf(P, "rowb", [1, 512], F32)
        self.epsb = sbuf(P, "epsb", [128, 8], F32)
        self._biases = {}
        self.C = {}

    def bias(self, val):
        if val not in self._biases:
            col = len(self._biases)
            self._biases[val] = col
            self.P.ins("vector", "memset", self.epsb[:, col:col + 1], float(val), sem=self.dve)
            self.P.wait("scalar", self.dve)
        col = self._biases[val]
        return self.epsb[:, col:col + 1]

    def loadc(self, names):
        P = self.P
        for nm in names:
            if nm in self.C:
                continue
            arr = CONSTS[nm]
            ap = din(P, "c_" + nm, arr.shape, F32)
            dt = F32 if nm in ("IDN", "ONES_R", "E64BC", "TRIU_EXCF", "ONES_CF", "T32", "MELB", "CMAP", "SMAP") else BF16
            t = sbuf(P, "ct_" + nm, arr.shape, dt)
            P.dma("gpsimd", t[:, :], ap, self.gdma)
            self.C[nm] = t
        for eng in ("tensor", "vector", "scalar"):
            P.wait(eng, self.gdma)

def bcast_row(ctx, row_ap, out_sb, n):
    P = ctx.P
    ps = ctx.psum[7]
    for j in range(0, n, 512):
        w = min(512, n - j)
        P.wait("sync", ctx.pe)
        P.dma("sync", ctx.rowb[0:1, 0:w], row_ap[0:1, j:j + w], ctx.sdma)
        P.wait("tensor", ctx.sdma)
        P.wait("tensor", ctx.gdma)
        P.wait("tensor", ctx.dve)
        P.mm(ps[:, 0:w], ctx.C["ONES_R"][0:1, :], ctx.rowb[0:1, 0:w], True, True, sem=ctx.pe)
        P.wait("vector", ctx.pe)
        P.ins("vector", "tensor_copy", out_sb[:, j:j + w], ps[:, 0:w], sem=ctx.dve)

def emit_norm(ctx, kind, get_in_tile, out_bf, w_bc=None, b_bc=None, eps=1e-5, out_fp_tile=None):
    P = ctx.P
    for tt in range(NTILE):
        xt = get_in_tile(tt)
        st = ctx.stat
        ot = out_bf[:, tt * D:(tt + 1) * D]
        if kind == "ln":
            P.ins("vector", "reduce_sum", st[:, 0:1], xt, axis=AX.X, sem=ctx.dve)
            P.ins("vector", "tensor_scalar", st[:, 0:1], st[:, 0:1], 1.0 / D, None, OP.mult, sem=ctx.dve)
            P.ins("vector", "tensor_scalar", ctx.scr1[:, :], xt, st[:, 0:1], None, OP.subtract, sem=ctx.dve)
            src = ctx.scr1[:, :]
        else:
            src = xt
        P.wait("scalar", ctx.dve)
        P.ins("scalar", "activation", ot, src, AF.Square, accum_out=st[:, 1:2], sem=ctx.act)
        P.wait("vector", ctx.act)
        P.ins("vector", "tensor_scalar", st[:, 1:2], st[:, 1:2], 1.0 / D, None, OP.mult, sem=ctx.dve)
        P.wait("scalar", ctx.dve)
        P.ins("scalar", "activation", st[:, 2:3], st[:, 1:2], AF.Ln, bias=ctx.bias(float(eps)), sem=ctx.act)
        P.ins("scalar", "activation", st[:, 2:3], st[:, 2:3], AF.Exp, scale=-0.5, sem=ctx.act)
        P.wait("vector", ctx.act)
        if kind == "ln":
            P.ins("vector", "tensor_scalar", ctx.scr1[:, :], ctx.scr1[:, :], st[:, 2:3], None, OP.mult,
                  sem=ctx.dve)
            nsrc = ctx.scr1[:, :]
        else:
            P.ins("vector", "tensor_scalar", ctx.scr1[:, :], src, st[:, 2:3], None, OP.mult, sem=ctx.dve)
            nsrc = ctx.scr1[:, :]
        if out_fp_tile is not None:
            out_fp_tile(tt, nsrc)
        if w_bc is None:
            P.ins("vector", "tensor_copy", ot, nsrc, sem=ctx.dve)
        else:
            P.ins("vector", "tensor_mul", ctx.scr1[:, :], nsrc, w_bc[:, :], sem=ctx.dve)
            if b_bc is not None:
                P.ins("vector", "tensor_add", ctx.scr1[:, :], ctx.scr1[:, :], b_bc[:, :], sem=ctx.dve)
            P.ins("vector", "tensor_copy", ot, ctx.scr1[:, :], sem=ctx.dve)

def transpose_to_fmajor(ctx, src_bf, dram_scratch, dst_T, n_chunks):
    P = ctx.P
    W = n_chunks * 128
    P.wait("sync", ctx.dve)
    P.dma("sync", dram_scratch.rearrange("(t p) j -> p t j", p=128),
          src_bf[:, 0:NTILE * W].rearrange("p (t j) -> p t j", j=W), ctx.sdma)
    P.wait("scalar", ctx.sdma)
    for c in range(n_chunks):
        P.dma("scalar", dst_T[:, c * S:(c + 1) * S], dram_scratch[:, c * 128:(c + 1) * 128], ctx.adma,
              transpose=True)

def proj_fmajor_rope(ctx, xT, n_cin, w_sb, w_cols, cout_list, outT, CS, SN, qknorm, vhead_of=None,
                     out_stride=None):
    P = ctx.P
    n_out = len(cout_list)
    ostr = S if out_stride is None else out_stride
    for qt in range(NQT):
        for j, co in enumerate(cout_list):
            ps = ctx.psum[j % 2]
            P.wait("tensor", ctx.act)
            for ci in range(n_cin):
                P.mm(ps[:, :], w_sb[:, ci * w_cols + co * 128: ci * w_cols + co * 128 + 128],
                     xT[:, ci * S + qt * 512: ci * S + qt * 512 + 512],
                     ci == 0, ci == n_cin - 1, sem=ctx.pe)
            P.wait("scalar", ctx.pe)
            osl = outT[:, j * ostr + qt * 512: j * ostr + qt * 512 + 512]
            P.ins("scalar", "activation", osl, ps[:, :], AF.Copy, sem=ctx.act)
            if qknorm:
                P.ins("scalar", "activation", ctx.sq[:, :], ps[:, :], AF.Square, sem=ctx.act)
                P.wait("tensor", ctx.act)
                hs = ctx.psum[2]
                vh = j if vhead_of is None else vhead_of[j]
                P.mm(hs[0:16, :], ctx.C["HMAP"][:, vh * 16:(vh + 1) * 16], ctx.sq[:, :],
                     j == 0, j == n_out - 1, sem=ctx.pe)
        if qknorm:
            P.wait("scalar", ctx.pe)
            P.ins("scalar", "activation", ctx.rstd[0:16, qt * 512:qt * 512 + 512], ctx.psum[2][0:16, :],
                  AF.Ln, bias=ctx.bias(EPS0)[0:16, 0:1], scale=1.0 / HD, sem=ctx.act)
            P.ins("scalar", "activation", ctx.rstd[0:16, qt * 512:qt * 512 + 512],
                  ctx.rstd[0:16, qt * 512:qt * 512 + 512], AF.Exp, scale=-0.5, sem=ctx.act)
    for j, co in enumerate(cout_list):
        for qt in range(NQT):
            osl = outT[:, j * ostr + qt * 512: j * ostr + qt * 512 + 512]
            ps_sw = ctx.psum[3]
            P.wait("tensor", ctx.act)
            P.wait("tensor", ctx.dve)
            P.mm(ps_sw[:, :], ctx.C["SWAP"][:, :], osl, True, True, sem=ctx.pe)
            if qknorm:
                ps_bc = ctx.psum[4]
                vh = j if vhead_of is None else vhead_of[j]
                P.mm(ps_bc[:, :], ctx.C["BMAP"][0:16, vh * 128:(vh + 1) * 128],
                     ctx.rstd[0:16, qt * 512:qt * 512 + 512], True, True, sem=ctx.pe)
            P.wait("scalar", ctx.pe)
            P.ins("scalar", "activation", ctx.sw[:, :], ps_sw[:, :], AF.Copy, sem=ctx.act)
            if qknorm:
                P.ins("scalar", "activation", ctx.bc[:, :], ps_bc[:, :], AF.Copy, sem=ctx.act)
            P.wait("vector", ctx.act)
            CSs = CS[:, qt * 512:qt * 512 + 512]
            SNs = SN[:, qt * 512:qt * 512 + 512]
            P.ins("vector", "tensor_mul", ctx.r1[:, :], osl, CSs, sem=ctx.dve)
            P.ins("vector", "tensor_mul", ctx.r2[:, :], ctx.sw[:, :], SNs, sem=ctx.dve)
            P.ins("vector", "tensor_add", ctx.r1[:, :], ctx.r1[:, :], ctx.r2[:, :], sem=ctx.dve)
            if qknorm:
                P.ins("vector", "tensor_mul", osl, ctx.r1[:, :], ctx.bc[:, :], sem=ctx.dve)
            else:
                P.ins("vector", "tensor_copy", osl, ctx.r1[:, :], sem=ctx.dve)

def attn_core(ctx, QT, KT_of_chunk, V_sb, vsec, OT, heads, act_kind, band_of, chunks=None):
    P = ctx.P
    if "p_sb" not in ctx.C:
        ctx.C["p_sb"] = sbuf(P, "p_sb", [128, 512], BF16)[:, :]
        ctx.C["p_sb2"] = sbuf(P, "p_sb2", [128, 512], BF16)[:, :]
        ctx.C["zt"] = sbuf(P, "zt", [128, 512], BF16)[:, :]
        P.ins("vector", "memset", ctx.C["zt"], 0.0, sem=ctx.dve)
    p_sbs = [ctx.C["p_sb"], ctx.C["p_sb2"]]
    zt = ctx.C["zt"]
    P.wait("tensor", ctx.act)
    P.wait("tensor", ctx.dve)
    am = {0: 0, 1: 0}
    dm = {0: 0, 1: 0}
    eo_mark = [0]
    for c in (range(NCHUNK) if chunks is None else chunks):
        KTc = KT_of_chunk(c)
        for hh in range(2):
            h = 2 * c + hh
            if h not in heads:
                continue
            base = hh * 64
            vm = heads[h]
            if act_kind == "taylor":
                ps_cs = ctx.psum[6]
                P.wait("tensor", ctx.dve)
                for kt in range(NTILE):
                    P.mm(ps_cs[0:65, 0:1], V_sb[:, (kt * vsec + vm) * 65:(kt * vsec + vm) * 65 + 65],
                         ctx.C["ONES_C"][:, 0:1], kt == 0, kt == NTILE - 1, sem=ctx.pe)
                P.wait("vector", ctx.pe)
                P.ins("vector", "tensor_scalar", ctx.stat[0:65, 3:4], ps_cs[0:65, 0:1], 0.5, None, OP.mult,
                      sem=ctx.dve)
            for qt in range(NQT):
                ps_o = ctx.psum[5]
                items = band_of(qt)
                nIt = len(items)
                zfill = not (items[0][1] == 0 and items[0][2] == 512)
                if zfill:
                    P.wait("tensor", ctx.dve, eo_mark[0])
                    P.mm(ps_o[0:65, 0:512], V_sb[:, vm * 65:vm * 65 + 65], zt, True, False, sem=ctx.pe)
                info = []
                for bi in range(nIt + 1):
                    if bi < nIt:
                        kc, qlo, qhi, masks = items[bi]
                        par = bi % 2
                        ps_st = ctx.psum[par]
                        pb = p_sbs[par]
                        N = qhi - qlo
                        P.wait("tensor", ctx.act, am[par])
                        P.mm(ps_st[:, 0:N],
                             KTc[base:base + 64, kc * 128:(kc + 1) * 128],
                             QT[base:base + 64, c * S + qt * 512 + qlo: c * S + qt * 512 + qhi],
                             True, True, sem=ctx.pe, tile_position=(base, 0))
                        P.wait("scalar", ctx.pe)
                        P.wait("scalar", ctx.dve, dm[par])
                        if act_kind == "exp":
                            P.ins("scalar", "activation", pb[:, 0:N], ps_st[:, 0:N], AF.Exp, scale=0.125,
                                  sem=ctx.act)
                        else:
                            P.ins("scalar", "activation", pb[:, 0:N], ps_st[:, 0:N], AF.Square,
                                  scale=0.08838834764831845, bias=ctx.bias(0.7071067811865476), sem=ctx.act)
                        am[par] = ctx.act.n
                        if masks:
                            P.wait("vector", ctx.act, am[par])
                            for (subcol, mname) in masks:
                                off = subcol - qlo
                                P.ins("vector", "tensor_mul", pb[:, off:off + 128], pb[:, off:off + 128],
                                      ctx.C[mname][:, :], sem=ctx.dve)
                            dm[par] = ctx.dve.n
                        info.append((kc, qlo, qhi, N, par, am[par], dm[par] if masks else 0))
                    if bi >= 1:
                        kc0, qlo0, qhi0, N0, par0, a0, d0 = info[bi - 1]
                        P.wait("tensor", ctx.act, a0)
                        if d0:
                            P.wait("tensor", ctx.dve, d0)
                        if bi - 1 == 0 and not zfill:
                            P.wait("tensor", ctx.dve, eo_mark[0])
                        P.mm(ps_o[0:65, qlo0:qhi0],
                             V_sb[:, (kc0 * vsec + vm) * 65:(kc0 * vsec + vm) * 65 + 65],
                             p_sbs[par0][:, 0:N0], (bi - 1 == 0) and not zfill, bi - 1 == nIt - 1,
                             sem=ctx.pe)
                P.wait("vector", ctx.pe)
                if act_kind == "taylor":
                    P.ins("vector", "tensor_scalar", ctx.scr1[0:65, 0:512], ps_o[0:65, :],
                          ctx.stat[0:65, 3:4], None, OP.add, sem=ctx.dve)
                else:
                    P.ins("vector", "tensor_copy", ctx.scr1[0:65, 0:512], ps_o[0:65, :], sem=ctx.dve)
                eo_mark[0] = ctx.dve.n
                ps_n = ctx.psum[4]
                P.wait("tensor", ctx.dve)
                P.mm(ps_n[0:64, :], ctx.C["E64BC"][0:65, 0:64], ctx.scr1[0:65, 0:512], True, True, sem=ctx.pe)
                P.wait("scalar", ctx.pe)
                P.ins("scalar", "activation", ctx.scr1[0:64, 512:1024], ps_n[0:64, :], AF.Ln, sem=ctx.act)
                P.ins("scalar", "activation", ctx.scr1[0:64, 512:1024], ctx.scr1[0:64, 512:1024], AF.Exp,
                      scale=-1.0, sem=ctx.act)
                P.wait("vector", ctx.act)
                P.ins("vector", "tensor_mul", OT[base:base + 64, c * S + qt * 512: c * S + qt * 512 + 512],
                      ctx.scr1[0:64, 0:512], ctx.scr1[0:64, 512:1024], sem=ctx.dve)

def band_causal(qt):
    items = []
    for kc in range(0, qt * 4 + 4):
        if kc < qt * 4:
            items.append((kc, 0, 512, []))
        else:
            j = kc - qt * 4
            items.append((kc, j * 128, 512, [(j * 128, "M_CAUS")]))
    return items

def band_window(qt):
    items = []
    for kc in range(max(0, qt * 4 - 2), qt * 4 + 4):
        j = kc - qt * 4
        if j == -2:
            items.append((kc, 0, 128, [(0, "M_WIN")]))
        elif j == -1:
            items.append((kc, 0, 256, [(128, "M_WIN")]))
        else:
            qhi = min(512, (j + 3) * 128)
            masks = [(j * 128, "M_CAUS")]
            if (j + 2) * 128 < 512:
                masks.append(((j + 2) * 128, "M_WIN"))
            items.append((kc, j * 128, qhi, masks))
    return items

def band_full(qt):
    return [(kc, 0, 512, []) for kc in range(NTILE)]

L1_CONSTS = ["SWAP", "ONES_R", "ONES_C", "HMAP", "BMAP", "M_CAUS", "M_WIN", "E64BC", "IDN"]

def build_l0a():
    P = new_prog()
    ctx = Ctx(P)
    x_in = din(P, "x", [S, D], F32)
    w_n1 = din(P, "n1w", [1, D], F32)
    w_n2 = din(P, "n2w", [1, D], F32)
    w_router = din(P, "wr", [D, E], F32)
    w_gq = din(P, "w_gq", [D, D], F32)
    w_kvd = din(P, "w_kvd", [D, LAT], F32)
    w_kup = din(P, "w_kup", [LAT, D], F32)
    w_vup = din(P, "w_vup", [LAT, LAT], F32)
    w_gwo = din(P, "w_gwo", [D, D], F32)
    w_lq = din(P, "w_lq", [D, D], F32)
    w_lk = din(P, "w_lk", [D, D], F32)
    w_lv = din(P, "w_lv", [D, LAT], F32)
    w_lwo = din(P, "w_lwo", [D, D], F32)
    o_xn = dout(P, "xn0", [S, D], BF16)
    o_logits = dout(P, "logits0", [S, E], F32)
    o_s0 = dout(P, "s0", [S, D], F32)
    a_dram = dtmp(P, "a_dram", [S, D], F32)
    tsc = dtmp(P, "tscratch", [S, D], BF16)
    tsc2 = dtmp(P, "tscratch2", [S, D], BF16)

    ctx.loadc(L1_CONSTS)
    c_csg = din(P, "c_CSG", [128, S], F32)
    c_sng = din(P, "c_SNG", [128, S], F32)
    c_csl = din(P, "c_CSL", [128, S], F32)
    c_snl = din(P, "c_SNL", [128, S], F32)
    CS_cur = sbuf(P, "CS_cur", [128, S], BF16)
    SN_cur = sbuf(P, "SN_cur", [128, S], BF16)
    wq_sb = sbuf(P, "wq_sb", [128, NCHUNK * D], BF16)
    wo_sb = sbuf(P, "wo_sb", [128, NCHUNK * D], BF16)
    wk_sb = sbuf(P, "wk_sb", [128, NCHUNK * D], BF16)
    wv_sb = sbuf(P, "wv_sb", [128, NCHUNK * LAT], BF16)
    wvu_sb = sbuf(P, "wvu_sb", [128, 2 * LAT], BF16)
    wr_sb = sbuf(P, "wr_sb", [128, NCHUNK * E], F32)
    xtT = sbuf(P, "xtT", [128, 128], F32)
    xT = sbuf(P, "xT", [128, NCHUNK * S], BF16)
    QT = sbuf(P, "QT", [128, NCHUNK * S], BF16)
    KTc = sbuf(P, "KTc", [128, S], BF16)
    latT = sbuf(P, "latT", [128, 2 * S], BF16)
    OT = sbuf(P, "OT", [128, NCHUNK * S], BF16)
    xnbf = OT
    V_sb = sbuf(P, "V_sb", [128, NTILE * 4 * 65], BF16)
    nwbc = sbuf(P, "nwbc", [128, D], BF16)
    atile = sbuf(P, "atile", [128, D], F32)
    xtile = atile
    logbuf = sbuf(P, "logbuf", [128, NTILE * E], F32)

    def load_w(dst, ap, rows, cols):
        nch = rows // 128
        P.dma("gpsimd", dst[:, 0:nch * cols].rearrange("p (ci j) -> p ci j", j=cols),
              ap.rearrange("(ci p) j -> p ci j", p=128), ctx.gdma)

    load_w(wq_sb, w_gq, D, D)
    load_w(wo_sb, w_gwo, D, D)
    load_w(wk_sb, w_kup, LAT, D)
    load_w(wv_sb, w_kvd, D, LAT)
    load_w(wvu_sb, w_vup, LAT, LAT)
    P.dma("sync", wr_sb[:, :].rearrange("p (ci j) -> p ci j", j=E),
          w_router.rearrange("(ci p) j -> p ci j", p=128), ctx.sdma)


    def get_x(tt):
        P.wait("sync", ctx.dve)
        P.dma("sync", xtile[:, :], x_in[tt * 128:(tt + 1) * 128, :], ctx.sdma)
        P.wait("vector", ctx.sdma)
        return xtile[:, :]

    def put_a(tt, src_ap):
        P.wait("sync", ctx.dve)
        P.dma("sync", a_dram[tt * 128:(tt + 1) * 128, :], src_ap, ctx.sdma)
        P.wait("vector", ctx.sdma)
        P.wait("tensor", ctx.dve)
        P.wait("tensor", ctx.sdma)
        P.ins("vector", "memset", logbuf[:, tt * E:(tt + 1) * E], 0.0, sem=ctx.dve)
        for cc in range(NCHUNK):
            pst = ctx.psum[4]
            P.wait("tensor", ctx.act)
            P.wait("tensor", ctx.dve)
            P.ins("tensor", "transpose", pst[:, 0:128], src_ap[:, cc * 128:(cc + 1) * 128],
                  ctx.C["IDN"][:, :], sem=ctx.pe)
            P.wait("scalar", ctx.pe)
            P.ins("scalar", "activation", xtT[:, :], pst[:, 0:128], AF.Copy, sem=ctx.act)
            P.wait("tensor", ctx.act)
            P.mm(ctx.psum[5][0:128, 0:E], xtT[:, :], wr_sb[:, cc * E:(cc + 1) * E],
                 True, True, sem=ctx.pe)
            P.wait("vector", ctx.pe)
            P.ins("vector", "tensor_add", logbuf[:, tt * E:(tt + 1) * E],
                  logbuf[:, tt * E:(tt + 1) * E], ctx.psum[5][0:128, 0:E], sem=ctx.dve)
        P.wait("vector", ctx.pe)

    emit_norm(ctx, "ln", get_x, xnbf, eps=1e-5, out_fp_tile=put_a)
    P.wait("sync", ctx.dve)
    P.dma("sync", o_xn.rearrange("(t p) j -> p t j", p=128), xnbf[:, :].rearrange("p (t j) -> p t j", j=D), ctx.sdma)
    transpose_to_fmajor(ctx, xnbf, tsc, xT, NCHUNK)
    P.wait("tensor", ctx.adma)
    P.wait("tensor", ctx.gdma)
    P.wait("sync", ctx.act)
    P.dma("sync", o_logits.rearrange("(t p) j -> p t j", p=128),
          logbuf[:, :].rearrange("p (t j) -> p t j", j=E), ctx.sdma)

    def attn_pass(kind):
        glob = kind == "glob"
        P.wait("gpsimd", ctx.dve)
        P.wait("gpsimd", ctx.pe)
        P.dma("gpsimd", CS_cur[:, :], c_csg if glob else c_csl, ctx.gdma)
        P.dma("gpsimd", SN_cur[:, :], c_sng if glob else c_snl, ctx.gdma)
        if FUSE["on"] and FUSE.get("cc_early"):
            FUSE["cc_early"]()
            FUSE["cc_early"] = None
        CS, SN = CS_cur, SN_cur
        bcast_row(ctx, w_n1 if glob else w_n2, nwbc, D)
        P.wait("vector", ctx.gdma)
        P.wait("vector", ctx.pe)
        def get_a(tt):
            P.wait("sync", ctx.dve)
            P.dma("sync", atile[:, :], a_dram[tt * 128:(tt + 1) * 128, :], ctx.sdma)
            P.wait("vector", ctx.sdma)
            return atile[:, :]
        emit_norm(ctx, "rms", get_a, xnbf, w_bc=nwbc, eps=EPS0)
        transpose_to_fmajor(ctx, xnbf, tsc, xT, NCHUNK)
        P.wait("tensor", ctx.adma)
        proj_fmajor_rope(ctx, xT, NCHUNK, wq_sb, D, list(range(NCHUNK)), QT, CS, SN, True)
        if glob:
            for c2 in range(2):
                for qt in range(NQT):
                    ps = ctx.psum[0]
                    P.wait("tensor", ctx.act)
                    for ci in range(NCHUNK):
                        P.mm(ps[:, :], wv_sb[:, ci * LAT + c2 * 128: ci * LAT + c2 * 128 + 128],
                             xT[:, ci * S + qt * 512: ci * S + qt * 512 + 512],
                             ci == 0, ci == NCHUNK - 1, sem=ctx.pe)
                    P.wait("scalar", ctx.pe)
                    P.ins("scalar", "activation", latT[:, c2 * S + qt * 512: c2 * S + qt * 512 + 512],
                          ps[:, :], AF.Copy, sem=ctx.act)
            P.wait("vector", ctx.gdma)
            P.ins("vector", "memset", V_sb[:, :], 1.0, sem=ctx.dve)
            for kt in range(NTILE):
                ps = ctx.psum[1]
                P.wait("tensor", ctx.act)
                P.wait("tensor", ctx.dve)
                for ci in range(2):
                    P.mm(ps[:, 0:LAT], latT[:, ci * S + kt * 128: ci * S + kt * 128 + 128],
                         wvu_sb[:, ci * LAT:(ci + 1) * LAT], ci == 0, ci == 1, sem=ctx.pe)
                P.wait("vector", ctx.pe)
                dst = V_sb[:, kt * 260: kt * 260 + 260].rearrange("p (s j) -> p s j", j=65)[:, :, 0:64]
                P.ins("vector", "tensor_copy", dst, ps[:, 0:LAT].rearrange("p (s j) -> p s j", j=64),
                      sem=ctx.dve)
        else:
            P.ins("vector", "memset", V_sb[:, :], 1.0, sem=ctx.dve)
            for kt in range(NTILE):
                ps = ctx.psum[1]
                P.wait("tensor", ctx.act)
                P.wait("tensor", ctx.dve)
                for ci in range(NCHUNK):
                    P.mm(ps[:, 0:LAT], xT[:, ci * S + kt * 128: ci * S + kt * 128 + 128],
                         wv_sb[:, ci * LAT:(ci + 1) * LAT], ci == 0, ci == NCHUNK - 1, sem=ctx.pe)
                P.wait("vector", ctx.pe)
                dst = V_sb[:, kt * 260: kt * 260 + 260].rearrange("p (s j) -> p s j", j=65)[:, :, 0:64]
                P.ins("vector", "tensor_copy", dst, ps[:, 0:LAT].rearrange("p (s j) -> p s j", j=64),
                      sem=ctx.dve)
        band = band_causal if glob else band_window
        for c in range(NCHUNK):
            n_cin = 2 if glob else NCHUNK
            srcT = latT if glob else xT
            proj_fmajor_rope(ctx, srcT, n_cin, wk_sb, D, [c], KTc, CS, SN, True, vhead_of={0: c},
                             out_stride=S)
            heads = {2 * c: (2 * c) // 4, 2 * c + 1: (2 * c + 1) // 4}
            attn_core(ctx, QT, lambda cc: KTc[:, :], V_sb, 4, OT, heads, "exp", band, chunks=[c])
        dst_dram = a_dram if glob else o_s0
        for tt in range(NTILE):
            P.wait("sync", ctx.dve)
            P.dma("sync", atile[:, :], a_dram[tt * 128:(tt + 1) * 128, :], ctx.sdma)
            P.wait("vector", ctx.sdma)
            for half in range(2):
                ps = ctx.psum[2 + half]
                P.wait("tensor", ctx.dve)
                for ci in range(NCHUNK):
                    P.mm(ps[:, :], OT[:, ci * S + tt * 128: ci * S + tt * 128 + 128],
                         wo_sb[:, ci * D + half * 512: ci * D + half * 512 + 512],
                         ci == 0, ci == NCHUNK - 1, sem=ctx.pe)
                P.wait("vector", ctx.pe)
                P.ins("vector", "tensor_add", atile[:, half * 512:(half + 1) * 512],
                      atile[:, half * 512:(half + 1) * 512], ps[:, :], sem=ctx.dve)
            P.wait("sync", ctx.dve)
            P.dma("sync", dst_dram[tt * 128:(tt + 1) * 128, :], atile[:, :], ctx.sdma)
            P.wait("vector", ctx.sdma)

    attn_pass("glob")
    P.wait("gpsimd", ctx.sdma)
    P.wait("gpsimd", ctx.pe)
    load_w(wq_sb, w_lq, D, D)
    load_w(wo_sb, w_lwo, D, D)
    load_w(wk_sb, w_lk, D, D)
    load_w(wv_sb, w_lv, D, LAT)
    P.wait("tensor", ctx.gdma)
    P.wait("scalar", ctx.gdma)
    P.wait("vector", ctx.gdma)
    attn_pass("loc")
    P.wait("gpsimd", ctx.sdma)
    P.wait("gpsimd", ctx.adma)
    P.finalize()
    return P

def prep_l0a_weights(inp):
    PC = perm_cols()
    P64 = _perm64()
    def rep_perm(w):
        out = np.empty((w.shape[0], D), np.float32)
        for h in range(H):
            kv = h // 4
            out[:, h * 64:(h + 1) * 64] = w[:, kv * 64 + P64]
        return out
    d = {}
    d["w_gq"] = np.ascontiguousarray(np.asarray(inp["L0_attn_g_wq"], np.float32)[:, PC])
    d["w_kvd"] = np.asarray(inp["L0_attn_g_kv_down"], np.float32)
    d["w_kup"] = np.ascontiguousarray(rep_perm(np.asarray(inp["L0_attn_g_k_up"], np.float32)))
    d["w_vup"] = np.asarray(inp["L0_attn_g_v_up"], np.float32)
    d["w_gwo"] = np.asarray(inp["L0_attn_g_wo"], np.float32)
    d["w_lq"] = np.ascontiguousarray(np.asarray(inp["L0_attn_l_wq"], np.float32)[:, PC])
    d["w_lk"] = np.ascontiguousarray(rep_perm(np.asarray(inp["L0_attn_l_wk"], np.float32)))
    d["w_lv"] = np.asarray(inp["L0_attn_l_wv"], np.float32)
    d["w_lwo"] = np.asarray(inp["L0_attn_l_wo"], np.float32)
    d["n1w"] = np.asarray(inp["L0_attn_norm1_w"], np.float32).reshape(1, D)
    d["n2w"] = np.asarray(inp["L0_attn_norm2_w"], np.float32).reshape(1, D)
    d["wr"] = np.asarray(inp["L0_ffn_w_router"], np.float32)
    for nm in L1_CONSTS + ["CSG", "SNG", "CSL", "SNL"]:
        d["c_" + nm] = CONSTS[nm]
    return d

L2_CONSTS = ["TRIU_INC", "TRIU_EXCF", "ONES_R", "ONES_C", "ONES_CF", "IOTA8R"]
TG = 8192
NTT = TG // 128

def build_l0f():
    P = new_prog()
    ctx = Ctx(P)
    lg_in = din(P, "logits_g", [TG, E], F32)
    xn_in = din(P, "xn_g", [TG, D], BF16)
    wi_in = [din(P, f"wi{i}", [D, FF], F32) for i in range(2)]
    wo_in = [din(P, f"wo{i}", [FF, D], F32) for i in range(2)]
    meta_in = din(P, "meta", [128, 2], F32)
    o_out = dout(P, "out_all", [2 * 640, D], BF16)
    o_gidx = dout(P, "gidx", [TG, 2], I32)
    o_gate = dout(P, "gates", [TG, 2], F32)
    xe_dram = dtmp(P, "xe_dram", [2 * 640, D], BF16)
    bscr = dtmp(P, "bscr", [1, 512], F32)
    ctx.loadc(L2_CONSTS)

    lgt = sbuf(P, "lgt", [128, NTT * E], F32)
    prb = sbuf(P, "prb", [128, NTT * E], F32)
    cnt = sbuf(P, "cnt", [128, NTT * E], F32)
    selb = sbuf(P, "selb", [128, NTT * E], BF16)
    mscr = sbuf(P, "mscr", [128, NTT * E], F32)
    ek = sbuf(P, "ek", [128, 2 * NTT], F32)
    pk = sbuf(P, "pk", [128, 2 * NTT], F32)
    vk = sbuf(P, "vk", [128, 2 * NTT], F32)
    gat = sbuf(P, "gat", [128, 2 * NTT], F32)
    didx = sbuf(P, "didx", [128, 2 * NTT], I32)
    gidx_sb = sbuf(P, "gidx_sb", [128, 2 * 16], I32)
    meta = sbuf(P, "meta_sb", [128, 2], F32)
    rowa = sbuf(P, "rowa", [1, 512], F32)
    t64 = sbuf(P, "t64", [64, E], F32)
    o64 = sbuf(P, "o64", [64, E], F32)
    tot0 = sbuf(P, "tot0", [1, E], F32)
    stg = [sbuf(P, f"stg{i}", [128, D], BF16) for i in range(2)]
    zt = sbuf(P, "ztile", [128, D], BF16)
    xeT = sbuf(P, "xeT", [128, NCHUNK * 640], BF16)
    wi_sl = [sbuf(P, f"wi_sl{i}", [128, NCHUNK * 512], BF16) for i in range(2)]
    wo_sl = [sbuf(P, f"wo_sl{i}", [128, 4 * D], BF16) for i in range(2)]
    hmT = sbuf(P, "hmT", [128, 4 * 128], BF16)
    ye_sb = sbuf(P, "ye_sb", [128, 5 * D], F32)
    outb = sbuf(P, "outb", [128, D], BF16)

    L3 = lambda t: t.rearrange("p (tt e) -> p tt e", e=E)
    P.dma("sync", lgt[:, :].rearrange("p (tt e) -> p tt e", e=E),
          lg_in.rearrange("(tt p) e -> p tt e", p=128), ctx.sdma)
    P.dma("sync", meta[:, :], meta_in, ctx.sdma)
    P.ins("vector", "memset", zt[:, :], 0.0, sem=ctx.dve)
    P.wait("gpsimd", ctx.dve)
    for i in range(10):
        P.dma("gpsimd", xe_dram[i * 128:(i + 1) * 128, :], zt[:, :], ctx.gdma)
    P.wait("vector", ctx.sdma)
    P.ins("vector", "tensor_reduce", mscr[:, 0:NTT], L3(lgt), axis=AX.X, op=OP.max, sem=ctx.dve)
    P.ins("vector", "tensor_tensor", L3(prb), L3(lgt),
          mscr[:, 0:NTT].unsqueeze(2).to_broadcast([128, NTT, E]), OP.subtract, sem=ctx.dve)
    P.wait("scalar", ctx.dve)
    P.ins("scalar", "activation", prb[:, :], prb[:, :], AF.Exp, sem=ctx.act)
    P.wait("vector", ctx.act)
    P.ins("vector", "tensor_reduce", mscr[:, 0:NTT], L3(prb), axis=AX.X, op=OP.add, sem=ctx.dve)
    P.ins("vector", "reciprocal", mscr[:, NTT:2 * NTT], mscr[:, 0:NTT], sem=ctx.dve)
    P.ins("vector", "tensor_tensor", L3(prb), L3(prb),
          mscr[:, NTT:2 * NTT].unsqueeze(2).to_broadcast([128, NTT, E]), OP.mult, sem=ctx.dve)
    P.ins("vector", "memset", cnt[:, :], 0.0, sem=ctx.dve)
    for dd in range(1, E):
        w = E - dd
        A, B = L3(lgt)[:, :, dd:E], L3(lgt)[:, :, 0:w]
        P.ins("vector", "tensor_tensor", L3(mscr)[:, :, 0:w], B, A, OP.is_ge, sem=ctx.dve)
        P.ins("vector", "tensor_tensor", L3(cnt)[:, :, dd:E], L3(cnt)[:, :, dd:E], L3(mscr)[:, :, 0:w],
              OP.add, sem=ctx.dve)
        P.ins("vector", "tensor_tensor", L3(mscr)[:, :, 0:w], A, B, OP.is_gt, sem=ctx.dve)
        P.ins("vector", "tensor_tensor", L3(cnt)[:, :, 0:w], L3(cnt)[:, :, 0:w], L3(mscr)[:, :, 0:w],
              OP.add, sem=ctx.dve)
    for k in range(2):
        kc = lambda t: t[:, k * NTT:(k + 1) * NTT]
        P.ins("vector", "tensor_scalar", mscr[:, :], cnt[:, :], float(k), None, OP.is_equal, sem=ctx.dve)
        P.ins("vector", "tensor_copy", selb[:, :], mscr[:, :], sem=ctx.dve)
        P.ins("vector", "tensor_tensor", mscr[:, :], mscr[:, :], prb[:, :], OP.mult, sem=ctx.dve)
        P.ins("vector", "tensor_reduce", kc(gat), L3(mscr), axis=AX.X, op=OP.add, sem=ctx.dve)
        P.ins("vector", "tensor_scalar", mscr[:, :], cnt[:, :], float(k), None, OP.is_equal, sem=ctx.dve)
        P.ins("vector", "tensor_tensor", L3(mscr), L3(mscr),
              ctx.C["IOTA8R"][:, 0:E].unsqueeze(1).to_broadcast([128, NTT, E]), OP.mult,
              sem=ctx.dve)
        P.ins("vector", "tensor_reduce", kc(ek), L3(mscr), axis=AX.X, op=OP.add, sem=ctx.dve)
        P.wait("tensor", ctx.dve)
        P.mm(ctx.psum[1][0:1, 0:512], ctx.C["ONES_C"][:, 0:1], selb[:, :], True, True, sem=ctx.pe)
        P.wait("vector", ctx.pe)
        P.ins("vector", "tensor_copy", rowa[0:1, :], ctx.psum[1][0:1, 0:512], sem=ctx.dve)
        P.wait("sync", ctx.dve)
        P.dma("sync", bscr, rowa[0:1, :], ctx.sdma)
        P.wait("sync", ctx.sdma)
        P.dma("sync", t64[0:64, 0:E], bscr.rearrange("a (t e) -> (a t) e", e=E), ctx.sdma)
        P.wait("tensor", ctx.sdma)
        P.mm(ctx.psum[2][0:64, 0:E], ctx.C["TRIU_EXCF"][0:64, 0:64], t64[0:64, 0:E], True,
             k == 0, sem=ctx.pe)
        if k == 0:
            P.mm(ctx.psum[3][0:1, 0:E], ctx.C["ONES_CF"][0:64, 0:1], t64[0:64, 0:E], True, True, sem=ctx.pe)
            P.wait("vector", ctx.pe)
            P.ins("vector", "tensor_copy", tot0[0:1, :], ctx.psum[3][0:1, 0:E], sem=ctx.dve)
        else:
            P.wait("tensor", ctx.dve)
            P.mm(ctx.psum[2][0:64, 0:E], ctx.C["ONES_R"][0:1, 0:64], tot0[0:1, :], False, True, sem=ctx.pe)
        P.wait("vector", ctx.pe)
        P.ins("vector", "tensor_copy", o64[0:64, :], ctx.psum[2][0:64, 0:E], sem=ctx.dve)
        P.wait("sync", ctx.dve)
        P.dma("sync", bscr.rearrange("a (t e) -> (a t) e", e=E), o64[0:64, :], ctx.sdma)
        P.wait("sync", ctx.sdma)
        P.dma("sync", rowa[0:1, :], bscr, ctx.sdma)
        P.wait("tensor", ctx.sdma)
        P.wait("tensor", ctx.dve)
        P.mm(ctx.psum[0][:, 0:512], ctx.C["TRIU_INC"][:, :], selb[:, :], True, False, sem=ctx.pe)
        P.mm(ctx.psum[0][:, 0:512], ctx.C["ONES_R"][0:1, :], rowa[0:1, :], False, True, sem=ctx.pe)
        P.wait("vector", ctx.pe)
        P.ins("vector", "tensor_scalar", mscr[:, :], ctx.psum[0][:, 0:512], 1.0, None, OP.subtract,
              sem=ctx.dve)
        P.ins("vector", "tensor_scalar", selb[:, :], cnt[:, :], float(k), None, OP.is_equal, sem=ctx.dve)
        P.ins("vector", "tensor_tensor", mscr[:, :], mscr[:, :], selb[:, :], OP.mult, sem=ctx.dve)
        P.ins("vector", "tensor_reduce", kc(pk), L3(mscr), axis=AX.X, op=OP.add, sem=ctx.dve)
        P.ins("vector", "tensor_scalar", kc(vk), kc(pk), 640.0, None, OP.is_lt, sem=ctx.dve)
    for k in range(2):
        kc = lambda t: t[:, k * NTT:(k + 1) * NTT]
        P.ins("vector", "tensor_scalar", kc(mscr), kc(ek), 640.0, None, OP.mult, sem=ctx.dve)
        P.ins("vector", "tensor_tensor", kc(mscr), kc(mscr), kc(pk), OP.add, sem=ctx.dve)
        P.ins("vector", "tensor_scalar", kc(mscr), kc(mscr), float(OOB), None, OP.subtract, sem=ctx.dve)
        P.ins("vector", "tensor_tensor", kc(mscr), kc(mscr), kc(vk), OP.mult, sem=ctx.dve)
        P.ins("vector", "tensor_scalar", kc(mscr), kc(mscr), float(OOB), None, OP.add, sem=ctx.dve)
    return P, ctx, dict(lgt=lgt, prb=prb, cnt=cnt, selb=selb, mscr=mscr, ek=ek, pk=pk, vk=vk, gat=gat,
                        didx=didx, gidx_sb=gidx_sb, meta=meta, stg=stg, zt=zt, xeT=xeT, wi_sl=wi_sl,
                        wo_sl=wo_sl, hmT=hmT, ye_sb=ye_sb, outb=outb, xe_dram=xe_dram, xn_in=xn_in,
                        wi_in=wi_in, wo_in=wo_in, o_out=o_out, o_gidx=o_gidx, o_gate=o_gate)

def build_l0f_full():
    P, ctx, B = build_l0f()
    lgt, prb, cnt, selb, mscr = B["lgt"], B["prb"], B["cnt"], B["selb"], B["mscr"]
    ek, pk, vk, gat, didx = B["ek"], B["pk"], B["vk"], B["gat"], B["didx"]
    meta, stg, zt, xeT = B["meta"], B["stg"], B["zt"], B["xeT"]
    wi_sl, wo_sl, hmT, ye_sb, outb = B["wi_sl"], B["wo_sl"], B["hmT"], B["ye_sb"], B["outb"]
    xe_dram, xn_in, wi_in, wo_in = B["xe_dram"], B["xn_in"], B["wi_in"], B["wo_in"]
    o_out, o_gidx, o_gate = B["o_out"], B["o_gidx"], B["o_gate"]
    for k in range(2):
        kc = lambda t: t[:, k * NTT:(k + 1) * NTT]
        P.ins("vector", "tensor_scalar", kc(cnt), kc(ek), meta[:, 0:1], None, OP.is_equal, sem=ctx.dve)
        P.ins("vector", "tensor_scalar", kc(mscr), kc(ek), meta[:, 1:2], None, OP.is_equal, sem=ctx.dve)
        P.ins("vector", "tensor_tensor", kc(cnt), kc(cnt), kc(mscr), OP.add, sem=ctx.dve)
        P.ins("vector", "tensor_tensor", kc(cnt), kc(cnt), kc(vk), OP.mult, sem=ctx.dve)
        P.ins("vector", "tensor_scalar", kc(mscr), kc(mscr), 640.0, None, OP.mult, sem=ctx.dve)
        P.ins("vector", "tensor_tensor", kc(mscr), kc(mscr), kc(pk), OP.add, sem=ctx.dve)
        P.ins("vector", "tensor_scalar", kc(mscr), kc(mscr), float(OOB), None, OP.subtract, sem=ctx.dve)
        P.ins("vector", "tensor_tensor", kc(mscr), kc(mscr), kc(cnt), OP.mult, sem=ctx.dve)
        P.ins("vector", "tensor_scalar", kc(mscr), kc(mscr), float(OOB), None, OP.add, sem=ctx.dve)
        P.ins("vector", "tensor_copy", kc(didx), kc(mscr), sem=ctx.dve)
    P.wait("gpsimd", ctx.dve)
    breg = P.breg(2 * 640 - 1)
    g0 = ctx.gsc.n
    for tt in range(NTT):
        par = tt % 2
        if tt >= 2:
            P.wait("sync", ctx.gsc, g0 + 32 * (tt - 1))
        P.dma("sync", stg[par][:, :], xn_in[tt * 128:(tt + 1) * 128, :], ctx.sdma)
        P.wait("gpsimd", ctx.sdma)
        for k in range(2):
            P.idma(xe_dram, IOOA(didx[:, k * NTT + tt:k * NTT + tt + 1], 0), stg[par][:, :], None,
                   ctx.gsc, bounds=breg, oob_is_err=False)
    gsc2 = sbuf(P, "gsc2", [128, 2 * NTT], F32)
    for k in range(2):
        kc = lambda t: t[:, k * NTT:(k + 1) * NTT]
        P.ins("vector", "tensor_scalar", kc(gsc2), kc(ek), 640.0, None, OP.mult, sem=ctx.dve)
        P.ins("vector", "tensor_tensor", kc(gsc2), kc(gsc2), kc(pk), OP.add, sem=ctx.dve)
        P.ins("vector", "tensor_scalar", kc(gsc2), kc(gsc2), float(OOB), None, OP.subtract, sem=ctx.dve)
        P.ins("vector", "tensor_tensor", kc(gsc2), kc(gsc2), kc(vk), OP.mult, sem=ctx.dve)
        P.ins("vector", "tensor_scalar", kc(gsc2), kc(gsc2), float(OOB), None, OP.add, sem=ctx.dve)
    gidx_i = sbuf(P, "gidx_i", [128, 2 * NTT], I32)
    gat2 = sbuf(P, "gat2", [128, 2 * NTT], F32)
    P.ins("vector", "tensor_copy", gidx_i[:, :].rearrange("p (tt k) -> p k tt", k=2),
          gsc2[:, :].rearrange("p (k tt) -> p k tt", k=2), sem=ctx.dve)
    P.ins("vector", "tensor_copy", gat2[:, :].rearrange("p (tt k) -> p k tt", k=2),
          gat[:, :].rearrange("p (k tt) -> p k tt", k=2), sem=ctx.dve)
    P.wait("sync", ctx.dve)
    P.dma("sync", o_gidx.rearrange("(tt p) k -> p tt k", p=128),
          gidx_i[:, :].rearrange("p (tt k) -> p tt k", k=2), ctx.sdma)
    P.dma("sync", o_gate.rearrange("(tt p) k -> p tt k", p=128),
          gat2[:, :].rearrange("p (tt k) -> p tt k", k=2), ctx.sdma)
    if FUSE["on"]:
        gidxp = dout(P, "gidxp", [TG, 16], I32)
        gatep = dout(P, "gatep", [TG, 16], F32)
        P.dma("sync", gidxp[:, 0:2].rearrange("(tt p) k -> p tt k", p=128),
              gidx_i[:, :].rearrange("p (tt k) -> p tt k", k=2), ctx.sdma)
        P.dma("sync", gatep[:, 0:2].rearrange("(tt p) k -> p tt k", p=128),
              gat2[:, :].rearrange("p (tt k) -> p tt k", k=2), ctx.sdma)
    P.wait("scalar", ctx.gsc)
    for el in range(2):
        P.wait("scalar", ctx.pe)
        for hc in range(NCHUNK):
            P.dma("scalar", xeT[:, hc * 640:(hc + 1) * 640],
                  xe_dram[el * 640:(el + 1) * 640, hc * 128:(hc + 1) * 128], ctx.adma, transpose=True)
        P.wait("vector", ctx.gdma)
        P.ins("vector", "memset", ye_sb[:, :], 0.0, sem=ctx.dve)
        P.wait("tensor", ctx.adma)
        for fs in range(NCHUNK):
            par = fs % 2
            if fs >= 2 or el > 0:
                P.wait("gpsimd", ctx.pe)
            P.dma("gpsimd", wi_sl[par][:, :].rearrange("p (hc f) -> p hc f", f=512),
                  wi_in[el][:, fs * 512:(fs + 1) * 512].rearrange("(hc p) f -> p hc f", p=128), ctx.gdma)
            P.dma("gpsimd", wo_sl[par][:, :].rearrange("p (fc h) -> p fc h", h=D),
                  wo_in[el][fs * 512:(fs + 1) * 512, :].rearrange("(fc p) h -> p fc h", p=128), ctx.gdma)
            P.wait("tensor", ctx.gdma)
            for rt in range(5):
                for fc in range(4):
                    ps = ctx.psum[fc % 2]
                    P.wait("tensor", ctx.act)
                    for hcc in range(NCHUNK):
                        P.mm(ps[0:128, 0:128], wi_sl[par][:, hcc * 512 + fc * 128: hcc * 512 + fc * 128 + 128],
                             xeT[:, hcc * 640 + rt * 128: hcc * 640 + rt * 128 + 128],
                             hcc == 0, hcc == NCHUNK - 1, sem=ctx.pe)
                    P.wait("scalar", ctx.pe)
                    P.ins("scalar", "activation", hmT[:, fc * 128:(fc + 1) * 128], ps[0:128, 0:128],
                          AF.Relu, sem=ctx.act)
                for half in range(2):
                    ps = ctx.psum[4 + half]
                    P.wait("tensor", ctx.act)
                    P.wait("tensor", ctx.dve)
                    for fc in range(4):
                        P.mm(ps[:, :], hmT[:, fc * 128:(fc + 1) * 128],
                             wo_sl[par][:, fc * D + half * 512: fc * D + half * 512 + 512],
                             fc == 0, fc == 3, sem=ctx.pe)
                    P.wait("vector", ctx.pe)
                    sl = ye_sb[:, rt * D + half * 512: rt * D + half * 512 + 512]
                    P.ins("vector", "tensor_add", sl, sl, ps[:, :], sem=ctx.dve)
        for rt in range(5):
            P.ins("vector", "tensor_copy", outb[:, :], ye_sb[:, rt * D:(rt + 1) * D], sem=ctx.dve)
            P.wait("sync", ctx.dve)
            P.dma("sync", o_out[el * 640 + rt * 128: el * 640 + (rt + 1) * 128, :], outb[:, :], ctx.sdma)
            P.wait("vector", ctx.sdma)
    P.wait("gpsimd", ctx.sdma)
    P.wait("gpsimd", ctx.adma)
    P.finalize()
    return P

L3_CONSTS = ["SWAP", "ONES_R", "ONES_C", "HMAP", "BMAP", "E64BC", "CMAP", "SMAP", "MELB", "IDN"]

def build_l1a(upto="all"):
    P = new_prog()
    ctx = Ctx(P, rstd=False)
    x_in = din(P, "x", [S, D], F32)
    s0_in = din(P, "s0", [S, D], F32)
    oa_in = din(P, "out_all_g", [5120, D], BF16)
    gi_in = din(P, "gidx", [S, 2], I32)
    ga_in = din(P, "gates", [S, 2], F32)
    lnw_in = din(P, "ln_w", [1, D], F32)
    lnb_in = din(P, "ln_b", [1, D], F32)
    nw_in = din(P, "nw", [1, D], F32)
    wq_in = din(P, "w_q", [D, D], F32)
    wk_in = din(P, "w_k", [D, D], F32)
    wv_in = din(P, "w_v", [D, D], F32)
    wo_in = din(P, "w_o", [D, D], F32)
    rwT_in = din(P, "rwT", [D, E], F32)
    rb_in = din(P, "rb", [1, E], F32)
    th_in = din(P, "theta", [1, 1], F32)
    o_x1 = dout(P, "x1", [S, D], F32)
    o_xnn = dout(P, "xnn", [S, D], BF16)
    o_lg = dout(P, "logits1", [S, E], F32)
    o_s1 = dout(P, "s1", [S, D], F32)
    tsc = dtmp(P, "tscratch", [S, D], BF16)
    tsc2 = dtmp(P, "tscratch2", [S, D], BF16)
    ctx.loadc(L3_CONSTS)
    t32_in = din(P, "c_T32", [32, S], F32)
    CS_cur = sbuf(P, "CS_cur", [128, S], BF16)
    SN_cur = sbuf(P, "SN_cur", [128, S], BF16)
    wA = sbuf(P, "wA", [128, NCHUNK * D], BF16)
    wB = sbuf(P, "wB", [128, NCHUNK * D], BF16)
    wr_sb = sbuf(P, "wr_sb", [128, NCHUNK * E], F32)
    xT = sbuf(P, "xT", [128, NCHUNK * S], BF16)
    QT = sbuf(P, "QT", [128, NCHUNK * S], BF16)
    KTc = sbuf(P, "KTc", [128, S], BF16)
    OT = sbuf(P, "OT", [128, NCHUNK * S], BF16)
    xnbf = OT
    V_sb = sbuf(P, "V_sb", [128, NTILE * 16 * 65], BF16)
    lnwbc = sbuf(P, "lnwbc", [128, D], BF16)
    lnbbc = sbuf(P, "lnbbc", [128, D], BF16)
    nwbc = sbuf(P, "nwbc", [128, D], F32)
    lgacc = sbuf(P, "lgacc", [128, E], F32)
    rbbc = sbuf(P, "rbbc", [128, E], F32)
    atile = sbuf(P, "atile", [128, D], F32)
    btile = sbuf(P, "btile", [128, D], F32)
    r0 = KTc[:, 0:D]
    r1b = KTc[:, D:2 * D]
    xtT = btile[:, 0:128]
    gidx_sb = sbuf(P, "gidx_sb", [128, NTILE * 2], I32)
    gat_sb = sbuf(P, "gat_sb", [128, NTILE * 2], F32)

    def load_w(dst, ap, rows, cols):
        nch = rows // 128
        P.dma("gpsimd", dst[:, 0:nch * cols].rearrange("p (ci j) -> p ci j", j=cols),
              ap.rearrange("(ci p) j -> p ci j", p=128), ctx.gdma)

    load_w(wA, wq_in, D, D)
    P.dma("sync", wr_sb[:, :].rearrange("p (ci j) -> p ci j", j=E),
          rwT_in.rearrange("(ci p) j -> p ci j", p=128), ctx.sdma)
    bcast_row(ctx, lnw_in, ctx.scr1, D)
    P.ins("vector", "tensor_copy", lnwbc[:, :], ctx.scr1[:, :], sem=ctx.dve)
    bcast_row(ctx, lnb_in, ctx.scr1, D)
    P.ins("vector", "tensor_copy", lnbbc[:, :], ctx.scr1[:, :], sem=ctx.dve)
    bcast_row(ctx, nw_in, ctx.scr1, D)
    P.ins("vector", "tensor_copy", nwbc[:, :], ctx.scr1[:, :], sem=ctx.dve)
    bcast_row(ctx, rb_in, ctx.scr1, E)
    P.ins("vector", "tensor_copy", rbbc[:, :], ctx.scr1[:, 0:E], sem=ctx.dve)
    if FUSE["on"]:
        rsel_in = din(P, "rowsel", [128, NTILE], I32)
        gidxp_in = din(P, "gidxp", [TG, 16], I32)
        gatep_in = din(P, "gatep", [TG, 16], F32)
        rsel_sb = sbuf(P, "rsel_sb", [128, NTILE], I32)
        gtmp_i = sbuf(P, "gtmp_i", [128, 16], I32)
        gtmp_f = sbuf(P, "gtmp_f", [128, 16], F32)
        P.dma("sync", rsel_sb[:, :], rsel_in, ctx.sdma)
        P.wait("gpsimd", ctx.sdma)
        brg = P.breg(TG - 1)
        for tt in range(NTILE):
            P.wait("gpsimd", ctx.dve)
            P.idma(gtmp_i[:, :], None, gidxp_in, IOOA(rsel_sb[:, tt:tt + 1], 0), ctx.gsc,
                   bounds=brg, oob_is_err=False)
            P.idma(gtmp_f[:, :], None, gatep_in, IOOA(rsel_sb[:, tt:tt + 1], 0), ctx.gsc,
                   bounds=brg, oob_is_err=False)
            P.wait("vector", ctx.gsc)
            P.ins("vector", "tensor_copy", gidx_sb[:, 2 * tt:2 * tt + 2], gtmp_i[:, 0:2], sem=ctx.dve)
            P.ins("vector", "tensor_copy", gat_sb[:, 2 * tt:2 * tt + 2], gtmp_f[:, 0:2], sem=ctx.dve)
    else:
        P.dma("sync", gidx_sb[:, :].rearrange("p (tt k) -> p tt k", k=2),
              gi_in.rearrange("(tt p) k -> p tt k", p=128), ctx.sdma)
        P.dma("sync", gat_sb[:, :].rearrange("p (tt k) -> p tt k", k=2),
              ga_in.rearrange("(tt p) k -> p tt k", p=128), ctx.sdma)

    if upto == "bc":
        P.wait("gpsimd", ctx.sdma)
        P.wait("gpsimd", ctx.gdma)
        P.finalize()
        return P
    t32_pc = atile[0:32, 0:512]
    th_bc = sbuf(P, "th_bc", [32, 1], F32)
    P.dma("sync", th_bc[0:1, 0:1], th_in, ctx.sdma)
    P.wait("tensor", ctx.sdma)
    P.wait("tensor", ctx.gdma)
    ps = ctx.psum[6]
    P.wait("tensor", ctx.dve)
    P.mm(ps[0:32, 0:1], ctx.C["ONES_R"][0:1, 0:32], th_bc[0:1, 0:1], True, True, sem=ctx.pe)
    P.wait("vector", ctx.pe)
    P.ins("vector", "tensor_copy", th_bc[0:32, 0:1], ps[0:32, 0:1], sem=ctx.dve)
    P.ins("vector", "tensor_mul", th_bc[0:32, 0:1], th_bc[0:32, 0:1], ctx.C["MELB"][0:32, 0:1], sem=ctx.dve)
    TWO_PI = float(2 * np.pi)
    C2PI_HI = float(np.float32(TWO_PI))
    C2PI_LO = float(TWO_PI - np.float64(np.float32(TWO_PI)))
    ang_pc = ctx.scr1[0:32, 0:512]
    kf_pc = ctx.scr1[0:32, 512:1024]
    for qt in range(NQT):
        P.wait("sync", ctx.pe)
        P.wait("sync", ctx.dve)
        P.dma("sync", t32_pc[0:32, :], t32_in[:, qt * 512:(qt + 1) * 512], ctx.sdma)
        P.wait("vector", ctx.sdma)
        for trig in range(2):
            P.wait("vector", ctx.pe)
            P.ins("vector", "tensor_scalar", ang_pc, t32_pc[0:32, :], th_bc[0:32, 0:1], None, OP.mult,
                  sem=ctx.dve)
            if trig == 1:
                P.ins("vector", "tensor_scalar", ang_pc, ang_pc, float(np.pi / 2), None, OP.add, sem=ctx.dve)
            P.ins("vector", "tensor_scalar", kf_pc, ang_pc, 1.0 / TWO_PI, None, OP.mult, sem=ctx.dve)
            MAGIC = float(1.5 * 2 ** 23)
            P.ins("vector", "tensor_scalar", kf_pc, kf_pc, MAGIC, None, OP.add, sem=ctx.dve)
            P.ins("vector", "tensor_scalar", kf_pc, kf_pc, MAGIC, None, OP.subtract, sem=ctx.dve)
            P.ins("vector", "scalar_tensor_tensor", ang_pc, kf_pc, -C2PI_HI, ang_pc, OP.mult, OP.add,
                  sem=ctx.dve)
            P.ins("vector", "scalar_tensor_tensor", ang_pc, kf_pc, -C2PI_LO, ang_pc, OP.mult, OP.add,
                  sem=ctx.dve)
            P.ins("vector", "tensor_scalar", kf_pc, ang_pc, float(np.pi), None, OP.is_gt, sem=ctx.dve)
            P.ins("vector", "scalar_tensor_tensor", ang_pc, kf_pc, -C2PI_HI, ang_pc, OP.mult, OP.add,
                  sem=ctx.dve)
            P.ins("vector", "tensor_scalar", kf_pc, ang_pc, float(-np.pi), None, OP.is_lt, sem=ctx.dve)
            P.ins("vector", "scalar_tensor_tensor", ang_pc, kf_pc, C2PI_HI, ang_pc, OP.mult, OP.add,
                  sem=ctx.dve)
            P.wait("scalar", ctx.dve)
            P.ins("scalar", "activation", ang_pc, ang_pc, AF.Sin, sem=ctx.act)
            P.wait("tensor", ctx.act)
            P.wait("tensor", ctx.dve)
            pst = ctx.psum[5 + trig]
            MAPN = "SMAP" if trig == 0 else "CMAP"
            P.mm(pst[:, :], ctx.C[MAPN][0:32, :], ang_pc, True, True, sem=ctx.pe)
            P.wait("scalar", ctx.pe)
            dstt = (SN_cur if trig == 0 else CS_cur)[:, qt * 512:(qt + 1) * 512]
            P.ins("scalar", "activation", dstt, pst[:, :], AF.Copy, sem=ctx.act)
    if upto == "tab":
        P.wait("gpsimd", ctx.sdma)
        P.wait("gpsimd", ctx.adma)
        P.finalize()
        return P
    breg = P.breg(5120 - 1)
    def issue_gather(tt):
        P.ins("vector", "memset", r0[:, :], 0.0, sem=ctx.dve)
        P.ins("vector", "memset", r1b[:, :], 0.0, sem=ctx.dve)
        P.wait("gpsimd", ctx.dve)
        P.wait("gpsimd", ctx.sdma)
        P.idma(r0[:, :], None, oa_in, IOOA(gidx_sb[:, 2 * tt:2 * tt + 1], 0), ctx.gsc, bounds=breg,
               oob_is_err=False)
        P.idma(r1b[:, :], None, oa_in, IOOA(gidx_sb[:, 2 * tt + 1:2 * tt + 2], 0), ctx.gsc, bounds=breg,
               oob_is_err=False)
    stg_xnn = V_sb[:, 0:D]
    stg_ln = V_sb[:, D:2 * D]
    P.wait("vector", ctx.gsc)
    issue_gather(0)
    for tt in range(NTILE):
        P.wait("sync", ctx.dve)
        P.wait("sync", ctx.pe)
        P.dma("sync", atile[:, :], x_in[tt * 128:(tt + 1) * 128, :], ctx.sdma)
        P.dma("sync", btile[:, :], s0_in[tt * 128:(tt + 1) * 128, :], ctx.sdma)
        P.wait("vector", ctx.gsc)
        P.wait("vector", ctx.sdma)
        P.ins("vector", "tensor_scalar", ctx.scr1[:, :], r0[:, :], gat_sb[:, 2 * tt:2 * tt + 1], None,
              OP.mult, sem=ctx.dve)
        P.ins("vector", "scalar_tensor_tensor", ctx.scr1[:, :], r1b[:, :], gat_sb[:, 2 * tt + 1:2 * tt + 2],
              ctx.scr1[:, :], OP.mult, OP.add, sem=ctx.dve)
        if tt + 1 < NTILE:
            issue_gather(tt + 1)
        P.wait("scalar", ctx.dve)
        P.ins("scalar", "activation", ctx.scr1[:, :], ctx.scr1[:, :], AF.Silu, sem=ctx.act)
        P.wait("vector", ctx.act)
        P.ins("vector", "tensor_mul", btile[:, :], btile[:, :], ctx.scr1[:, :], sem=ctx.dve)
        P.ins("vector", "tensor_add", atile[:, :], atile[:, :], btile[:, :], sem=ctx.dve)
        P.wait("sync", ctx.dve)
        P.dma("sync", o_x1[tt * 128:(tt + 1) * 128, :], atile[:, :], ctx.sdma)
        P.wait("vector", ctx.sdma)
        st = ctx.stat
        P.ins("vector", "reduce_sum", st[:, 0:1], atile[:, :], axis=AX.X, sem=ctx.dve)
        P.ins("vector", "tensor_scalar", st[:, 0:1], st[:, 0:1], 1.0 / D, None, OP.mult, sem=ctx.dve)
        P.ins("vector", "tensor_scalar", atile[:, :], atile[:, :], st[:, 0:1], None, OP.subtract, sem=ctx.dve)
        P.wait("scalar", ctx.dve)
        P.ins("scalar", "activation", btile[:, :], atile[:, :], AF.Square, accum_out=st[:, 1:2], sem=ctx.act)
        P.wait("vector", ctx.act)
        P.ins("vector", "tensor_scalar", st[:, 1:2], st[:, 1:2], 1.0 / D, None, OP.mult, sem=ctx.dve)
        P.wait("scalar", ctx.dve)
        P.ins("scalar", "activation", st[:, 2:3], st[:, 1:2], AF.Ln, bias=ctx.bias(EPS1), sem=ctx.act)
        P.ins("scalar", "activation", st[:, 2:3], st[:, 2:3], AF.Exp, scale=-0.5, sem=ctx.act)
        P.wait("vector", ctx.act)
        P.ins("vector", "tensor_scalar", atile[:, :], atile[:, :], st[:, 2:3], None, OP.mult, sem=ctx.dve)
        P.wait("scalar", ctx.dve)
        P.ins("scalar", "activation", btile[:, :], atile[:, :], AF.Square, accum_out=st[:, 1:2], sem=ctx.act)
        P.wait("vector", ctx.act)
        P.ins("vector", "tensor_scalar", st[:, 1:2], st[:, 1:2], 1.0 / D, None, OP.mult, sem=ctx.dve)
        P.wait("scalar", ctx.dve)
        P.ins("scalar", "activation", st[:, 3:4], st[:, 1:2], AF.Ln, bias=ctx.bias(EPS1), sem=ctx.act)
        P.ins("scalar", "activation", st[:, 3:4], st[:, 3:4], AF.Exp, scale=-0.5, sem=ctx.act)
        P.wait("vector", ctx.act)
        P.ins("vector", "tensor_scalar", ctx.scr1[:, :], atile[:, :], st[:, 3:4], None, OP.mult, sem=ctx.dve)
        P.ins("vector", "tensor_mul", ctx.scr1[:, :], ctx.scr1[:, :], nwbc[:, :], sem=ctx.dve)
        P.ins("vector", "tensor_copy", stg_xnn, ctx.scr1[:, :], sem=ctx.dve)
        P.wait("sync", ctx.dve)
        P.dma("sync", o_xnn[tt * 128:(tt + 1) * 128, :], stg_xnn, ctx.sdma)
        P.wait("tensor", ctx.dve)
        P.wait("tensor", ctx.sdma)
        P.wait("vector", ctx.sdma)
        P.ins("vector", "tensor_copy", lgacc[:, 0:E], rbbc[:, 0:E], sem=ctx.dve)
        for cc in range(NCHUNK):
            pst = ctx.psum[4]
            P.wait("tensor", ctx.act)
            P.wait("tensor", ctx.dve)
            P.ins("tensor", "transpose", pst[:, 0:128], ctx.scr1[:, cc * 128:(cc + 1) * 128],
                  ctx.C["IDN"][:, :], sem=ctx.pe)
            P.wait("scalar", ctx.pe)
            P.ins("scalar", "activation", xtT[:, :], pst[:, 0:128], AF.Copy, sem=ctx.act)
            P.wait("tensor", ctx.act)
            P.mm(ctx.psum[5][0:128, 0:E], xtT[:, :], wr_sb[:, cc * E:(cc + 1) * E],
                 True, True, sem=ctx.pe)
            P.wait("vector", ctx.pe)
            P.ins("vector", "tensor_add", lgacc[:, 0:E], lgacc[:, 0:E],
                  ctx.psum[5][0:128, 0:E], sem=ctx.dve)
        P.wait("sync", ctx.dve)
        P.dma("sync", o_lg[tt * 128:(tt + 1) * 128, :], lgacc[:, 0:E], ctx.sdma)
        P.ins("vector", "reduce_sum", st[:, 0:1], atile[:, :], axis=AX.X, sem=ctx.dve)
        P.ins("vector", "tensor_scalar", st[:, 0:1], st[:, 0:1], 1.0 / D, None, OP.mult, sem=ctx.dve)
        P.ins("vector", "tensor_scalar", atile[:, :], atile[:, :], st[:, 0:1], None, OP.subtract, sem=ctx.dve)
        P.wait("scalar", ctx.dve)
        P.wait("scalar", ctx.pe)
        P.ins("scalar", "activation", btile[:, :], atile[:, :], AF.Square, accum_out=st[:, 1:2], sem=ctx.act)
        P.wait("vector", ctx.act)
        P.ins("vector", "tensor_scalar", st[:, 1:2], st[:, 1:2], 1.0 / D, None, OP.mult, sem=ctx.dve)
        P.wait("scalar", ctx.dve)
        P.ins("scalar", "activation", st[:, 2:3], st[:, 1:2], AF.Ln, bias=ctx.bias(EPS1), sem=ctx.act)
        P.ins("scalar", "activation", st[:, 2:3], st[:, 2:3], AF.Exp, scale=-0.5, sem=ctx.act)
        P.wait("vector", ctx.act)
        P.ins("vector", "tensor_scalar", atile[:, :], atile[:, :], st[:, 2:3], None, OP.mult, sem=ctx.dve)
        P.ins("vector", "tensor_mul", atile[:, :], atile[:, :], lnwbc[:, :], sem=ctx.dve)
        P.ins("vector", "tensor_add", atile[:, :], atile[:, :], lnbbc[:, :], sem=ctx.dve)
        P.ins("vector", "tensor_copy", stg_ln, atile[:, :], sem=ctx.dve)
        P.wait("sync", ctx.dve)
        P.dma("sync", tsc[tt * 128:(tt + 1) * 128, :], stg_ln, ctx.sdma)
        P.wait("vector", ctx.sdma)
    if upto == "comb":
        P.wait("gpsimd", ctx.sdma)
        P.wait("gpsimd", ctx.adma)
        P.finalize()
        return P
    P.wait("scalar", ctx.sdma)
    for c in range(NCHUNK):
        P.dma("scalar", xT[:, c * S:(c + 1) * S], tsc[:, c * 128:(c + 1) * 128], ctx.adma, transpose=True)
    P.wait("tensor", ctx.adma)
    if upto == "logits":
        P.wait("gpsimd", ctx.sdma)
        P.wait("gpsimd", ctx.adma)
        P.finalize()
        return P
    P.wait("vector", ctx.sdma)
    P.wait("vector", ctx.adma)
    ctx.C["p_sb"] = btile[:, :].bitcast(BF16)[:, 0:512]
    ctx.C["p_sb2"] = btile[:, :].bitcast(BF16)[:, 1024:1536]
    ctx.C["zt"] = btile[:, :].bitcast(BF16)[:, 512:1024]
    P.ins("vector", "memset", ctx.C["zt"], 0.0, sem=ctx.dve)
    proj_fmajor_rope(ctx, xT, NCHUNK, wA, D, list(range(NCHUNK)), QT, CS_cur, SN_cur, False)
    P.wait("gpsimd", ctx.pe)
    load_w(wB, wv_in, D, D)
    P.wait("vector", ctx.gdma)
    P.ins("vector", "memset", V_sb[:, :], 1.0, sem=ctx.dve)
    P.wait("tensor", ctx.gdma)
    for kt in range(NTILE):
        for vh in range(2):
            psv = ctx.psum[1]
            P.wait("tensor", ctx.act)
            P.wait("tensor", ctx.dve)
            for ci in range(NCHUNK):
                P.mm(psv[:, :], xT[:, ci * S + kt * 128: ci * S + kt * 128 + 128],
                     wB[:, ci * D + vh * 512: ci * D + vh * 512 + 512], ci == 0, ci == NCHUNK - 1,
                     sem=ctx.pe)
            P.wait("vector", ctx.pe)
            dst = V_sb[:, kt * 1040 + vh * 8 * 65: kt * 1040 + (vh + 1) * 8 * 65].rearrange(
                "p (s j) -> p s j", j=65)[:, :, 0:64]
            P.ins("vector", "tensor_copy", dst, psv[:, :].rearrange("p (s j) -> p s j", j=64), sem=ctx.dve)
    P.wait("gpsimd", ctx.pe)
    load_w(wA, wk_in, D, D)
    if FUSE["on"] and FUSE.get("cc_early"):
        FUSE["cc_early"]()
        FUSE["cc_early"] = None
    P.wait("tensor", ctx.gdma)
    P.wait("scalar", ctx.sdma)
    P.wait("scalar", ctx.gsc)
    P.wait("scalar", ctx.dve)
    for c in range(NCHUNK):
        proj_fmajor_rope(ctx, xT, NCHUNK, wA, D, [c], KTc, CS_cur, SN_cur, False, out_stride=S)
        heads = {2 * c: 2 * c, 2 * c + 1: 2 * c + 1}
        attn_core(ctx, QT, lambda cc: KTc[:, :], V_sb, 16, OT, heads, "taylor", band_full, chunks=[c])
    P.wait("gpsimd", ctx.pe)
    load_w(wB, wo_in, D, D)
    P.wait("tensor", ctx.gdma)
    for tt in range(NTILE):
        for half in range(2):
            ps = ctx.psum[2 + half]
            P.wait("tensor", ctx.dve)
            for ci in range(NCHUNK):
                P.mm(ps[:, :], OT[:, ci * S + tt * 128: ci * S + tt * 128 + 128],
                     wB[:, ci * D + half * 512: ci * D + half * 512 + 512], ci == 0, ci == NCHUNK - 1,
                     sem=ctx.pe)
            P.wait("vector", ctx.pe)
            P.ins("vector", "tensor_copy", atile[:, half * 512:(half + 1) * 512], ps[:, :], sem=ctx.dve)
        P.wait("sync", ctx.dve)
        P.dma("sync", o_s1[tt * 128:(tt + 1) * 128, :], atile[:, :], ctx.sdma)
        P.wait("vector", ctx.sdma)
    P.wait("gpsimd", ctx.sdma)
    P.wait("gpsimd", ctx.adma)
    P.finalize()
    return P

L4_CONSTS = ["TRIU_INC", "TRIU_EXCF", "ONES_R", "ONES_C", "ONES_CF", "IOTA8R"]
NT2 = 16384 // 128
BIS_IT = 50

def build_l1f():
    P = new_prog()
    ctx = Ctx(P, rstd=False)
    lg_in = din(P, "logits_a", [16384, E], F32)
    xn_in = din(P, "xnn_a", [16384, D], BF16)
    f1T_in = din(P, "fc1T", [D, FF], F32)
    f1b_in = din(P, "fc1b", [128, 32], F32)
    f2T_in = din(P, "fc2T", [FF, D], F32)
    f2b_in = din(P, "fc2b", [1, D], F32)
    meta_in = din(P, "meta4", [128, 1], F32)
    o_ye = dout(P, "ye", [S, D], BF16)
    o_thr = dout(P, "thr", [2, E], F32)
    xe_dram = dtmp(P, "xe_dram", [S, 1040], BF16)
    bscr = dtmp(P, "bscr", [1, 1024], F32)
    ctx.loadc(L4_CONSTS)
    lgt = sbuf(P, "lgt", [128, NT2 * E], F32)
    prb = sbuf(P, "prb", [128, NT2 * E], F32)
    mscr = sbuf(P, "mscr", [128, NT2 * E], F32)
    selb = sbuf(P, "selb", [128, NT2], BF16)
    lo = sbuf(P, "lo", [128, E], F32)
    hi = sbuf(P, "hi", [128, E], F32)
    mid = sbuf(P, "mid", [128, E], F32)
    cnt8 = sbuf(P, "cnt8", [128, E], F32)
    meta = sbuf(P, "meta_sb", [128, 1], F32)
    pe_ = sbuf(P, "p_e", [128, NT2], F32)
    re_ = sbuf(P, "r_e", [128, NT2], F32)
    ve_ = sbuf(P, "v_e", [128, NT2], F32)
    didx = sbuf(P, "didx", [128, NT2], I32)
    rowa = sbuf(P, "rowa", [1, 1024], F32)
    t128 = sbuf(P, "t128", [128, 1], F32)
    stg = [sbuf(P, f"stg{i}", [128, 1040], BF16) for i in range(2)]
    zt = sbuf(P, "ztile", [128, 1040], BF16)
    xeT = sbuf(P, "xeT", [128, NCHUNK * 512], BF16)
    w1_sl = [sbuf(P, f"w1_sl{i}", [128, NCHUNK * 512], BF16) for i in range(2)]
    w2_sl = [sbuf(P, f"w2_sl{i}", [128, 4 * D], BF16) for i in range(2)]
    f1b_sb = sbuf(P, "f1b_sb", [128, 32], F32)
    f2bbc = sbuf(P, "f2bbc", [128, D], F32)
    hmT = sbuf(P, "hmT", [128, 4 * 512], BF16)
    ye_sb = sbuf(P, "ye_sb", [128, 4 * D], F32)
    gats = sbuf(P, "gats", [128, 16], F32)
    outb = sbuf(P, "outb", [128, D], BF16)
    L3v = lambda t: t.rearrange("p (tt e) -> p tt e", e=E)

    P.dma("sync", lgt[:, :].rearrange("p (tt e) -> p tt e", e=E),
          lg_in.rearrange("(tt p) e -> p tt e", p=128), ctx.sdma)
    P.dma("sync", meta[:, :], meta_in, ctx.sdma)
    P.dma("gpsimd", f1b_sb[:, :], f1b_in, ctx.gdma)
    bcast_row(ctx, f2b_in, f2bbc, D)
    P.ins("vector", "memset", zt[:, :], 0.0, sem=ctx.dve)
    P.wait("gpsimd", ctx.dve)
    for i in range(16):
        P.dma("gpsimd", xe_dram[i * 128:(i + 1) * 128, :], zt[:, :], ctx.gdma)
    P.wait("vector", ctx.sdma)
    P.ins("vector", "tensor_reduce", mscr[:, 0:NT2], L3v(lgt), axis=AX.X, op=OP.max, sem=ctx.dve)
    P.ins("vector", "tensor_tensor", L3v(prb), L3v(lgt),
          mscr[:, 0:NT2].unsqueeze(2).to_broadcast([128, NT2, E]), OP.subtract, sem=ctx.dve)
    P.wait("scalar", ctx.dve)
    P.ins("scalar", "activation", prb[:, :], prb[:, :], AF.Exp, sem=ctx.act)
    P.wait("vector", ctx.act)
    P.ins("vector", "tensor_reduce", mscr[:, 0:NT2], L3v(prb), axis=AX.X, op=OP.add, sem=ctx.dve)
    P.ins("vector", "reciprocal", mscr[:, NT2:2 * NT2], mscr[:, 0:NT2], sem=ctx.dve)
    P.ins("vector", "tensor_tensor", L3v(prb), L3v(prb),
          mscr[:, NT2:2 * NT2].unsqueeze(2).to_broadcast([128, NT2, E]), OP.mult, sem=ctx.dve)
    P.ins("vector", "memset", lo[:, :], 0.0, sem=ctx.dve)
    P.ins("vector", "memset", hi[:, :], 1.0, sem=ctx.dve)
    for it in range(BIS_IT):
        P.ins("vector", "tensor_add", mid[:, :], lo[:, :], hi[:, :], sem=ctx.dve)
        P.ins("vector", "tensor_scalar", mid[:, :], mid[:, :], 0.5, None, OP.mult, sem=ctx.dve)
        P.ins("vector", "tensor_tensor", L3v(mscr), L3v(prb),
              mid[:, :].unsqueeze(1).to_broadcast([128, NT2, E]), OP.is_ge, sem=ctx.dve)
        P.ins("vector", "tensor_reduce", L3v(mscr)[:, :, 0:E].rearrange("p t e -> p (t e)") if False else mscr[:, :],
              L3v(mscr), axis=AX.X, op=OP.add, sem=ctx.dve) if False else None
        P.wait("tensor", ctx.dve)
        ps = ctx.psum[0]
        P.mm(ps[0:1, 0:NT2 * E], ctx.C["ONES_CF"][:, 0:1], mscr[:, :], True, True, sem=ctx.pe) if False else None
        P.mm(ps[0:1, 0:512], ctx.C["ONES_CF"][:, 0:1], mscr[:, 0:512], True, True, sem=ctx.pe)
        P.mm(ctx.psum[1][0:1, 0:512], ctx.C["ONES_CF"][:, 0:1], mscr[:, 512:1024], True, True, sem=ctx.pe)
        P.wait("vector", ctx.pe)
        P.ins("vector", "tensor_copy", rowa[0:1, 0:512], ps[0:1, 0:512], sem=ctx.dve)
        P.ins("vector", "tensor_copy", rowa[0:1, 512:1024], ctx.psum[1][0:1, 0:512], sem=ctx.dve)
        P.ins("vector", "tensor_reduce", rowa[0:1, 0:E],
              rowa[0:1, :].rearrange("a (tt e) -> a e tt", e=E), axis=AX.X, op=OP.add, sem=ctx.dve)
        P.wait("tensor", ctx.dve)
        P.mm(ctx.psum[2][0:128, 0:E], ctx.C["ONES_R"][0:1, :], rowa[0:1, 0:E], True, True, sem=ctx.pe)
        P.wait("vector", ctx.pe)
        P.ins("vector", "tensor_copy", cnt8[:, :], ctx.psum[2][0:128, 0:E], sem=ctx.dve)
        P.ins("vector", "tensor_scalar", cnt8[:, :], cnt8[:, :], 2048.0, None, OP.is_ge, sem=ctx.dve)
        P.ins("vector", "tensor_sub", mid[:, :], mid[:, :], lo[:, :], sem=ctx.dve)
        P.ins("vector", "tensor_tensor", mscr[:, 0:E], mid[:, :], cnt8[:, :], OP.mult, sem=ctx.dve)
        P.ins("vector", "tensor_add", lo[:, :], lo[:, :], mscr[:, 0:E], sem=ctx.dve)
        P.ins("vector", "tensor_add", mid[:, :], mid[:, :], lo[:, :], sem=ctx.dve) if False else None
        P.ins("vector", "tensor_scalar", cnt8[:, :], cnt8[:, :], -1.0, None, OP.mult, sem=ctx.dve)
        P.ins("vector", "tensor_scalar", cnt8[:, :], cnt8[:, :], 1.0, None, OP.add, sem=ctx.dve)
        P.ins("vector", "tensor_tensor", mscr[:, 0:E], mid[:, :], cnt8[:, :], OP.mult, sem=ctx.dve)
        P.ins("vector", "tensor_tensor", mscr[:, E:2 * E], lo[:, :], cnt8[:, :], OP.mult, sem=ctx.dve)
        P.ins("vector", "tensor_add", mscr[:, 0:E], mscr[:, 0:E], mscr[:, E:2 * E], sem=ctx.dve)
        P.ins("vector", "tensor_scalar", cnt8[:, :], cnt8[:, :], -1.0, None, OP.mult, sem=ctx.dve)
        P.ins("vector", "tensor_scalar", cnt8[:, :], cnt8[:, :], 1.0, None, OP.add, sem=ctx.dve)
        P.ins("vector", "tensor_tensor", hi[:, :], hi[:, :], cnt8[:, :], OP.mult, sem=ctx.dve)
        P.ins("vector", "tensor_add", hi[:, :], hi[:, :], mscr[:, 0:E], sem=ctx.dve) if False else None
        P.ins("vector", "tensor_scalar", cnt8[:, :], cnt8[:, :], -1.0, None, OP.mult, sem=ctx.dve)
        P.ins("vector", "tensor_scalar", cnt8[:, :], cnt8[:, :], 1.0, None, OP.add, sem=ctx.dve)
        P.ins("vector", "tensor_tensor", mscr[:, 0:E], mscr[:, 0:E], cnt8[:, :], OP.mult, sem=ctx.dve)
        P.ins("vector", "tensor_add", hi[:, :], hi[:, :], mscr[:, 0:E], sem=ctx.dve)
    P.wait("sync", ctx.dve)
    P.dma("sync", o_thr[0:1, :], lo[0:1, :], ctx.sdma)
    P.dma("sync", o_thr[1:2, :], hi[0:1, :], ctx.sdma)
    onehot = cnt8
    P.ins("vector", "tensor_scalar", onehot[:, :], ctx.C["IOTA8R"][:, 0:E], meta[:, 0:1], None,
          OP.is_equal, sem=ctx.dve)
    def col_extract(dst, src3):
        P.ins("vector", "tensor_tensor", L3v(mscr), src3,
              onehot[:, :].unsqueeze(1).to_broadcast([128, NT2, E]), OP.mult, sem=ctx.dve)
        P.ins("vector", "tensor_reduce", dst, L3v(mscr), axis=AX.X, op=OP.add, sem=ctx.dve)
    col_extract(pe_[:, :], L3v(prb))
    P.ins("vector", "tensor_tensor", mscr[:, 0:E], lo[:, :], onehot[:, :], OP.mult, sem=ctx.dve)
    P.ins("vector", "tensor_reduce", t128[:, 0:1], mscr[:, 0:E], axis=AX.X, op=OP.add, sem=ctx.dve)
    P.ins("vector", "tensor_scalar", re_[:, 0:1], t128[:, 0:1], 0.0, None, OP.add, sem=ctx.dve)
    P.ins("vector", "tensor_tensor", mscr[:, 0:E], hi[:, :], onehot[:, :], OP.mult, sem=ctx.dve)
    P.ins("vector", "tensor_reduce", t128[:, 0:1], mscr[:, 0:E], axis=AX.X, op=OP.add, sem=ctx.dve)
    P.ins("vector", "tensor_scalar", ve_[:, :], pe_[:, :], t128[:, 0:1], None, OP.is_ge, sem=ctx.dve)
    P.ins("vector", "tensor_scalar", mscr[:, 0:NT2], pe_[:, :], re_[:, 0:1], None, OP.is_ge, sem=ctx.dve)
    P.ins("vector", "tensor_scalar", mscr[:, NT2:2 * NT2], pe_[:, :], t128[:, 0:1], None, OP.is_lt, sem=ctx.dve)
    P.ins("vector", "tensor_tensor", mscr[:, 0:NT2], mscr[:, 0:NT2], mscr[:, NT2:2 * NT2], OP.mult,
          sem=ctx.dve)
    P.ins("vector", "tensor_copy", selb[:, :], ve_[:, :], sem=ctx.dve)
    P.wait("tensor", ctx.dve)
    P.mm(ctx.psum[0][0:1, 0:NT2], ctx.C["ONES_CF"][:, 0:1], ve_[:, :], True, True, sem=ctx.pe)
    P.wait("vector", ctx.pe)
    P.ins("vector", "tensor_reduce", rowa[0:1, 0:1], ctx.psum[0][0:1, 0:NT2], axis=AX.X, op=OP.add,
          sem=ctx.dve)
    P.ins("vector", "tensor_scalar", rowa[0:1, 0:1], rowa[0:1, 0:1], -1.0, None, OP.mult, sem=ctx.dve)
    P.ins("vector", "tensor_scalar", rowa[0:1, 0:1], rowa[0:1, 0:1], 2048.0, None, OP.add, sem=ctx.dve)
    P.wait("tensor", ctx.dve)
    P.mm(ctx.psum[2][0:128, 0:1], ctx.C["ONES_R"][0:1, :], rowa[0:1, 0:1], True, True, sem=ctx.pe)
    P.wait("vector", ctx.pe)
    P.ins("vector", "tensor_copy", t128[:, 0:1], ctx.psum[2][0:128, 0:1], sem=ctx.dve)
    def gcum(src_col_sb, out_sb):
        P.ins("vector", "tensor_copy", selb[:, :], src_col_sb, sem=ctx.dve)
        P.wait("tensor", ctx.dve)
        P.mm(ctx.psum[0][0:1, 0:NT2], ctx.C["ONES_CF"][:, 0:1], src_col_sb, True, True, sem=ctx.pe)
        P.wait("vector", ctx.pe)
        P.ins("vector", "tensor_copy", rowa[0:1, 0:NT2], ctx.psum[0][0:1, 0:NT2], sem=ctx.dve)
        P.wait("sync", ctx.dve)
        P.dma("sync", bscr[0:1, 0:NT2], rowa[0:1, 0:NT2], ctx.sdma)
        P.wait("sync", ctx.sdma)
        P.dma("sync", ctx.stat[:, 4:5], bscr[0:1, 0:NT2].rearrange("a (t b) -> (a t) b", b=1), ctx.sdma)
        P.wait("tensor", ctx.sdma)
        P.mm(ctx.psum[1][0:128, 0:1], ctx.C["TRIU_EXCF"][:, :], ctx.stat[:, 4:5], True, True, sem=ctx.pe)
        P.wait("vector", ctx.pe)
        P.ins("vector", "tensor_copy", ctx.stat[:, 5:6], ctx.psum[1][0:128, 0:1], sem=ctx.dve)
        P.wait("sync", ctx.dve)
        P.dma("sync", bscr[0:1, 0:NT2].rearrange("a (t b) -> (a t) b", b=1), ctx.stat[:, 5:6], ctx.sdma)
        P.wait("sync", ctx.sdma)
        P.dma("sync", rowa[0:1, 0:NT2], bscr[0:1, 0:NT2], ctx.sdma)
        P.wait("tensor", ctx.sdma)
        P.wait("tensor", ctx.dve)
        P.mm(ctx.psum[3][:, 0:NT2], ctx.C["TRIU_INC"][:, :], selb[:, :], True, False, sem=ctx.pe)
        P.mm(ctx.psum[3][:, 0:NT2], ctx.C["ONES_R"][0:1, :], rowa[0:1, 0:NT2], False, True, sem=ctx.pe)
        P.wait("vector", ctx.pe)
        P.ins("vector", "tensor_copy", out_sb, ctx.psum[3][:, 0:NT2], sem=ctx.dve)
    gcum(mscr[:, 0:NT2], mscr[:, NT2:2 * NT2])
    P.ins("vector", "tensor_scalar", mscr[:, NT2:2 * NT2], mscr[:, NT2:2 * NT2], t128[:, 0:1], None,
          OP.is_le, sem=ctx.dve)
    P.ins("vector", "tensor_tensor", mscr[:, 0:NT2], mscr[:, 0:NT2], mscr[:, NT2:2 * NT2], OP.mult,
          sem=ctx.dve)
    P.ins("vector", "tensor_add", ve_[:, :], ve_[:, :], mscr[:, 0:NT2], sem=ctx.dve)
    gcum(ve_[:, :], re_[:, :])
    P.ins("vector", "tensor_scalar", re_[:, :], re_[:, :], 1.0, None, OP.subtract, sem=ctx.dve)
    P.ins("vector", "tensor_scalar", mscr[:, 0:NT2], re_[:, :], float(OOB), None, OP.subtract, sem=ctx.dve)
    P.ins("vector", "tensor_tensor", mscr[:, 0:NT2], mscr[:, 0:NT2], ve_[:, :], OP.mult, sem=ctx.dve)
    P.ins("vector", "tensor_scalar", mscr[:, 0:NT2], mscr[:, 0:NT2], float(OOB), None, OP.add, sem=ctx.dve)
    P.ins("vector", "tensor_copy", didx[:, :], mscr[:, 0:NT2], sem=ctx.dve)
    breg = P.breg(S - 1)
    P.ins("vector", "memset", stg[0][:, 1024:1040], 0.0, sem=ctx.dve)
    P.ins("vector", "memset", stg[1][:, 1024:1040], 0.0, sem=ctx.dve)
    P.wait("gpsimd", ctx.dve)
    g0 = ctx.gsc.n
    for tt in range(NT2):
        par = tt % 2
        if tt >= 2:
            P.wait("sync", ctx.gsc, g0 + 16 * (tt - 1))
        P.dma("sync", stg[par][:, 0:1024], xn_in[tt * 128:(tt + 1) * 128, :], ctx.sdma)
        P.wait("vector", ctx.sdma)
        P.wait("vector", ctx.gsc, g0 + 16 * max(0, tt - 1))
        P.ins("vector", "tensor_copy", stg[par][:, 1024:1025], pe_[:, tt:tt + 1], sem=ctx.dve)
        P.wait("gpsimd", ctx.sdma)
        P.wait("gpsimd", ctx.dve)
        P.idma(xe_dram, IOOA(didx[:, tt:tt + 1], 0), stg[par][:, :], None, ctx.gsc, bounds=breg,
               oob_is_err=False)
    P.wait("scalar", ctx.gsc)
    P.wait("gpsimd", ctx.gsc)
    P.dma("gpsimd", gats[:, :].rearrange("p (rt a) -> p rt a", a=1),
          xe_dram.rearrange("(rt p) j -> p rt j", p=128)[:, :, 1024:1025], ctx.gdma,
          allow_slow_non_contiguous=True)
    for rg in range(4):
        P.wait("scalar", ctx.pe)
        for hc in range(NCHUNK):
            P.dma("scalar", xeT[:, hc * 512:(hc + 1) * 512],
                  xe_dram[rg * 512:(rg + 1) * 512, hc * 128:(hc + 1) * 128], ctx.adma, transpose=True)
        P.wait("vector", ctx.gdma)
        P.wait("vector", ctx.pe)
        for rsub in range(4):
            P.ins("vector", "tensor_copy", ye_sb[:, rsub * D:(rsub + 1) * D], f2bbc[:, :], sem=ctx.dve)
        P.wait("tensor", ctx.adma)
        for fs in range(NCHUNK):
            par = fs % 2
            if fs >= 2 or rg > 0:
                P.wait("gpsimd", ctx.pe)
            P.dma("gpsimd", w1_sl[par][:, :].rearrange("p (hc f) -> p hc f", f=512),
                  f1T_in[:, fs * 512:(fs + 1) * 512].rearrange("(hc p) f -> p hc f", p=128), ctx.gdma)
            P.dma("gpsimd", w2_sl[par][:, :].rearrange("p (fc h) -> p fc h", h=D),
                  f2T_in[fs * 512:(fs + 1) * 512, :].rearrange("(fc p) h -> p fc h", p=128), ctx.gdma)
            P.wait("tensor", ctx.gdma)
            for fc in range(4):
                ps = ctx.psum[fc % 2]
                P.wait("tensor", ctx.act)
                for hcc in range(NCHUNK):
                    P.mm(ps[:, :], w1_sl[par][:, hcc * 512 + fc * 128: hcc * 512 + fc * 128 + 128],
                         xeT[:, hcc * 512:(hcc + 1) * 512], hcc == 0, hcc == NCHUNK - 1, sem=ctx.pe)
                P.wait("scalar", ctx.pe)
                P.ins("scalar", "activation", hmT[:, fc * 512:(fc + 1) * 512], ps[:, :], AF.Gelu,
                      bias=f1b_sb[:, fs * 4 + fc:fs * 4 + fc + 1], sem=ctx.act)
            for rsub in range(4):
                for half in range(2):
                    ps = ctx.psum[4 + half]
                    P.wait("tensor", ctx.act)
                    P.wait("tensor", ctx.dve)
                    for fc in range(4):
                        P.mm(ps[:, :], hmT[:, fc * 512 + rsub * 128: fc * 512 + rsub * 128 + 128],
                             w2_sl[par][:, fc * D + half * 512: fc * D + half * 512 + 512],
                             fc == 0, fc == 3, sem=ctx.pe)
                    P.wait("vector", ctx.pe)
                    sl = ye_sb[:, rsub * D + half * 512: rsub * D + half * 512 + 512]
                    P.ins("vector", "tensor_add", sl, sl, ps[:, :], sem=ctx.dve)
        for rsub in range(4):
            P.ins("vector", "tensor_scalar", outb[:, :], ye_sb[:, rsub * D:(rsub + 1) * D],
                  gats[:, rg * 4 + rsub:rg * 4 + rsub + 1], None, OP.mult, sem=ctx.dve)
            P.wait("sync", ctx.dve)
            P.dma("sync", o_ye[rg * 512 + rsub * 128: rg * 512 + (rsub + 1) * 128, :], outb[:, :], ctx.sdma)
            P.wait("vector", ctx.sdma)
    P.wait("gpsimd", ctx.sdma)
    P.wait("gpsimd", ctx.adma)
    P.finalize()
    return P

L5_CONSTS = ["TRIU_INC", "TRIU_EXCF", "ONES_R", "ONES_CF", "IOTA8R"]

def build_l1c():
    P = new_prog()
    ctx = Ctx(P, rstd=False)
    x1_in = din(P, "x1", [S, D], F32)
    s1_in = din(P, "s1", [S, D], F32)
    ye_in = din(P, "ye_all", [16384, D], BF16)
    thr_in = din(P, "thr", [2, E], F32)
    lg_in = din(P, "logits_a", [16384, E], F32)
    mylg_in = din(P, "my_lg", [S, E], F32)
    mask_in = din(P, "mymask", [128, 1], F32)
    o_out = dout(P, "out", [S, D], F32)
    bscr = dtmp(P, "bscr", [1, 1024], F32)
    ctx.loadc(L5_CONSTS)
    lgt = sbuf(P, "lgt", [128, NT2 * E], F32)
    prb = sbuf(P, "prb", [128, NT2 * E], F32)
    mscr = sbuf(P, "mscr", [128, NT2 * E], F32)
    selb = sbuf(P, "selb", [128, 128], BF16)
    thr2 = sbuf(P, "thr2", [1, E], F32)
    thr2b = sbuf(P, "thr2b", [1, E], F32)
    lo_bc = sbuf(P, "lo_bc", [128, E], F32)
    hi_bc = sbuf(P, "hi_bc", [128, E], F32)
    tge = sbuf(P, "tge", [128, E], F32)
    ttd = sbuf(P, "ttd", [128, E], F32)
    aux = sbuf(P, "aux", [128, 4 * E], F32)
    mymask = sbuf(P, "mymask_sb", [128, 1], F32)
    rowa = sbuf(P, "rowa", [1, 1024], F32)
    mprb = sbuf(P, "mprb", [128, 16 * E], F32)
    mge = sbuf(P, "mge", [128, 16 * E], F32)
    mtd = sbuf(P, "mtd", [128, 16 * E], F32)
    mrk = sbuf(P, "mrk", [128, 16 * E], F32)
    didx = sbuf(P, "didx", [128, 16 * E], I32)
    y1 = sbuf(P, "y1", [128, NTILE * D], BF16)
    gsum = sbuf(P, "gsum", [128, 16], F32)
    atile = sbuf(P, "atile", [128, D], F32)
    btile = sbuf(P, "btile", [128, D], F32)
    L3v = lambda t: t.rearrange("p (tt e) -> p tt e", e=E)
    M3 = lambda t: t.rearrange("p (tt e) -> p tt e", e=E)

    P.dma("sync", lgt[:, :].rearrange("p (tt e) -> p tt e", e=E),
          lg_in.rearrange("(tt p) e -> p tt e", p=128), ctx.sdma)
    P.dma("sync", thr2[0:1, :], thr_in[0:1, :], ctx.sdma)
    P.dma("sync", thr2b[0:1, :], thr_in[1:2, :], ctx.sdma)
    P.dma("sync", mymask[:, :], mask_in, ctx.sdma)
    P.wait("vector", ctx.sdma)
    P.wait("vector", ctx.gdma)

    def softmax_ops(buf, n):
        L = lambda t: t.rearrange("p (tt e) -> p tt e", e=E)
        P.ins("vector", "tensor_reduce", mscr[:, 0:n], L(buf), axis=AX.X, op=OP.max, sem=ctx.dve)
        P.ins("vector", "tensor_tensor", L(buf), L(buf),
              mscr[:, 0:n].unsqueeze(2).to_broadcast([128, n, E]), OP.subtract, sem=ctx.dve)
        P.wait("scalar", ctx.dve)
        P.ins("scalar", "activation", buf[:, 0:n * E], buf[:, 0:n * E], AF.Exp, sem=ctx.act)
        P.wait("vector", ctx.act)
        P.ins("vector", "tensor_reduce", mscr[:, 0:n], L(buf), axis=AX.X, op=OP.add, sem=ctx.dve)
        P.ins("vector", "reciprocal", mscr[:, n:2 * n], mscr[:, 0:n], sem=ctx.dve)
        P.ins("vector", "tensor_tensor", L(buf), L(buf),
              mscr[:, n:2 * n].unsqueeze(2).to_broadcast([128, n, E]), OP.mult, sem=ctx.dve)

    P.ins("vector", "tensor_copy", prb[:, :], lgt[:, :], sem=ctx.dve)
    softmax_ops(prb, NT2)
    P.wait("tensor", ctx.sdma)
    P.wait("tensor", ctx.gdma)
    P.wait("tensor", ctx.dve)
    P.mm(ctx.psum[2][0:128, 0:E], ctx.C["ONES_R"][0:1, :], thr2[0:1, :], True, True, sem=ctx.pe)
    P.mm(ctx.psum[3][0:128, 0:E], ctx.C["ONES_R"][0:1, :], thr2b[0:1, :], True, True, sem=ctx.pe)
    P.wait("vector", ctx.pe)
    P.ins("vector", "tensor_copy", lo_bc[:, :], ctx.psum[2][0:128, 0:E], sem=ctx.dve)
    P.ins("vector", "tensor_copy", hi_bc[:, :], ctx.psum[3][0:128, 0:E], sem=ctx.dve)
    P.ins("vector", "tensor_tensor", L3v(lgt), L3v(prb),
          hi_bc[:, :].unsqueeze(1).to_broadcast([128, NT2, E]), OP.is_ge, sem=ctx.dve)
    P.ins("vector", "tensor_tensor", L3v(mscr), L3v(prb),
          lo_bc[:, :].unsqueeze(1).to_broadcast([128, NT2, E]), OP.is_ge, sem=ctx.dve)
    P.ins("vector", "tensor_sub", prb[:, :], mscr[:, :], lgt[:, :], sem=ctx.dve)
    def tile_counts(buf, dst):
        P.wait("tensor", ctx.dve)
        P.mm(ctx.psum[0][0:1, 0:512], ctx.C["ONES_CF"][:, 0:1], buf[:, 0:512], True, True, sem=ctx.pe)
        P.mm(ctx.psum[1][0:1, 0:512], ctx.C["ONES_CF"][:, 0:1], buf[:, 512:1024], True, True, sem=ctx.pe)
        P.wait("vector", ctx.pe)
        P.ins("vector", "tensor_copy", rowa[0:1, 0:512], ctx.psum[0][0:1, 0:512], sem=ctx.dve)
        P.ins("vector", "tensor_copy", rowa[0:1, 512:1024], ctx.psum[1][0:1, 0:512], sem=ctx.dve)
        P.wait("sync", ctx.dve)
        P.dma("sync", bscr[0:1, :], rowa[0:1, :], ctx.sdma)
        P.wait("sync", ctx.sdma)
        P.dma("sync", dst[:, :], bscr.rearrange("a (t e) -> (a t) e", e=E), ctx.sdma)
        P.wait("vector", ctx.sdma)
        P.wait("tensor", ctx.sdma)
    tile_counts(lgt, tge)
    tile_counts(prb, ttd)
    P.wait("tensor", ctx.dve)
    P.mm(ctx.psum[0][0:1, 0:E], ctx.C["ONES_CF"][:, 0:1], tge[:, :], True, True, sem=ctx.pe)
    P.wait("vector", ctx.pe)
    P.ins("vector", "tensor_scalar", rowa[0:1, 0:E], ctx.psum[0][0:1, 0:E], -1.0, None, OP.mult, sem=ctx.dve)
    P.ins("vector", "tensor_scalar", rowa[0:1, 0:E], rowa[0:1, 0:E], 2048.0, None, OP.add, sem=ctx.dve)
    P.wait("tensor", ctx.dve)
    P.mm(ctx.psum[2][0:128, 0:E], ctx.C["ONES_R"][0:1, :], rowa[0:1, 0:E], True, True, sem=ctx.pe)
    P.wait("vector", ctx.pe)
    P.ins("vector", "tensor_copy", aux[:, 0:E], ctx.psum[2][0:128, 0:E], sem=ctx.dve)
    P.wait("tensor", ctx.dve)
    P.mm(ctx.psum[3][0:128, 0:E], ctx.C["TRIU_EXCF"][:, :], ttd[:, :], True, True, sem=ctx.pe)
    P.wait("vector", ctx.pe)
    P.ins("vector", "tensor_copy", aux[:, E:2 * E], ctx.psum[3][0:128, 0:E], sem=ctx.dve)
    P.ins("vector", "tensor_sub", aux[:, 2 * E:3 * E], aux[:, 0:E], aux[:, E:2 * E], sem=ctx.dve)
    P.ins("vector", "tensor_scalar", aux[:, 2 * E:3 * E], aux[:, 2 * E:3 * E], 0.0, None, OP.max, sem=ctx.dve)
    P.ins("vector", "tensor_tensor", aux[:, 2 * E:3 * E], aux[:, 2 * E:3 * E], ttd[:, :], OP.min, sem=ctx.dve)
    P.ins("vector", "tensor_add", aux[:, 3 * E:4 * E], aux[:, 2 * E:3 * E], tge[:, :], sem=ctx.dve)
    P.wait("tensor", ctx.dve)
    P.mm(ctx.psum[0][0:1, 0:E], mymask[:, 0:1], aux[:, 3 * E:4 * E], True, True, sem=ctx.pe)
    P.mm(ctx.psum[1][0:1, 0:E], mymask[:, 0:1], ttd[:, :], True, True, sem=ctx.pe)
    P.wait("vector", ctx.pe)
    P.ins("vector", "tensor_copy", rowa[0:1, 0:E], ctx.psum[0][0:1, 0:E], sem=ctx.dve)
    P.ins("vector", "tensor_copy", rowa[0:1, E:2 * E], ctx.psum[1][0:1, 0:E], sem=ctx.dve)
    P.wait("tensor", ctx.dve)
    P.mm(ctx.psum[2][0:128, 0:E], ctx.C["ONES_R"][0:1, :], rowa[0:1, 0:E], True, True, sem=ctx.pe)
    P.mm(ctx.psum[3][0:128, 0:E], ctx.C["ONES_R"][0:1, :], rowa[0:1, E:2 * E], True, True, sem=ctx.pe)
    P.wait("vector", ctx.pe)
    P.ins("vector", "tensor_copy", aux[:, E:2 * E], ctx.psum[2][0:128, 0:E], sem=ctx.dve)
    P.ins("vector", "tensor_copy", aux[:, 2 * E:3 * E], ctx.psum[3][0:128, 0:E], sem=ctx.dve)
    P.wait("sync", ctx.pe)
    P.dma("sync", mprb[:, :].rearrange("p (tt e) -> p tt e", e=E),
          mylg_in.rearrange("(tt p) e -> p tt e", p=128), ctx.sdma)
    P.wait("vector", ctx.sdma)
    softmax_ops(mprb, 16)
    P.ins("vector", "tensor_tensor", M3(mge), M3(mprb),
          hi_bc[:, :].unsqueeze(1).to_broadcast([128, 16, E]), OP.is_ge, sem=ctx.dve)
    P.ins("vector", "tensor_tensor", M3(mtd), M3(mprb),
          lo_bc[:, :].unsqueeze(1).to_broadcast([128, 16, E]), OP.is_ge, sem=ctx.dve)
    P.ins("vector", "tensor_sub", mtd[:, :], mtd[:, :], mge[:, :], sem=ctx.dve)
    def gcum16(src, out, off_col):
        P.ins("vector", "tensor_copy", selb[:, :], src, sem=ctx.dve)
        P.wait("tensor", ctx.dve)
        P.mm(ctx.psum[0][0:1, 0:128], ctx.C["ONES_CF"][:, 0:1], src, True, True, sem=ctx.pe)
        P.wait("vector", ctx.pe)
        P.ins("vector", "tensor_copy", rowa[0:1, 0:128], ctx.psum[0][0:1, 0:128], sem=ctx.dve)
        P.wait("sync", ctx.dve)
        P.dma("sync", bscr[0:1, 0:128], rowa[0:1, 0:128], ctx.sdma)
        P.wait("sync", ctx.sdma)
        P.dma("sync", tge[0:16, 0:E], bscr.rearrange("a (t e) -> (a t) e", e=E)[0:16, :], ctx.sdma)
        P.wait("tensor", ctx.sdma)
        P.mm(ctx.psum[1][0:16, 0:E], ctx.C["TRIU_EXCF"][0:16, 0:16], tge[0:16, 0:E], True, True,
             sem=ctx.pe)
        P.wait("vector", ctx.pe)
        P.ins("vector", "tensor_copy", tge[0:16, 0:E], ctx.psum[1][0:16, 0:E], sem=ctx.dve)
        P.wait("sync", ctx.dve)
        P.dma("sync", bscr.rearrange("a (t e) -> (a t) e", e=E)[0:16, :], tge[0:16, 0:E], ctx.sdma)
        P.wait("sync", ctx.sdma)
        P.dma("sync", rowa[0:1, 0:128], bscr[0:1, 0:128], ctx.sdma)
        P.wait("tensor", ctx.sdma)
        P.wait("tensor", ctx.dve)
        P.mm(ctx.psum[2][:, 0:128], ctx.C["TRIU_INC"][:, :], selb[:, :], True, False, sem=ctx.pe)
        P.mm(ctx.psum[2][:, 0:128], ctx.C["ONES_R"][0:1, :], rowa[0:1, 0:128], False, True, sem=ctx.pe)
        P.wait("vector", ctx.pe)
        P.ins("vector", "tensor_tensor", M3(out), ctx.psum[2][:, 0:128].rearrange("p (tt e) -> p tt e", e=E),
              aux[:, off_col:off_col + E].unsqueeze(1).to_broadcast([128, 16, E]), OP.add, sem=ctx.dve)
    gcum16(mtd[:, :], mrk, 2 * E)
    P.ins("vector", "tensor_tensor", M3(mrk), M3(mrk),
          aux[:, 0:E].unsqueeze(1).to_broadcast([128, 16, E]), OP.is_le, sem=ctx.dve)
    P.ins("vector", "tensor_tensor", mtd[:, :], mtd[:, :], mrk[:, :], OP.mult, sem=ctx.dve)
    P.ins("vector", "tensor_add", mge[:, :], mge[:, :], mtd[:, :], sem=ctx.dve)
    gcum16(mge[:, :], mrk, E)
    P.ins("vector", "tensor_scalar", mrk[:, :], mrk[:, :], 1.0, None, OP.subtract, sem=ctx.dve)
    P.ins("vector", "tensor_scalar", M3(mscr[:, 0:16 * E]),
          ctx.C["IOTA8R"][:, 0:E].unsqueeze(1).to_broadcast([128, 16, E]), 2048.0, None, OP.mult,
          sem=ctx.dve)
    P.ins("vector", "tensor_add", mrk[:, :], mrk[:, :], mscr[:, 0:16 * E], sem=ctx.dve)
    P.ins("vector", "tensor_scalar", mrk[:, :], mrk[:, :], float(OOB), None, OP.subtract, sem=ctx.dve)
    P.ins("vector", "tensor_tensor", mrk[:, :], mrk[:, :], mge[:, :], OP.mult, sem=ctx.dve)
    P.ins("vector", "tensor_scalar", mrk[:, :], mrk[:, :], float(OOB), None, OP.add, sem=ctx.dve)
    P.ins("vector", "tensor_copy", didx[:, :], mrk[:, :], sem=ctx.dve)
    P.ins("vector", "memset", y1[:, :], 0.0, sem=ctx.dve)
    gb = [sbuf(P, f"gbuf{i}", [128, D], BF16) for i in range(2)]
    breg = P.breg(16384 - 1)
    P.wait("gpsimd", ctx.dve)
    g0 = ctx.gsc.n
    for j, (e, mt) in enumerate([(e, mt) for e in range(E) for mt in range(NTILE)]):
        par = j % 2
        if j >= 2:
            P.wait("vector", ctx.gsc, g0 + 16 * (j - 1))
        P.ins("vector", "memset", gb[par][:, :], 0.0, sem=ctx.dve)
        P.wait("gpsimd", ctx.dve)
        P.idma(gb[par][:, :], None, ye_in, IOOA(didx[:, mt * E + e:mt * E + e + 1], 0),
               ctx.gsc, bounds=breg, oob_is_err=False)
        P.wait("vector", ctx.gsc)
        sl = y1[:, mt * D:(mt + 1) * D]
        P.ins("vector", "tensor_add", sl, sl, gb[par][:, :], sem=ctx.dve)
    P.ins("vector", "tensor_tensor", mscr[:, 0:16 * E], mge[:, :], mprb[:, :], OP.mult, sem=ctx.dve)
    P.ins("vector", "tensor_reduce", gsum[:, :], M3(mscr[:, 0:16 * E]), axis=AX.X, op=OP.add, sem=ctx.dve)
    P.ins("vector", "tensor_scalar", gsum[:, :], gsum[:, :], 1e-9, None, OP.max, sem=ctx.dve)
    P.wait("scalar", ctx.dve)
    P.ins("scalar", "activation", gsum[:, :], gsum[:, :], AF.Ln, sem=ctx.act)
    P.ins("scalar", "activation", gsum[:, :], gsum[:, :], AF.Exp, scale=-1.0, sem=ctx.act)
    P.wait("vector", ctx.act)
    P.wait("vector", ctx.gsc)
    for mt in range(NTILE):
        P.wait("sync", ctx.dve)
        P.dma("sync", atile[:, :], x1_in[mt * 128:(mt + 1) * 128, :], ctx.sdma)
        P.dma("sync", btile[:, :], s1_in[mt * 128:(mt + 1) * 128, :], ctx.sdma)
        P.wait("vector", ctx.sdma)
        P.ins("vector", "tensor_scalar", ctx.scr1[:, :], y1[:, mt * D:(mt + 1) * D],
              gsum[:, mt:mt + 1], None, OP.mult, sem=ctx.dve)
        P.wait("scalar", ctx.dve)
        P.ins("scalar", "activation", ctx.scr1[:, :], ctx.scr1[:, :], AF.Silu, sem=ctx.act)
        P.wait("vector", ctx.act)
        P.ins("vector", "tensor_mul", btile[:, :], btile[:, :], ctx.scr1[:, :], sem=ctx.dve)
        P.ins("vector", "tensor_add", atile[:, :], atile[:, :], btile[:, :], sem=ctx.dve)
        P.wait("sync", ctx.dve)
        P.dma("sync", o_out[mt * 128:(mt + 1) * 128, :], atile[:, :], ctx.sdma)
        P.wait("vector", ctx.sdma)
    P.wait("gpsimd", ctx.sdma)
    P.finalize()
    return P

FUSED_CONSTS = ["SWAP", "ONES_R", "ONES_C", "HMAP", "BMAP", "M_CAUS", "M_WIN", "E64BC", "IDN",
                "TRIU_INC", "TRIU_EXCF", "ONES_CF", "IOTA8R", "CMAP", "SMAP", "MELB"]

def stage_barrier(P, ctx):
    for eng in Prog.ENGINES:
        for s in (ctx.sdma, ctx.adma, ctx.gdma, ctx.pe, ctx.dve, ctx.act, ctx.gsc, ctx.ccs):
            P.wait(eng, s)

def _stage(tag):
    FUSE["stage"] = tag
    FUSE["stack"] = contextlib.ExitStack()

def _stage_end():
    FUSE["stack"].close()
    FUSE["stack"] = None
    FUSE["ctx"].C.clear()

def build_fused():
    FUSE["on"] = True
    FUSE["P"] = None
    FUSE["ctx"] = None
    FUSE["stage"] = ""
    FUSE["stack"] = None
    P = new_prog()
    ctx = Ctx(P, rstd=False)
    nc = P.nc
    OPB = OP.bypass
    G4 = [[0, 1, 2, 3], [4, 5, 6, 7]]
    G8 = [list(range(8))]
    xn_g = nc.dram_tensor("xn_g", [TG, D], BF16).ap()
    lg_g = nc.dram_tensor("logits_g", [TG, E], F32).ap()
    oa_g = nc.dram_tensor("out_all_g", [4 * 1280, D], BF16).ap()
    xnn_a = nc.dram_tensor("xnn_a", [8 * S, D], BF16).ap()
    lg_a = nc.dram_tensor("logits_a", [8 * S, E], F32).ap()
    ye_a = nc.dram_tensor("ye_all", [8 * S, D], BF16).ap()
    P.iomap.update({"xn_g": xn_g, "logits_g": lg_g, "out_all_g": oa_g,
                    "xnn_a": xnn_a, "logits_a": lg_a, "ye_all": ye_a})
    _stage("s1_")
    ctx.bc = sbuf(P, "bcscr", [128, 512], BF16)
    ctx.sq = sbuf(P, "sqscr", [128, 512], BF16)
    ctx.rstd = sbuf(P, "rstdscr", [16, 2048], BF16)
    def _early1():
        P.wait("gpsimd", ctx.sdma)
        P.cc("AllGather", OPB, G4, [P.iomap["xn0"].opt()], [xn_g.opt()], ctx.ccs)
        P.cc("AllGather", OPB, G4, [P.iomap["logits0"].opt()], [lg_g.opt()], ctx.ccs)
    FUSE["cc_early"] = _early1
    build_l0a()
    _stage_end()
    ctx.bc = ctx.sq = ctx.rstd = None
    stage_barrier(P, ctx)
    for eng in Prog.ENGINES:
        P.wait(eng, ctx.ccs)
    _stage("s2_")
    build_l0f_full()
    _stage_end()
    stage_barrier(P, ctx)
    P.cc("AllGather", OPB, G4, [P.iomap["out_all"].opt()], [oa_g.opt()], ctx.ccs)
    for eng in Prog.ENGINES:
        P.wait(eng, ctx.ccs)
    _stage("s3_")
    def _early3():
        P.wait("gpsimd", ctx.sdma)
        P.cc("AllGather", OPB, G8, [P.iomap["xnn"].opt()], [xnn_a.opt()], ctx.ccs)
        P.cc("AllGather", OPB, G8, [P.iomap["logits1"].opt()], [lg_a.opt()], ctx.ccs)
    FUSE["cc_early"] = _early3
    build_l1a()
    _stage_end()
    stage_barrier(P, ctx)
    for eng in Prog.ENGINES:
        P.wait(eng, ctx.ccs)
    P.iomap["my_lg"] = P.iomap["logits1"]
    _stage("s4_")
    build_l1f()
    _stage_end()
    stage_barrier(P, ctx)
    P.cc("AllGather", OPB, G8, [P.iomap["ye"].opt()], [ye_a.opt()], ctx.ccs)
    for eng in Prog.ENGINES:
        P.wait(eng, ctx.ccs)
    _stage("s5_")
    build_l1c()
    _stage_end()
    stage_barrier(P, ctx)
    if FUSE.get("dbg"):
        for nm in ("xn0", "logits0", "s0", "out_all", "gidx", "gates", "gidxp",
                   "x1", "xnn", "logits1", "s1", "ye", "thr",
                   "xn_g", "logits_g", "out_all_g", "xnn_a", "logits_a", "ye_all"):
            src = P.iomap[nm]
            dst = nc.dram_tensor("dbg_" + nm, list(src.tensor.shape), src.tensor.dtype,
                                 kind="ExternalOutput").ap()
            P.dma("sync", dst, src, ctx.sdma)
        P.wait("gpsimd", ctx.sdma)
    P.finalize(force=True)
    FUSE["on"] = False
    FUSE["P"] = None
    FUSE["ctx"] = None
    return P

def kernel(**inputs):
    inp = {k: np.asarray(v) for k, v in inputs.items()}
    B = 8
    x = np.ascontiguousarray(inp["L0_attn_hidden"], dtype=np.float32).reshape(B, S, D)
    d1 = prep_l0a_weights(inp)
    wi = np.asarray(inp["L0_ffn_wi"], np.float32)
    wo = np.asarray(inp["L0_ffn_wo"], np.float32)
    PC = perm_cols()
    w_kv = np.asarray(inp["L1_attn_w_kv"], np.float32)
    d3 = dict(ln_w=np.asarray(inp["L1_attn_ln_w"], np.float32).reshape(1, D),
              ln_b=np.asarray(inp["L1_attn_ln_b"], np.float32).reshape(1, D),
              nw=np.asarray(inp["L1_ffn_norm_weight"], np.float32).reshape(1, D),
              w_q=np.ascontiguousarray(np.asarray(inp["L1_attn_w_q"], np.float32)[:, PC]),
              w_k=np.ascontiguousarray(w_kv[:, :D][:, PC]),
              w_v=np.ascontiguousarray(w_kv[:, D:]),
              w_o=np.asarray(inp["L1_attn_w_o"], np.float32),
              rwT=np.ascontiguousarray(np.asarray(inp["L1_ffn_router_w"], np.float32).T),
              rb=np.asarray(inp["L1_ffn_router_b"], np.float32).reshape(1, E),
              theta=np.asarray(inp["L1_attn_theta"], np.float32).reshape(1, 1))
    fc1 = np.asarray(inp["L1_ffn_fc1_w"], np.float32)
    fc1b = np.asarray(inp["L1_ffn_fc1_b"], np.float32)
    fc2 = np.asarray(inp["L1_ffn_fc2_w"], np.float32)
    fc2b = np.asarray(inp["L1_ffn_fc2_b"], np.float32)
    dc = {"c_" + nm: CONSTS[nm] for nm in FUSED_CONSTS}
    dc["c_T32"] = CONSTS["T32"]
    Pf = build_fused()
    maps = []
    for c in range(B):
        e0 = 2 * (c % 4)
        mask = np.zeros((128, 1), np.float32)
        mask[:c * 16] = 1.0
        rowsel = ((c % 4) * S + np.arange(NTILE)[None, :] * 128
                  + np.arange(128)[:, None]).astype(np.int32)
        m = dict(d1)
        m.update(d3)
        m.update(dc)
        m.update(x=np.ascontiguousarray(x[c]),
                 wi0=np.ascontiguousarray(wi[e0]), wi1=np.ascontiguousarray(wi[e0 + 1]),
                 wo0=np.ascontiguousarray(wo[e0]), wo1=np.ascontiguousarray(wo[e0 + 1]),
                 meta=np.tile(np.array([[e0, e0 + 1]], np.float32), (128, 1)),
                 fc1T=np.ascontiguousarray(fc1[c].T),
                 fc1b=np.ascontiguousarray(fc1b[c].reshape(32, 128).T),
                 fc2T=np.ascontiguousarray(fc2[c].T),
                 fc2b=np.ascontiguousarray(fc2b[c].reshape(1, D)),
                 meta4=np.full((128, 1), float(c), np.float32),
                 mymask=mask, rowsel=np.ascontiguousarray(rowsel))
        maps.append(m)
    r = run_bass_kernel_spmd(Pf.nc, maps, core_ids=list(range(B))).results
    globals()["_last_results"] = r
    out = np.stack([np.asarray(r[c]["out"], np.float32) for c in range(B)], 0)
    return out.reshape(B, S, D)

def kernel_unfused(**inputs):
    inp = {k: np.asarray(v) for k, v in inputs.items()}
    B = 8
    x = np.ascontiguousarray(inp["L0_attn_hidden"], dtype=np.float32).reshape(B, S, D)
    d1 = prep_l0a_weights(inp)
    P1 = build_l0a()
    maps1 = [dict(d1, x=np.ascontiguousarray(x[c])) for c in range(B)]
    r1 = run_bass_kernel_spmd(P1.nc, maps1, core_ids=list(range(B))).results
    xn0 = [r1[c]["xn0"] for c in range(B)]
    lg0 = [np.asarray(r1[c]["logits0"], np.float32) for c in range(B)]
    s0 = [np.asarray(r1[c]["s0"], np.float32) for c in range(B)]
    P2 = build_l0f_full()
    wi = np.asarray(inp["L0_ffn_wi"], np.float32)
    wo = np.asarray(inp["L0_ffn_wo"], np.float32)
    c2 = {"c_" + nm: CONSTS[nm] for nm in L2_CONSTS}
    maps2 = []
    for c in range(B):
        g, e0 = c // 4, 2 * (c % 4)
        maps2.append(dict(c2, logits_g=np.concatenate(lg0[g * 4:(g + 1) * 4], 0),
                          xn_g=np.concatenate(xn0[g * 4:(g + 1) * 4], 0),
                          wi0=np.ascontiguousarray(wi[e0]), wi1=np.ascontiguousarray(wi[e0 + 1]),
                          wo0=np.ascontiguousarray(wo[e0]), wo1=np.ascontiguousarray(wo[e0 + 1]),
                          meta=np.tile(np.array([[e0, e0 + 1]], np.float32), (128, 1))))
    r2 = run_bass_kernel_spmd(P2.nc, maps2, core_ids=list(range(B))).results
    out_all_g = [np.concatenate([r2[g * 4 + j]["out_all"] for j in range(4)], 0) for g in range(2)]
    P3 = build_l1a()
    PC = perm_cols()
    w_kv = np.asarray(inp["L1_attn_w_kv"], np.float32)
    c3 = {"c_" + nm: CONSTS[nm] for nm in L3_CONSTS + ["T32"]}
    d3 = dict(c3,
              ln_w=np.asarray(inp["L1_attn_ln_w"], np.float32).reshape(1, D),
              ln_b=np.asarray(inp["L1_attn_ln_b"], np.float32).reshape(1, D),
              nw=np.asarray(inp["L1_ffn_norm_weight"], np.float32).reshape(1, D),
              w_q=np.ascontiguousarray(np.asarray(inp["L1_attn_w_q"], np.float32)[:, PC]),
              w_k=np.ascontiguousarray(w_kv[:, :D][:, PC]),
              w_v=np.ascontiguousarray(w_kv[:, D:]),
              w_o=np.asarray(inp["L1_attn_w_o"], np.float32),
              rwT=np.ascontiguousarray(np.asarray(inp["L1_ffn_router_w"], np.float32).T),
              rb=np.asarray(inp["L1_ffn_router_b"], np.float32).reshape(1, E),
              theta=np.asarray(inp["L1_attn_theta"], np.float32).reshape(1, 1))
    maps3 = []
    for c in range(B):
        g = c // 4
        gi = np.asarray(r2[c]["gidx"])[(c % 4) * S:(c % 4 + 1) * S]
        ga = np.asarray(r2[c]["gates"], np.float32)[(c % 4) * S:(c % 4 + 1) * S]
        maps3.append(dict(d3, x=np.ascontiguousarray(x[c]), s0=s0[c], out_all_g=out_all_g[g],
                          gidx=np.ascontiguousarray(gi), gates=np.ascontiguousarray(ga)))
    r3 = run_bass_kernel_spmd(P3.nc, maps3, core_ids=list(range(B))).results
    x1 = [np.asarray(r3[c]["x1"], np.float32) for c in range(B)]
    xnn = [r3[c]["xnn"] for c in range(B)]
    lg1 = [np.asarray(r3[c]["logits1"], np.float32) for c in range(B)]
    s1 = [np.asarray(r3[c]["s1"], np.float32) for c in range(B)]
    P4 = build_l1f()
    fc1 = np.asarray(inp["L1_ffn_fc1_w"], np.float32)
    fc1b = np.asarray(inp["L1_ffn_fc1_b"], np.float32)
    fc2 = np.asarray(inp["L1_ffn_fc2_w"], np.float32)
    fc2b = np.asarray(inp["L1_ffn_fc2_b"], np.float32)
    xnn_all = np.concatenate(xnn, 0)
    lg1_all = np.concatenate(lg1, 0)
    c4 = {"c_" + nm: CONSTS[nm] for nm in L4_CONSTS}
    maps4 = []
    for c in range(B):
        maps4.append(dict(c4, logits_a=lg1_all, xnn_a=xnn_all,
                          fc1T=np.ascontiguousarray(fc1[c].T),
                          fc1b=np.ascontiguousarray(fc1b[c].reshape(32, 128).T),
                          fc2T=np.ascontiguousarray(fc2[c].T),
                          fc2b=np.ascontiguousarray(fc2b[c].reshape(1, D)),
                          meta4=np.full((128, 1), float(c), np.float32)))
    r4 = run_bass_kernel_spmd(P4.nc, maps4, core_ids=list(range(B))).results
    ye_all = np.concatenate([r4[c]["ye"] for c in range(B)], 0)
    thr = np.asarray(r4[0]["thr"], np.float32)
    P5 = build_l1c()
    c5 = {"c_" + nm: CONSTS[nm] for nm in L5_CONSTS}
    maps5 = []
    for c in range(B):
        mask = np.zeros((128, 1), np.float32)
        mask[:c * 16] = 1.0
        maps5.append(dict(c5, x1=x1[c], s1=s1[c], ye_all=ye_all, thr=thr, logits_a=lg1_all,
                          my_lg=np.ascontiguousarray(lg1_all[c * S:(c + 1) * S]), mymask=mask))
    r5 = run_bass_kernel_spmd(P5.nc, maps5, core_ids=list(range(B))).results
    out = np.stack([np.asarray(r5[c]["out"], np.float32) for c in range(B)], 0)
    return out.reshape(B, S, D)
